# version 01
# baseline (speedup 1.0000x reference)
import math
from contextlib import ExitStack

import numpy as np
import ml_dtypes
import concourse.bass as bass
import concourse.mybir as mybir
from concourse.bass_utils import run_bass_kernel_spmd

F32 = mybir.dt.float32
BF16 = mybir.dt.bfloat16
AF = mybir.ActivationFunctionType
ALU = mybir.AluOpType
AX = mybir.AxisListType
bf = ml_dtypes.bfloat16
ts = bass.ts

B_CORE = 2
S = 2048
N = B_CORE * S
D = 1024
NT = N // 128
H0 = 16
DH0 = 64

TAPS: list = []
STAGES = {"ln0", "rproj", "rrot", "rattn", "rfin", "ffn0", "comb0",
          "ln1", "gla", "moe", "comb1"}


class Sem:
    def __init__(self, handle):
        self.h = handle
        self.v = 0


class Prog:
    ENGINES = ("sync", "gpsimd", "tensor", "vector", "scalar")

    def __init__(self, nc, space=False):
        self.nc = nc
        self.space = space
        self.items = {e: [] for e in self.ENGINES}

    def add(self, eng, fn, waits=(), inc=None, inc_n=1):
        w = [(s.h, v) for (s, v) in waits]
        out = None
        if inc is not None:
            inc.v += inc_n
            out = inc.v
        self.items[eng].append((fn, w, inc.h if inc else None, inc_n))
        if self.space and eng in ("tensor", "vector") and inc is not None:
            self.items[eng].append((lambda e: e.nop(), [], None, 1))
        return out

    def emit(self):
        with self.nc.Block() as blk:
            for ename in self.ENGINES:
                items = self.items[ename]
                if not items:
                    continue
                def body(eng, items=items):
                    for fn, waits, inch, incn in items:
                        for h, v in waits:
                            eng.wait_ge(h, v)
                        inst = fn(eng)
                        if inch is not None:
                            inst.then_inc(inch, incn)
                getattr(blk, ename)(body)
        self.items = {e: [] for e in self.ENGINES}


def w(sem):
    return (sem, sem.v)


class SemBank:
    def __init__(self, nc, ctx):
        self.nc = nc
        self.ctx = ctx
        self.sems = {}

    def get(self, name):
        if name not in self.sems:
            self.sems[name] = Sem(self.ctx.enter_context(self.nc.semaphore(name)))
        return self.sems[name]


def _geo_rowsum(decay, q):
    return (1.0 - np.exp(decay * (q + 1))) / (1.0 - np.exp(decay))


def host_tables():
    t = {}
    kk = np.arange(128)[:, None]
    qq = np.arange(512)[None, :]
    t["ident_f32"] = np.eye(128, dtype=np.float32)
    t["ident_bf16"] = np.eye(128, dtype=np.float32).astype(bf)
    a = 1.0 / 10000.0 ** np.linspace(0.0, 1.0, DH0 // 2)
    idx = np.arange(S, dtype=np.float64)
    cos2 = np.cos(idx[:, None] * a[None, :])
    sin2 = np.sin(idx[:, None] * a[None, :])
    decay = np.log(1.0 - 2.0 ** (-5.0 - np.arange(H0, dtype=np.float64)))
    rowsum = _geo_rowsum(decay[None, :], idx[:, None])
    dq = np.exp(decay[None, :] * idx[:, None]) / np.sqrt(rowsum)
    dk = np.exp(-decay[None, :] * idx[:, None])
    t["rcq"] = (cos2[:, None, :] * dq[:, :, None]).reshape(S, 512).astype(np.float32)
    t["rsq"] = (sin2[:, None, :] * dq[:, :, None]).reshape(S, 512).astype(np.float32)
    t["rck"] = (cos2[:, None, :] * dk[:, :, None] / 32.0).reshape(S, 512).astype(np.float32)
    t["rsk"] = (sin2[:, None, :] * dk[:, :, None] / 32.0).reshape(S, 512).astype(np.float32)
    for j in range(4):
        t[f"trilq_{j}"] = (qq >= kk + 128 * j).astype(np.float32).astype(bf)
    for j in range(4):
        t[f"lamu_{j}"] = (qq <= kk + 128 * j).astype(np.float32).astype(bf)
    t["e0col"] = np.ones((128, 1), dtype=np.float32).astype(bf)
    r0 = np.zeros((1, 128), dtype=np.float32); r0[0, :64] = 1
    r1 = np.zeros((1, 128), dtype=np.float32); r1[0, 64:] = 1
    t["bc_lo"] = r0.astype(bf)
    t["bc_hi"] = r1.astype(bf)
    t["bc_all"] = np.ones((1, 128), dtype=np.float32).astype(bf)
    DHF, WINDOW = 64, 512
    half = np.arange(0, DHF, 2, dtype=np.float64)
    inv_freq = 1.0 / 10000.0 ** (half / DHF)
    tt = np.arange(S, dtype=np.float64)
    freqs = tt[:, None] * inv_freq[None, :]
    base = (half + 0.4 * DHF) / (1.4 * DHF)
    power = (tt - S // 2) / float(WINDOW // 2)
    scale = base[None, :] ** power[:, None]
    t["xc_q"] = (np.cos(freqs) * scale).astype(np.float32)
    t["xs_q"] = (np.sin(freqs) * scale).astype(np.float32)
    t["xc_k"] = (np.cos(freqs) / scale).astype(np.float32)
    t["xs_k"] = (np.sin(freqs) / scale).astype(np.float32)
    ss = np.arange(128)[:, None]
    t2 = np.arange(128)[None, :]
    t["tril_inc"] = (t2 >= ss).astype(np.float32).astype(bf)
    t["ucum"] = (t2 >= ss).astype(np.float32)
    t["sel63"] = (ss == 63).astype(np.float32) @ np.ones((1, 128), np.float32)
    t["sel127"] = (ss == 127).astype(np.float32) @ np.ones((1, 128), np.float32)
    t["ones_row"] = np.ones((1, 512), dtype=np.float32).astype(bf)
    t["zrow"] = np.zeros((1, 1024), dtype=np.float32).astype(bf)
    sel = np.zeros((8, 8, 128), dtype=np.float32)
    for e in range(8):
        sel[e, e, :] = 1.0
    t["sel_moe"] = np.concatenate(list(sel), axis=1).astype(bf)
    return t


def build_nc():
    nc = bass.Bass("TRN2", target_bir_lowering=False)
    tabs = host_tables()

    IN = {}
    def extin(name, shape, dt=F32):
        IN[name] = nc.dram_tensor(name, list(shape), dt, kind="ExternalInput")
        return IN[name]

    x_in = extin("x_in", (N, D))
    for nm, shp in [
        ("L0_attn_Wq", (D, D)), ("L0_attn_Wk", (D, D)), ("L0_attn_Wv", (D, D)),
        ("L0_attn_Wg", (D, D)),
        ("L0_ffn_wqkv", (2, D, 1536)), ("L0_ffn_q_scale", (2, 64)),
        ("L0_ffn_k_scale", (2, 64)), ("L0_ffn_wo", (2, 512, D)),
        ("L0_ffn_ln_g", (2, D)), ("L0_ffn_ln_b", (2, D)),
        ("L0_ffn_w1", (2, D, 5460)), ("L0_ffn_b1", (2, 5460)),
        ("L0_ffn_w2", (2, 2730, D)), ("L0_ffn_b2", (2, D)),
        ("L1_attn_Wq", (D, D)), ("L1_attn_Wk", (D, D)), ("L1_attn_Wv", (D, D)),
        ("L1_attn_Wg", (D, D)), ("L1_attn_Wo", (D, D)),
        ("L1_attn_w1", (D, 2736)), ("L1_attn_w2", (D, 2736)),
        ("L1_attn_w3", (2736, D)),
        ("L1_attn_g_attn", (1, D)), ("L1_attn_g_ffn", (1, D)),
        ("L1_ffn_Wg", (8, D)), ("L1_ffn_bg", (1, 8)),
        ("L1_ffn_w1", (8, 2048, D)), ("L1_ffn_w2", (8, D, 1024)),
    ]:
        extin(nm, shp)
    TAB = {}
    for nm, arr in tabs.items():
        dt = BF16 if arr.dtype == bf else F32
        TAB[nm] = nc.dram_tensor("tab_" + nm, list(arr.shape), dt, kind="ExternalInput")

    out = nc.dram_tensor("out", [N, D], F32, kind="ExternalOutput")

    SC = {}
    def scr(name, shape, dt):
        kind = "ExternalOutput" if name in TAPS else "Internal"
        SC[name] = nc.dram_tensor(name, list(shape), dt, kind=kind)
        return SC[name]

    xn_t = scr("XN_T", (N, D), BF16)
    xn_f32 = scr("XN_F32", (N, D), F32)
    q0_f = scr("Q0_F", (D, N), BF16)
    k0_f = scr("K0_F", (D, N), BF16)
    v0_f = scr("V0_F", (D, N), BF16)
    g0_f = scr("G0_F", (D, N), BF16)
    qr0_t = scr("QR0_T", (N, D), BF16)
    kr0_t = scr("KR0_T", (N, D), BF16)
    ot0_f = scr("OT0_F", (D, N), BF16)
    s0_t = scr("S0_T", (N, D), F32)
    scr("XN_PAD", (B_CORE * PAD_S, D), BF16)
    for l in range(2):
        scr(f"QKV{l}_F", (1536, N), BF16)
        scr(f"QRX{l}_T", (N, 512), BF16)
        scr(f"KRX{l}_T", (N, 512), BF16)
        scr(f"QN{l}_T", (N, 8), F32)
        scr(f"KN{l}_T", (N, 8), F32)
        scr(f"OT1{l}_F", (512, N), BF16)
        scr(f"WO{l}_F", (D, N), BF16)
        scr(f"XF1{l}_F32", (N, D), F32)
        scr(f"XF1{l}_PAD", (B_CORE * PAD_S, D), BF16)
        scr(f"H2N{l}_T", (N, D), BF16)
        scr(f"UA{l}_F", (2730, N), BF16)
        scr(f"UG{l}_F", (2730, N), BF16)
        scr(f"H3{l}_F", (2730, N), BF16)
        scr(f"W2O{l}_F", (D, N), BF16)
        scr(f"XF2{l}_F32", (N, D), F32)
    scr("XF20_PAD", (B_CORE * PAD_S, D), BF16)
    scr("X1_T", (N, D), F32)
    scr("XN1_T", (N, D), BF16)
    scr("XN1_F32", (N, D), F32)
    scr("HA_T", (N, D), BF16)
    for nm in ["QG_F", "KG_F", "VG_F", "ZG_F"]:
        scr(nm, (D, N), BF16)
    scr("QA_T", (N, D), BF16)
    scr("KD_T", (N, D), BF16)
    scr("KD2_T", (N, D), BF16)
    scr("CE63R", (B_CORE * NCH, D), BF16)
    scr("CE127R", (B_CORE * NCH, D), BF16)
    scr("OG_F", (D, N), BF16)
    scr("WOG_F", (D, N), BF16)
    scr("XA_T", (N, D), F32)
    scr("HF_T", (N, D), BF16)
    scr("U1_F", (2736, N), BF16)
    scr("U2_F", (2736, N), BF16)
    scr("FF1_F", (2736, N), BF16)
    scr("W3O_F", (D, N), BF16)
    scr("S1_T", (N, D), F32)
    scr("W1BF", (8, 2048, D), BF16)
    scr("W2BF", (8, D, 1024), BF16)
    scr("CE_F", (8, N), BF16)
    scr("HE_F", (2048, N), BF16)
    for e in range(8):
        scr(f"ACE{e}_F", (D, N), BF16)
    scr("MOE_F", (D, N), BF16)

    with ExitStack() as gctx:
        nc._sb = SemBank(nc, gctx)
        G = load_const_tiles(nc, gctx, TAB)

        final_src = x_in
        if "ln0" in STAGES:
            emit_ln(nc, gctx, G, x_in, xn_t, xn_f32, eps=1e-5, tag="ln0",
                    dst_pad=SC["XN_PAD"])
            final_src = xn_f32
        if "rproj" in STAGES:
            for nm, dst in [("L0_attn_Wq", q0_f), ("L0_attn_Wk", k0_f),
                            ("L0_attn_Wv", v0_f), ("L0_attn_Wg", g0_f)]:
                emit_mm(nc, gctx, G, tag="p0" + nm[-2:],
                        Ktot=D, Mtot=D,
                        wget=wget_plain(IN[nm]),
                        rhs_load=rhs_xbar(xn_t),
                        out_store=store_F(dst))
        if "rrot" in STAGES:
            emit_ret_rot(nc, gctx, G, TAB, q0_f, k0_f, qr0_t, kr0_t, tag="rrot")
        if "rattn" in STAGES:
            emit_ret_attn(nc, gctx, G, qr0_t, kr0_t, v0_f, ot0_f, tag="rat")
        if "rfin" in STAGES:
            emit_ret_final(nc, gctx, G, ot0_f, g0_f, s0_t, tag="rfin")
            final_src = s0_t

        if "ffn0" in STAGES:
            xn_pad = SC["XN_PAD"]
            emit_zero_pad_rows(nc, gctx, G, xn_pad, tag="zp0")
            stream_pad, stream_f32 = xn_pad, xn_f32
            for l in range(2):
                qkvf = SC[f"QKV{l}_F"]
                emit_mm(nc, gctx, G, tag=f"fq{l}", Ktot=D, Mtot=1536,
                        wget=wget_plain3(IN["L0_ffn_wqkv"], l),
                        rhs_load=rhs_xbar_pad(stream_pad),
                        out_store=store_F(qkvf))
                emit_lffn_rot(nc, gctx, G, TAB, IN, l, qkvf,
                              SC[f"QRX{l}_T"], SC[f"KRX{l}_T"],
                              SC[f"QN{l}_T"], SC[f"KN{l}_T"], tag=f"fr{l}")
                emit_lattn(nc, gctx, G, SC[f"QRX{l}_T"], SC[f"KRX{l}_T"], qkvf,
                           SC[f"QN{l}_T"], SC[f"KN{l}_T"], SC[f"OT1{l}_F"],
                           tag=f"fa{l}")
                emit_mm(nc, gctx, G, tag=f"fo{l}", Ktot=512, Mtot=D,
                        wget=wget_plain3(IN["L0_ffn_wo"], l),
                        rhs_load=rhs_direct(SC[f"OT1{l}_F"]),
                        out_store=store_F(SC[f"WO{l}_F"]))
                emit_zero_pad_rows(nc, gctx, G, SC[f"XF1{l}_PAD"], tag=f"zp1{l}")
                emit_combine(nc, gctx, G, tag=f"fc1{l}",
                             srcs={"a": stream_f32, "b": SC[f"WO{l}_F"]},
                             dst_f32=SC[f"XF1{l}_F32"], dst_pad=SC[f"XF1{l}_PAD"],
                             mode="resid")
                emit_ln(nc, gctx, G, None, SC[f"H2N{l}_T"], None, eps=1e-5,
                        tag=f"fl{l}", gain=IN["L0_ffn_ln_g"][l:l + 1, :],
                        bias_row=IN["L0_ffn_ln_b"][l:l + 1, :],
                        src_pad=SC[f"XF1{l}_PAD"])
                emit_mm(nc, gctx, G, tag=f"fwa{l}", Ktot=D, Mtot=2730,
                        wget=wget_cols(IN["L0_ffn_w1"], l, 0),
                        bias_get=bias_cols(IN["L0_ffn_b1"], l, 0),
                        rhs_load=rhs_xbar(SC[f"H2N{l}_T"]),
                        out_store=store_F(SC[f"UA{l}_F"]))
                emit_mm(nc, gctx, G, tag=f"fwg{l}", Ktot=D, Mtot=2730,
                        wget=wget_cols(IN["L0_ffn_w1"], l, 2730),
                        bias_get=bias_cols(IN["L0_ffn_b1"], l, 2730),
                        rhs_load=rhs_xbar(SC[f"H2N{l}_T"]),
                        out_store=store_F(SC[f"UG{l}_F"]))
                emit_ew_F(nc, gctx, G, tag=f"fgg{l}", Ktot=2730,
                          src_a=SC[f"UA{l}_F"], src_g=SC[f"UG{l}_F"],
                          dst=SC[f"H3{l}_F"], act=AF.Gelu)
                emit_mm(nc, gctx, G, tag=f"fw2{l}", Ktot=2730, Mtot=D,
                        wget=wget_plain3(IN["L0_ffn_w2"], l),
                        bias_get=bias_cols(IN["L0_ffn_b2"], l, 0),
                        rhs_load=rhs_direct(SC[f"H3{l}_F"]),
                        out_store=store_F(SC[f"W2O{l}_F"]), t_ext=2048)
                if l == 0:
                    emit_zero_pad_rows(nc, gctx, G, SC["XF20_PAD"], tag="zp20")
                emit_combine(nc, gctx, G, tag=f"fc2{l}",
                             srcs={"a": SC[f"XF1{l}_F32"], "b": SC[f"W2O{l}_F"]},
                             dst_f32=SC[f"XF2{l}_F32"],
                             dst_pad=SC["XF20_PAD"] if l == 0 else None,
                             mode="resid")
                stream_pad = SC["XF20_PAD"] if l == 0 else None
                stream_f32 = SC[f"XF2{l}_F32"]
            final_src = SC["XF21_F32"]

        if "comb0" in STAGES:
            emit_combine(nc, gctx, G, tag="cmb0",
                         srcs={"a": x_in, "s": s0_t, "g": SC["XF21_F32"]},
                         dst_f32=SC["X1_T"], mode="gate")
            final_src = SC["X1_T"]

        if "gla_dbg" in STAGES:
            STAGES.add("gla")
        if "ln1" in STAGES:
            emit_ln(nc, gctx, G, SC["X1_T"], SC["XN1_T"], SC["XN1_F32"],
                    eps=1e-5, tag="ln1")
            final_src = SC["XN1_F32"]
        if "gla" in STAGES:
            import os as _os
            _gskip = set((_os.environ.get("GLA_SKIP") or "").split(","))
            emit_ln(nc, gctx, G, SC["XN1_F32"], SC["HA_T"], None, eps=1e-6,
                    tag="grms", gain=IN["L1_attn_g_attn"][0:1, :], rms=True)
            for nm, dst in [("L1_attn_Wq", "QG_F"), ("L1_attn_Wk", "KG_F"),
                            ("L1_attn_Wv", "VG_F"), ("L1_attn_Wg", "ZG_F")]:
                if "proj" not in _gskip:
                    emit_mm(nc, gctx, G, tag="g" + nm[-2:], Ktot=D, Mtot=D,
                            wget=wget_plain(IN[nm]), rhs_load=rhs_xbar(SC["HA_T"]),
                            out_store=store_F(SC[dst]))
            if "prep" not in _gskip:
                emit_gla_prep(nc, gctx, G, SC["QG_F"], SC["KG_F"], SC["ZG_F"],
                              SC["QA_T"], SC["KD_T"], SC["KD2_T"],
                              SC["CE63R"], SC["CE127R"], tag="gp")
            if "scan" not in _gskip:
                emit_gla_scan(nc, gctx, G, SC["QA_T"], SC["KD_T"], SC["KD2_T"],
                              SC["VG_F"], SC["CE63R"], SC["CE127R"], SC["OG_F"],
                              tag="gs")
            emit_mm(nc, gctx, G, tag="gwo", Ktot=D, Mtot=D,
                    wget=wget_plain(IN["L1_attn_Wo"]),
                    rhs_load=rhs_direct(SC["OG_F"]),
                    out_store=store_F(SC["WOG_F"]))
            emit_combine(nc, gctx, G, tag="gxa",
                         srcs={"a": SC["XN1_F32"], "b": SC["WOG_F"]},
                         dst_f32=SC["XA_T"], mode="resid")
            emit_ln(nc, gctx, G, SC["XA_T"], SC["HF_T"], None, eps=1e-6,
                    tag="grm2", gain=IN["L1_attn_g_ffn"][0:1, :], rms=True)
            emit_mm(nc, gctx, G, tag="gw1", Ktot=D, Mtot=2736,
                    wget=wget_plain(IN["L1_attn_w1"]),
                    rhs_load=rhs_xbar(SC["HF_T"]), out_store=store_F(SC["U1_F"]))
            emit_mm(nc, gctx, G, tag="gw2", Ktot=D, Mtot=2736,
                    wget=wget_plain(IN["L1_attn_w2"]),
                    rhs_load=rhs_xbar(SC["HF_T"]), out_store=store_F(SC["U2_F"]))
            emit_ew_F(nc, gctx, G, tag="gsm", Ktot=2736, src_a=SC["U2_F"],
                      src_g=SC["U1_F"], dst=SC["FF1_F"], act=AF.Silu)
            emit_mm(nc, gctx, G, tag="gw3", Ktot=2736, Mtot=D,
                    wget=wget_plain(IN["L1_attn_w3"]),
                    rhs_load=rhs_direct(SC["FF1_F"]),
                    out_store=store_F(SC["W3O_F"]), t_ext=2048)
            emit_combine(nc, gctx, G, tag="gs1",
                         srcs={"a": SC["XA_T"], "b": SC["W3O_F"]},
                         dst_f32=SC["S1_T"], mode="resid")
            final_src = SC["S1_T"]

        if "moe" in STAGES:
            emit_moe_cast(nc, gctx, G, IN["L1_ffn_w1"], IN["L1_ffn_w2"],
                          SC["W1BF"], SC["W2BF"], tag="mc")
            emit_moe_gate(nc, gctx, G, IN, SC["XN1_T"], SC["CE_F"], tag="mg")
            for e in range(8):
                emit_mm(nc, gctx, G, tag=f"m1e{e}", Ktot=D, Mtot=2048,
                        wget=wget_xbar3(SC["W1BF"], e, D),
                        rhs_load=rhs_xbar(SC["XN1_T"]),
                        out_store=store_F(SC["HE_F"]), wq="sync")
                emit_ew_F(nc, gctx, G, tag=f"ma{e}", Ktot=D,
                          src_a=SC["HE_F"], a_off=1024, src_g=SC["HE_F"], g_off=0,
                          dst=SC[f"ACE{e}_F"], act=AF.Silu,
                          ce_f=SC["CE_F"], ce_row=e)
            def s2_wget(eng, dst, k0, ks, m0, ms):
                e_, kk = k0 // 1024, k0 % 1024
                return eng.dma_start(dst[0:ks, 0:ms],
                                     SC["W2BF"][e_, m0:m0 + ms, kk:kk + ks],
                                     transpose=True)
            def s2_rhs(dst, k0, ks, t0, tsz):
                e_, kk = k0 // 1024, k0 % 1024
                return [lambda eng: eng.dma_start(
                    dst[0:ks, 0:tsz], SC[f"ACE{e_}_F"][kk:kk + ks, t0:t0 + tsz])]
            emit_mm(nc, gctx, G, tag="m2", Ktot=8 * D, Mtot=D,
                    wget=s2_wget, rhs_load=s2_rhs,
                    out_store=store_F(SC["MOE_F"]), wq="sync", t_ext=1024)

        if "comb1" in STAGES:
            emit_combine(nc, gctx, G, tag="cmb1",
                         srcs={"a": SC["X1_T"], "s": SC["S1_T"],
                               "g_f": SC["MOE_F"]},
                         dst_f32=out, mode="gate")
            final_src = None

        if "lnaff_dbg" in STAGES:
            emit_ln(nc, gctx, G, None, SC["H2N0_T"], None, eps=1e-5,
                    tag="fl0", gain=IN["L0_ffn_ln_g"][0:1, :],
                    bias_row=IN["L0_ffn_ln_b"][0:1, :],
                    src_pad=SC["XF10_PAD"])
            final_src = xn_f32

        if final_src is not None:
            emit_copy_out(nc, gctx, final_src, out, tag="cout")

    nc.finalize()
    return nc


def load_const_tiles(nc, ctx, TAB):
    sb = lambda *a, **k: ctx.enter_context(nc.sbuf_tensor(*a, **k))
    G = {}
    specs = [
        ("ident_bf16", [128, 128], BF16), ("ident_f32", [128, 128], F32),
        ("e0col", [128, 1], BF16), ("bc_lo", [1, 128], BF16),
        ("bc_hi", [1, 128], BF16), ("bc_all", [1, 128], BF16),
        ("ones_row", [1, 512], BF16),
        ("tril_inc", [128, 128], BF16), ("ucum", [128, 128], F32),
        ("sel63", [128, 128], F32), ("sel127", [128, 128], F32),
        ("sel_moe", [8, 1024], BF16), ("zrow", [1, 1024], BF16),
    ] + [(f"trilq_{j}", [128, 512], BF16) for j in range(4)] \
      + [(f"lamu_{j}", [128, 512], BF16) for j in range(4)]
    s_c = nc._sb.get("sb_const")
    p = Prog(nc)
    for nm, shape, dt in specs:
        G[nm] = sb("g_" + nm, shape, dt)
        p.add("sync", lambda e, g=G[nm], t=TAB[nm], sh=shape: e.dma_start(
            g[tuple(slice(0, x) for x in sh)], t[tuple(slice(0, x) for x in sh)]),
            inc=s_c, inc_n=16)
    p.add("sync", lambda e: e.nop(), waits=[w(s_c)])
    p.emit()
    return G


def wget_plain(wt):
    def f(eng, dst, k0, ks, m0, ms):
        return eng.dma_start(dst[0:ks, 0:ms], wt[k0:k0 + ks, m0:m0 + ms])
    return f


def wget_plain3(wt, l):
    def f(eng, dst, k0, ks, m0, ms):
        return eng.dma_start(dst[0:ks, 0:ms], wt[l, k0:k0 + ks, m0:m0 + ms])
    return f


def wget_cols(wt, l, c0):
    def f(eng, dst, k0, ks, m0, ms):
        return eng.dma_start(dst[0:ks, 0:ms],
                             wt[l, k0:k0 + ks, c0 + m0:c0 + m0 + ms])
    return f


def bias_cols(bv, l, c0):
    def f(eng, dst, m0, ms):
        return eng.dma_start(dst, bv[l:l + 1, c0 + m0:c0 + m0 + ms])
    return f


def rhs_xbar(src_t, col_off=0, row_off=0):
    def f(dst, k0, ks, t0, tsz):
        return [lambda e: e.dma_start(dst[0:ks, 0:tsz],
                                      src_t[row_off + t0: row_off + t0 + tsz,
                                            col_off + k0: col_off + k0 + ks],
                                      transpose=True)]
    return f


def rhs_direct(src_f, row_off=0):
    def f(dst, k0, ks, t0, tsz):
        return [lambda e: e.dma_start(dst[0:ks, 0:tsz],
                                      src_f[row_off + k0: row_off + k0 + ks,
                                            t0:t0 + tsz])]
    return f


def rhs_xbar_pad(src_pad, shift_ge=512):
    def f(dst, k0, ks, t0, tsz):
        outs = []
        sh = 0 if k0 + ks <= shift_ge else 1
        assert k0 >= shift_ge or k0 + ks <= shift_ge
        for b0 in range(t0 // S, (t0 + tsz - 1) // S + 1):
            lo = max(t0, b0 * S)
            hi = min(t0 + tsz, (b0 + 1) * S)
            row = b0 * (S + 1) + (1 - sh) + (lo - b0 * S)
            outs.append(lambda e, lo=lo, hi=hi, row=row: e.dma_start(
                dst[0:ks, lo - t0: hi - t0],
                src_pad[row: row + (hi - lo), k0:k0 + ks], transpose=True))
        return outs
    return f


def store_F(dst_f, row_off=0):
    def f(eng, src, m0, ms, t0, tsz):
        return eng.dma_start(dst_f[row_off + m0: row_off + m0 + ms, t0:t0 + tsz], src)
    return f


def emit_mm(nc, ctx, G, tag, *, Ktot, Mtot, wget, rhs_load, out_store,
            bias_get=None, t_ext=None, evac=None, wq="gpsimd"):
    with ExitStack() as pctx:
        sb = lambda *a, **k: pctx.enter_context(nc.sbuf_tensor(*a, **k))
        psum = lambda *a, **k: pctx.enter_context(nc.psum_tensor(*a, **k))
        if t_ext is None:
            t_ext = N if Ktot <= 1100 else 2048
        kch = _chunks(Ktot, 128)
        mch = _chunks(Mtot, 128)
        groups = _chunks(N, t_ext)
        ntb = t_ext // 512
        nwpm = len(kch) + (1 if bias_get else 0)

        rhs_sb = [sb(f"{tag}_r{j}", [128, t_ext], BF16, is_transpose_dest=True)
                  for j in range(len(kch))]
        wsb = [sb(f"{tag}_w{j}", [128, 128], BF16) for j in range(2)]
        ev = [sb(f"{tag}_e{j}", [128, t_ext], BF16) for j in range(2)]
        ps = psum(f"{tag}_ps", [128, t_ext], F32)

        s_rhs = nc._sb.get("sb_rhs")
        s_w = [nc._sb.get(f"sb_w{j}" if wq == "gpsimd" else f"sb_x{j}")
               for j in range(2)]
        s_mm = nc._sb.get("sb_mm")
        s_ev = nc._sb.get("sb_ev")
        s_st = [nc._sb.get(f"sb_st{j}") for j in range(2)]

        p = Prog(nc)
        jw = 0
        wtile_done = []
        ev_done = []
        st_done = [[], []]
        rhs_done = []

        for g, (t0, text) in enumerate(groups):
            waits = [w(s_mm)] if g > 0 else []
            for j, (k0, ks) in enumerate(kch):
                for fi, fn in enumerate(rhs_load(rhs_sb[j], k0, ks, t0, text)):
                    p.add("sync", fn, waits=waits if (j == 0 and fi == 0) else (),
                          inc=s_rhs, inc_n=16)
            rhs_done.append(s_rhs.v)
            for mi, (m0, ms) in enumerate(mch):
                gm = g * len(mch) + mi
                e2 = gm % 2
                for ki, (k0, ks) in enumerate(kch):
                    wb = jw % 2
                    wwaits = []
                    if jw >= 2:
                        wwaits.append((s_mm, wtile_done[jw - 2]))
                    p.add(wq, (lambda e, dst=wsb[wb], k0=k0, ks=ks, m0=m0,
                               ms=ms, wg=wget: wg(e, dst, k0, ks, m0, ms)),
                          waits=wwaits, inc=s_w[wb], inc_n=16)
                    mm_waits = [w(s_w[wb]), (s_rhs, rhs_done[g])]
                    if gm > 0:
                        mm_waits.append((s_ev, ev_done[gm - 1]))
                    for tb in range(ntb):
                        p.add("tensor", (lambda e, tb=tb, wb=wb, ki=ki, ks=ks,
                                         ms=ms, j=ki, last=(ki == len(kch) - 1
                                                            and bias_get is None):
                                         e.matmul(ps[0:ms, tb * 512:(tb + 1) * 512],
                                                  wsb[wb][0:ks, 0:ms],
                                                  rhs_sb[j][0:ks, tb * 512:(tb + 1) * 512],
                                                  start=(ki == 0), stop=last)),
                              waits=mm_waits if tb == 0 else (),
                              inc=s_mm if tb == ntb - 1 else None)
                    wtile_done.append(s_mm.v)
                    jw += 1
                if bias_get is not None:
                    wb = jw % 2
                    wwaits = []
                    if jw >= 2:
                        wwaits.append((s_mm, wtile_done[jw - 2]))
                    p.add("gpsimd", (lambda e, dst=wsb[wb], m0=m0, ms=ms,
                                     bg=bias_get: bg(e, dst[0:1, 0:ms], m0, ms)),
                          waits=wwaits, inc=s_w[wb], inc_n=16)
                    for tb in range(ntb):
                        p.add("tensor", (lambda e, tb=tb, wb=wb, ms=ms:
                                         e.matmul(ps[0:ms, tb * 512:(tb + 1) * 512],
                                                  wsb[wb][0:1, 0:ms],
                                                  G["ones_row"][0:1, :],
                                                  start=False, stop=True)),
                              waits=[w(s_w[wb])] if tb == 0 else (),
                              inc=s_mm if tb == ntb - 1 else None)
                    wtile_done.append(s_mm.v)
                    jw += 1
                evw = [w(s_mm)]
                if gm >= 2:
                    evw.append((s_st[e2], st_done[e2][-1]))
                if evac is None:
                    p.add("vector", (lambda e, e2=e2, ms=ms, text=text:
                                     e.tensor_copy(ev[e2][0:ms, 0:text],
                                                   ps[0:ms, 0:text])),
                          waits=evw, inc=s_ev)
                else:
                    evac(p, evw, s_ev, ps, ev[e2], m0, ms, t0, text)
                ev_done.append(s_ev.v)
                p.add("sync", (lambda e, e2=e2, m0=m0, ms=ms, t0=t0, text=text,
                               os=out_store: os(e, ev[e2][0:ms, 0:text], m0, ms,
                                                t0, text)),
                      waits=[(s_ev, ev_done[gm])], inc=s_st[e2], inc_n=16)
                st_done[e2].append(s_st[e2].v)
        p.add("sync", lambda e: e.nop(), waits=[w(s_st[0]), w(s_st[1])])
        p.emit()


def emit_ln(nc, ctx, G, src, dst_bf, dst_f32, eps, tag, gain=None, bias_row=None,
            rms=False, src_pad=None, dst_pad=None):
    with ExitStack() as pctx:
        sb = lambda *a, **k: pctx.enter_context(nc.sbuf_tensor(*a, **k))
        s_ld = [nc._sb.get(f"sb_l{j}") for j in range(2)]
        s_v = nc._sb.get("sb_v")
        s_a = nc._sb.get("sb_a")
        s_st = [nc._sb.get(f"sb_s{j}") for j in range(2)]
        s_g = nc._sb.get("sb_g")
        xt = [sb(f"{tag}_x{j}", [128, D], BF16 if src_pad is not None else F32)
              for j in range(2)]
        xc = [sb(f"{tag}_c{j}", [128, D], F32) for j in range(2)]
        sq = sb(f"{tag}_sq", [128, D], F32)
        ob = [sb(f"{tag}_ob{j}", [128, D], BF16) for j in range(2)]
        of = [sb(f"{tag}_of{j}", [128, D], F32) for j in range(2)]
        sm = sb(f"{tag}_sm", [128, 8], F32)
        gb = sb(f"{tag}_gb", [128, D], F32) if gain is not None else None
        bb = sb(f"{tag}_bb", [128, D], F32) if bias_row is not None else None
        p = Prog(nc)
        if gain is not None or bias_row is not None:
            psb = pctx.enter_context(nc.psum_tensor(f"{tag}_pg", [128, 1024], F32))
            row = sb(f"{tag}_row", [1, D], BF16)
            rowf = sb(f"{tag}_rowf", [1, D], F32)
            for which, (vec, dstt) in enumerate([(gain, gb), (bias_row, bb)]):
                if vec is None:
                    continue
                v1 = p.add("sync", (lambda e, vec=vec: e.dma_start(
                    rowf[0:1, :], vec)), waits=[w(s_v)], inc=s_g, inc_n=16)
                v2 = p.add("vector", lambda e: e.tensor_copy(row[0:1, :], rowf[0:1, :]),
                           waits=[(s_g, v1)], inc=s_v)
                smm = nc._sb.get("sb_bcmm")
                for h in range(2):
                    p.add("tensor", (lambda e, h=h: e.matmul(
                        psb[:, h * 512:(h + 1) * 512], G["bc_all"][0:1, :],
                        row[0:1, h * 512:(h + 1) * 512],
                        start=True, stop=True)),
                        waits=[(s_v, v2)] if h == 0 else (), inc=smm)
                p.add("vector", (lambda e, dstt=dstt: e.tensor_copy(dstt[:, :], psb[:, :])),
                      waits=[w(smm)], inc=s_v)
        for i in range(NT):
            b = i % 2
            lw = []
            if i >= 2:
                lw.append((s_st[b], s_st[b].v))
            if src_pad is None:
                vld = p.add("sync", (lambda e, b=b, i=i: e.dma_start(
                    xt[b][:, :], src[ts(i, 128), :])), waits=lw, inc=s_ld[b], inc_n=16)
            else:
                pr = pad_row(i * 128)
                p.add("sync", (lambda e, b=b, pr=pr: e.dma_start(
                    xt[b][:, 0:512], src_pad[pr:pr + 128, 0:512])),
                    waits=lw, inc=s_ld[b], inc_n=16)
                vld = p.add("sync", (lambda e, b=b, pr=pr: e.dma_start(
                    xt[b][:, 512:1024], src_pad[pr - 1:pr + 127, 512:1024])),
                    inc=s_ld[b], inc_n=16)
            v1 = p.add("vector", (lambda e, b=b: e.reduce_sum(sm[:, 0:1], xt[b][:, :], AX.X)),
                       waits=[(s_ld[b], vld)], inc=s_v)
            if not rms:
                v2 = p.add("vector", lambda e: e.tensor_scalar(
                    sm[:, 1:2], sm[:, 0:1], -1.0 / D, None, ALU.mult),
                    waits=[(s_v, v1)], inc=s_v)
                v3 = p.add("vector", (lambda e, b=b: e.tensor_scalar(
                    xc[b][:, :], xt[b][:, :], sm[:, 1:2], None, ALU.add)),
                    waits=[(s_v, v2)], inc=s_v)
            else:
                v3 = p.add("vector", (lambda e, b=b: e.tensor_copy(xc[b][:, :], xt[b][:, :])),
                           waits=[(s_v, v1)], inc=s_v)
            va = p.add("scalar", (lambda e, b=b: e.activation(
                sq[:, :], xc[b][:, :], AF.Square, accum_out=sm[:, 2:3])),
                waits=[(s_v, v3)], inc=s_a)
            v4 = p.add("vector", lambda e: e.tensor_scalar(
                sm[:, 3:4], sm[:, 2:3], 1.0 / D, eps, ALU.mult, ALU.add),
                waits=[(s_a, va)], inc=s_v)
            v5 = p.add("vector", lambda e: e.reciprocal(sm[:, 4:5], sm[:, 3:4]),
                       waits=[(s_v, v4)], inc=s_v)
            va2 = p.add("scalar", lambda e: e.activation(sm[:, 5:6], sm[:, 4:5], AF.Sqrt),
                        waits=[(s_v, v5)], inc=s_a)
            if gain is None:
                v6 = p.add("vector", (lambda e, b=b: e.tensor_scalar(
                    ob[b][:, :], xc[b][:, :], sm[:, 5:6], None, ALU.mult)),
                    waits=[(s_a, va2)], inc=s_v)
                v7 = p.add("vector", (lambda e, b=b: e.tensor_scalar(
                    of[b][:, :], xc[b][:, :], sm[:, 5:6], None, ALU.mult)),
                    inc=s_v) if dst_f32 is not None else v6
            else:
                v6a = p.add("vector", (lambda e, b=b: e.tensor_scalar(
                    xc[b][:, :], xc[b][:, :], sm[:, 5:6], None, ALU.mult)),
                    waits=[(s_a, va2)], inc=s_v)
                if bias_row is not None:
                    v6b = p.add("vector", (lambda e, b=b: e.tensor_mul(
                        xc[b][:, :], xc[b][:, :], gb[:, :])),
                        waits=[(s_v, v6a)], inc=s_v)
                    v6 = p.add("vector", (lambda e, b=b: e.tensor_add(
                        xc[b][:, :], xc[b][:, :], bb[:, :])), waits=[(s_v, v6b)], inc=s_v)
                else:
                    v6 = p.add("vector", (lambda e, b=b: e.tensor_mul(
                        xc[b][:, :], xc[b][:, :], gb[:, :])), waits=[(s_v, v6a)], inc=s_v)
                v6 = p.add("vector", (lambda e, b=b: e.tensor_copy(ob[b][:, :], xc[b][:, :])),
                           waits=[(s_v, v6)], inc=s_v)
                v7 = p.add("vector", (lambda e, b=b: e.tensor_copy(of[b][:, :], xc[b][:, :])),
                           inc=s_v) if dst_f32 is not None else v6
            last_v = max(v6, v7 if dst_f32 is not None else v6)
            if dst_bf is not None:
                p.add("gpsimd", (lambda e, b=b, i=i: e.dma_start(
                    dst_bf[ts(i, 128), :], ob[b][:, :])), waits=[(s_v, last_v)],
                    inc=s_st[b], inc_n=16)
            if dst_f32 is not None:
                p.add("gpsimd", (lambda e, b=b, i=i: e.dma_start(
                    dst_f32[ts(i, 128), :], of[b][:, :])), waits=[(s_v, last_v)] if dst_bf is None else (),
                    inc=s_st[b], inc_n=16)
            if dst_pad is not None:
                pr = pad_row(i * 128)
                p.add("gpsimd", (lambda e, b=b, pr=pr: e.dma_start(
                    dst_pad[pr:pr + 128, :], ob[b][:, :])),
                    waits=[(s_v, last_v)] if (dst_bf is None and dst_f32 is None) else (),
                    inc=s_st[b], inc_n=16)
        p.add("gpsimd", lambda e: e.nop(), waits=[w(s_st[0]), w(s_st[1])])
        p.emit()


def emit_copy_out(nc, ctx, src, dst, tag):
    with ExitStack() as pctx:
        sb = lambda *a, **k: pctx.enter_context(nc.sbuf_tensor(*a, **k))
        s_ld = [nc._sb.get(f"sb_l{j}") for j in range(2)]
        s_st = [nc._sb.get(f"sb_y{j}") for j in range(2)]
        t = [sb(f"{tag}_t{j}", [128, D], F32) for j in range(2)]
        p = Prog(nc)
        for i in range(NT):
            b = i % 2
            lw = [(s_st[b], s_st[b].v)] if i >= 2 else []
            vld = p.add("sync", (lambda e, b=b, i=i: e.dma_start(
                t[b][:, :], src[ts(i, 128), :])), waits=lw, inc=s_ld[b], inc_n=16)
            p.add("sync", (lambda e, b=b, i=i: e.dma_start(
                dst[ts(i, 128), :], t[b][:, :])), waits=[(s_ld[b], vld)],
                inc=s_st[b], inc_n=16)
        p.add("sync", lambda e: e.nop(), waits=[w(s_st[0]), w(s_st[1])])
        p.emit()


def _chunks(total, step):
    out = []
    o = 0
    while o < total:
        out.append((o, min(step, total - o)))
        o += step
    return out


def emit_ret_rot(nc, ctx, G, TAB, q_f, k_f, qr_t, kr_t, tag):
    with ExitStack() as pctx:
        sb = lambda *a, **k: pctx.enter_context(nc.sbuf_tensor(*a, **k))
        s_ld = [nc._sb.get(f"sb_l{j}") for j in range(2)]
        s_v = nc._sb.get("sb_v")
        s_st = [nc._sb.get(f"sb_s{j}") for j in range(2)]
        qt = [sb(f"{tag}_q{j}", [128, D], BF16, is_transpose_dest=True) for j in range(2)]
        kt = [sb(f"{tag}_k{j}", [128, D], BF16, is_transpose_dest=True) for j in range(2)]
        rc = [sb(f"{tag}_rc{j}", [128, 512], F32) for j in range(2)]
        rs = [sb(f"{tag}_rs{j}", [128, 512], F32) for j in range(2)]
        rck = [sb(f"{tag}_rck{j}", [128, 512], F32) for j in range(2)]
        rsk = [sb(f"{tag}_rsk{j}", [128, 512], F32) for j in range(2)]
        tmp = sb(f"{tag}_tmp", [128, D], F32)
        oq = [sb(f"{tag}_oq{j}", [128, D], BF16) for j in range(2)]
        ok = [sb(f"{tag}_ok{j}", [128, D], BF16) for j in range(2)]

        def ev3(t, par):
            return bass.AP(t, par, [[D, 128], [64, 16], [2, 32]])

        def tb3(t):
            return bass.AP(t, 0, [[512, 128], [32, 16], [1, 32]])

        p = Prog(nc)
        for i in range(NT):
            b = i % 2
            srow = (i % (S // 128)) * 128
            lw = [(s_st[b], s_st[b].v)] if i >= 2 else []
            p.add("sync", (lambda e, b=b, i=i: e.dma_start(
                qt[b][:, :], q_f[0:D, i * 128:(i + 1) * 128], transpose=True)),
                waits=lw, inc=s_ld[b], inc_n=16)
            p.add("sync", (lambda e, b=b, i=i: e.dma_start(
                kt[b][:, :], k_f[0:D, i * 128:(i + 1) * 128], transpose=True)),
                inc=s_ld[b], inc_n=16)
            for dst, tnm in [(rc[b], "rcq"), (rs[b], "rsq"), (rck[b], "rck"),
                             (rsk[b], "rsk")]:
                p.add("sync", (lambda e, dst=dst, tnm=tnm, srow=srow: e.dma_start(
                    dst[:, :], TAB[tnm][srow:srow + 128, :])), inc=s_ld[b], inc_n=16)
            vld = s_ld[b].v
            for src_t, ctab, stab, dst_t in [(qt, rc, rs, oq), (kt, rck, rsk, ok)]:
                st, dt, ct, sb_ = src_t[b], dst_t[b], ctab[b], stab[b]
                tmp3 = lambda par: bass.AP(tmp, par, [[D, 128], [64, 16], [2, 32]])
                dst3 = lambda par, dt=dt: bass.AP(dt, par, [[D, 128], [64, 16], [2, 32]])
                p.add("vector", (lambda e, st=st, ct=ct, tmp3=tmp3:
                                 e.tensor_tensor(tmp3(0), ev3(st, 0), tb3(ct), ALU.mult)),
                      waits=[(s_ld[b], vld)], inc=s_v)
                p.add("vector", (lambda e, st=st, ct=ct, tmp3=tmp3:
                                 e.tensor_tensor(tmp3(1), ev3(st, 1), tb3(ct), ALU.mult)),
                      inc=s_v)
                a3 = p.add("vector", (lambda e, st=st, sb_=sb_, dst3=dst3:
                                      e.tensor_tensor(dst3(0), ev3(st, 1), tb3(sb_), ALU.mult)),
                           inc=s_v)
                p.add("vector", (lambda e, dst3=dst3, tmp3=tmp3:
                                 e.tensor_tensor(dst3(0), tmp3(0), dst3(0), ALU.subtract)),
                      waits=[(s_v, a3)], inc=s_v)
                a5 = p.add("vector", (lambda e, st=st, sb_=sb_, dst3=dst3:
                                      e.tensor_tensor(dst3(1), ev3(st, 0), tb3(sb_), ALU.mult)),
                           inc=s_v)
                p.add("vector", (lambda e, dst3=dst3, tmp3=tmp3:
                                 e.tensor_tensor(dst3(1), tmp3(1), dst3(1), ALU.add)),
                      waits=[(s_v, a5)], inc=s_v)
            vdone = s_v.v
            p.add("gpsimd", (lambda e, b=b, i=i: e.dma_start(
                qr_t[ts(i, 128), :], oq[b][:, :])), waits=[(s_v, vdone)],
                inc=s_st[b], inc_n=16)
            p.add("gpsimd", (lambda e, b=b, i=i: e.dma_start(
                kr_t[ts(i, 128), :], ok[b][:, :])), inc=s_st[b], inc_n=16)
        p.add("gpsimd", lambda e: e.nop(), waits=[w(s_st[0]), w(s_st[1])])
        p.emit()


def emit_ret_attn(nc, ctx, G, qr_t, kr_t, v_f, ot_f, tag):
    with ExitStack() as pctx:
        sb = lambda *a, **k: pctx.enter_context(nc.sbuf_tensor(*a, **k))
        psum = lambda *a, **k: pctx.enter_context(nc.psum_tensor(*a, **k))
        qr = [sb(f"{tag}_qr{j}", [128, S], BF16, is_transpose_dest=True) for j in range(8)]
        kr = [sb(f"{tag}_kr{j}", [128, S], BF16, is_transpose_dest=True) for j in range(8)]
        vt = [sb(f"{tag}_vt{j}", [128, D], BF16, is_transpose_dest=True) for j in range(16)]
        ssb = [sb(f"{tag}_s{j}", [128, 512], BF16) for j in range(2)]
        sab = [sb(f"{tag}_a{j}", [128, 512], BF16) for j in range(2)]
        dsb = sb(f"{tag}_d", [1, 512], F32)
        dsb2 = sb(f"{tag}_d2", [1, 512], F32)
        rcp = [sb(f"{tag}_rc{j}", [1, 512], BF16) for j in range(2)]
        rbc = sb(f"{tag}_rbc", [128, 512], BF16)
        oev = [sb(f"{tag}_o{j}", [128, 512], BF16) for j in range(2)]
        psS = [psum(f"{tag}_pS{j}", [128, 512], F32) for j in range(2)]
        psAb = [psum(f"{tag}_pA{j}", [1, 512], F32) for j in range(2)]
        psO = [psum(f"{tag}_pO{j}", [128, 512], F32) for j in range(2)]
        psR = psum(f"{tag}_pR", [128, 512], F32)

        s_ld = nc._sb.get("sb_ld")
        s_mm = nc._sb.get("sb_mm")
        s_v = nc._sb.get("sb_v")
        s_a = nc._sb.get("sb_a")
        s_st = [nc._sb.get(f"sb_t{j}") for j in range(2)]

        p = Prog(nc)
        for b in range(B_CORE):
            tb0 = b * S
            ldw = [w(s_mm)] if b > 0 else []
            for j in range(8):
                p.add("sync", (lambda e, j=j, tb0=tb0: e.dma_start(
                    qr[j][:, :], qr_t[tb0:tb0 + S, j * 128:(j + 1) * 128],
                    transpose=True)), waits=ldw if j == 0 else (), inc=s_ld, inc_n=16)
                p.add("sync", (lambda e, j=j, tb0=tb0: e.dma_start(
                    kr[j][:, :], kr_t[tb0:tb0 + S, j * 128:(j + 1) * 128],
                    transpose=True)), inc=s_ld, inc_n=16)
            for j in range(16):
                p.add("sync", (lambda e, j=j, tb0=tb0: e.dma_start(
                    vt[j][:, :], v_f[0:D, tb0 + j * 128: tb0 + (j + 1) * 128],
                    transpose=True)), inc=s_ld, inc_n=16)
            vld = s_ld.v
            for hp in range(8):
                h = 2 * hp
                for qc in range(4):
                    kmax = 4 * qc + 4
                    ev_prev_o = s_st
                    for kc in range(kmax):
                        diag = kc - 4 * qc
                        sc = kc % 2
                        mmw = [(s_ld, vld)]
                        mmw.append((s_v, s_v.v))
                        vmm = p.add("tensor", (lambda e, hp=hp, kc=kc, qc=qc, sc=sc:
                                               e.matmul(psS[sc][:, :],
                                                        kr[hp][0:64, kc * 128:(kc + 1) * 128],
                                                        qr[hp][0:64, qc * 512:(qc + 1) * 512],
                                                        start=True, stop=True,
                                                        tile_position=(0, 0))),
                                    waits=mmw, inc=s_mm)
                        vmm = p.add("tensor", (lambda e, hp=hp, kc=kc, qc=qc, sc=sc:
                                               e.matmul(psS[sc ^ 1][:, :],
                                                        kr[hp][64:128, kc * 128:(kc + 1) * 128],
                                                        qr[hp][64:128, qc * 512:(qc + 1) * 512],
                                                        start=True, stop=True,
                                                        tile_position=(64, 0))),
                                    inc=s_mm)
                        for hh in range(2):
                            src_ps = psS[sc] if hh == 0 else psS[sc ^ 1]
                            if diag >= 0:
                                vv = p.add("vector", (lambda e, hh=hh, src_ps=src_ps, diag=diag:
                                                      e.tensor_tensor(ssb[hh][:, :], src_ps[:, :],
                                                                      G[f"trilq_{diag}"][:, :],
                                                                      ALU.mult)),
                                           waits=[(s_mm, vmm)], inc=s_v)
                            else:
                                vv = p.add("vector", (lambda e, hh=hh, src_ps=src_ps:
                                                      e.tensor_copy(ssb[hh][:, :], src_ps[:, :])),
                                           waits=[(s_mm, vmm)], inc=s_v)
                            vaa = p.add("scalar", (lambda e, hh=hh: e.activation(
                                sab[hh][:, :], ssb[hh][:, :], AF.Abs)),
                                waits=[(s_v, vv)], inc=s_a)
                            p.add("tensor", (lambda e, hh=hh, kc=kc:
                                             e.matmul(psAb[hh][:, :], G["e0col"][:, 0:1],
                                                      sab[hh][:, :], start=(kc == 0),
                                                      stop=(kc == kmax - 1))),
                                  waits=[(s_a, vaa)], inc=s_mm)
                            p.add("tensor", (lambda e, hh=hh, kc=kc, hp=hp, kmax=kmax:
                                             e.matmul(psO[hh][hh * 64:(hh + 1) * 64, :],
                                                      vt[kc][:, (2 * hp + hh) * 64:(2 * hp + hh + 1) * 64],
                                                      ssb[hh][:, :], start=(kc == 0),
                                                      stop=(kc == kmax - 1),
                                                      tile_position=(0, 64 * hh))),
                                  waits=[(s_v, vv)], inc=s_mm)
                    vmm_all = s_mm.v
                    for hh in range(2):
                        vv = p.add("vector", (lambda e, hh=hh: e.tensor_scalar(
                            dsb[0:1, :], psAb[hh][0:1, :], 1.0, 50000.0, ALU.max, ALU.min)),
                            waits=[(s_mm, vmm_all)], inc=s_v)
                        vv = p.add("vector", lambda e: e.reciprocal(dsb2[0:1, :], dsb[0:1, :]),
                                   waits=[(s_v, vv)], inc=s_v)
                        vv = p.add("vector", (lambda e, hh=hh: e.tensor_copy(
                            rcp[hh][0:1, :], dsb2[0:1, :])), waits=[(s_v, vv)], inc=s_v)
                    vrc = s_v.v
                    vbc = p.add("tensor", (lambda e: e.matmul(
                        psR[:, :], G["bc_lo"][0:1, :], rcp[0][0:1, :], start=True,
                        stop=False)), waits=[(s_v, vrc)], inc=s_mm)
                    vbc = p.add("tensor", (lambda e: e.matmul(
                        psR[:, :], G["bc_hi"][0:1, :], rcp[1][0:1, :], start=False,
                        stop=True)), inc=s_mm)
                    vv = p.add("vector", lambda e: e.tensor_copy(rbc[:, :], psR[:, :]),
                               waits=[(s_mm, vbc)], inc=s_v)
                    e2 = (hp * 4 + qc) % 2
                    stw = [(s_st[e2], s_st[e2].v)]
                    vo = p.add("vector", (lambda e, e2=e2: e.tensor_tensor(
                        oev[e2][0:64, :], psO[0][0:64, :], rbc[0:64, :], ALU.mult)),
                        waits=[(s_v, vv)] + stw, inc=s_v)
                    vo = p.add("vector", (lambda e, e2=e2: e.tensor_tensor(
                        oev[e2][64:128, :], psO[1][64:128, :], rbc[64:128, :],
                        ALU.mult)), waits=[(s_v, vo)], inc=s_v)
                    p.add("sync", (lambda e, e2=e2, hp=hp, qc=qc, tb0=tb0: e.dma_start(
                        ot_f[hp * 128:(hp + 1) * 128, tb0 + qc * 512: tb0 + (qc + 1) * 512],
                        oev[e2][:, :])), waits=[(s_v, vo)], inc=s_st[e2], inc_n=16)
        p.add("sync", lambda e: e.nop(), waits=[w(s_st[0]), w(s_st[1])])
        p.emit()


def emit_ret_final(nc, ctx, G, ot_f, g_f, s0_t, tag):
    with ExitStack() as pctx:
        sb = lambda *a, **k: pctx.enter_context(nc.sbuf_tensor(*a, **k))
        s_ld = [nc._sb.get(f"sb_l{j}") for j in range(2)]
        s_v = nc._sb.get("sb_v")
        s_a = nc._sb.get("sb_a")
        s_st = [nc._sb.get(f"sb_s{j}") for j in range(2)]
        ot = [sb(f"{tag}_o{j}", [128, D], BF16, is_transpose_dest=True) for j in range(2)]
        gt = [sb(f"{tag}_g{j}", [128, D], BF16, is_transpose_dest=True) for j in range(2)]
        sq = sb(f"{tag}_sq", [128, D], F32)
        sil = sb(f"{tag}_sil", [128, D], F32)
        grp = sb(f"{tag}_grp", [128, 16 * 4], F32)
        o1 = sb(f"{tag}_onorm", [128, D], F32)
        of = [sb(f"{tag}_of{j}", [128, D], F32) for j in range(2)]

        def g3(col):
            return bass.AP(grp, col * 16, [[64, 128], [1, 16]])

        p = Prog(nc)
        for i in range(NT):
            b = i % 2
            lw = [(s_st[b], s_st[b].v)] if i >= 2 else []
            v1 = p.add("sync", (lambda e, b=b, i=i: e.dma_start(
                ot[b][:, :], ot_f[0:D, i * 128:(i + 1) * 128], transpose=True)),
                waits=lw, inc=s_ld[b], inc_n=16)
            p.add("sync", (lambda e, b=b, i=i: e.dma_start(
                gt[b][:, :], g_f[0:D, i * 128:(i + 1) * 128], transpose=True)),
                inc=s_ld[b], inc_n=16)
            vld = s_ld[b].v
            va = p.add("scalar", (lambda e, b=b: e.activation(sq[:, :], ot[b][:, :],
                                                              AF.Square)),
                       waits=[(s_ld[b], vld)], inc=s_a)
            vv = p.add("vector", lambda e: e.tensor_reduce(
                g3(0), bass.AP(sq, 0, [[D, 128], [64, 16], [1, 64]]), AX.X, ALU.add),
                waits=[(s_a, va)], inc=s_v)
            vv = p.add("vector", lambda e: e.tensor_scalar(
                g3(1), g3(0), 1.0 / 64, 1e-6, ALU.mult, ALU.add),
                waits=[(s_v, vv)], inc=s_v)
            vv = p.add("vector", lambda e: e.reciprocal(g3(2), g3(1)),
                       waits=[(s_v, vv)], inc=s_v)
            va2 = p.add("scalar", lambda e: e.activation(g3(3), g3(2), AF.Sqrt),
                        waits=[(s_v, vv)], inc=s_a)
            va3 = p.add("scalar", (lambda e, b=b: e.activation(sil[:, :], gt[b][:, :],
                                                               AF.Silu)),
                        waits=[(s_ld[b], vld)], inc=s_a)
            vv = p.add("vector", (lambda e, b=b: e.tensor_tensor(
                bass.AP(o1, 0, [[D, 128], [64, 16], [1, 64]]),
                bass.AP(ot[b], 0, [[D, 128], [64, 16], [1, 64]]),
                bass.AP(grp, 48, [[64, 128], [1, 16], [0, 64]]), ALU.mult)),
                waits=[(s_a, va2)], inc=s_v)
            vv = p.add("vector", (lambda e, b=b: e.tensor_mul(of[b][:, :], o1[:, :],
                                                              sil[:, :])),
                       waits=[(s_v, vv), (s_a, va3)], inc=s_v)
            p.add("gpsimd", (lambda e, b=b, i=i: e.dma_start(
                s0_t[ts(i, 128), :], of[b][:, :])), waits=[(s_v, vv)],
                inc=s_st[b], inc_n=16)
        p.add("gpsimd", lambda e: e.nop(), waits=[w(s_st[0]), w(s_st[1])])
        p.emit()


PAD_S = S + 1


def pad_row(tok):
    b0, s0 = tok // S, tok % S
    return b0 * PAD_S + 1 + s0


def emit_zero_pad_rows(nc, ctx, G, dst_pad, tag):
    s_z = nc._sb.get("sb_z")
    p = Prog(nc)
    for b0 in range(B_CORE):
        p.add("sync", (lambda e, b0=b0: e.dma_start(
            dst_pad[b0 * PAD_S: b0 * PAD_S + 1, :], G["zrow"][0:1, :])),
            inc=s_z, inc_n=16)
    p.add("sync", lambda e: e.nop(), waits=[w(s_z)])
    p.emit()


def emit_ew_F(nc, ctx, G, tag, *, Ktot, src_a, src_g, dst, act=AF.Gelu,
              t_step=2048, a_off=0, g_off=0, ce_f=None, ce_row=0):
    with ExitStack() as pctx:
        sb = lambda *a, **k: pctx.enter_context(nc.sbuf_tensor(*a, **k))
        psum = lambda *a, **k: pctx.enter_context(nc.psum_tensor(*a, **k))
        s_ld = [nc._sb.get(f"sb_l{j}") for j in range(2)]
        s_v = nc._sb.get("sb_v")
        s_a = nc._sb.get("sb_a")
        s_mm = nc._sb.get("sb_mm")
        s_st = [nc._sb.get(f"sb_y{j}") for j in range(2)]
        at = [sb(f"{tag}_at{j}", [128, t_step], BF16) for j in range(2)]
        gt = [sb(f"{tag}_gt{j}", [128, t_step], BF16) for j in range(2)]
        ga = [sb(f"{tag}_ga{j}", [128, t_step], BF16) for j in range(2)]
        ot = [sb(f"{tag}_ot{j}", [128, t_step], BF16) for j in range(2)]
        if ce_f is not None:
            cesb = sb(f"{tag}_ce", [128, t_step], BF16)
            cerow = sb(f"{tag}_cer", [8, t_step], BF16)
            psce = psum(f"{tag}_pce", [128, t_step], F32)
        p = Prog(nc)
        kch = _chunks(Ktot, 128)
        tiles = [(k0, ks, t0) for (t0, _) in _chunks(N, t_step)
                 for (k0, ks) in kch]
        vce = 0
        for i, (k0, ks, t0) in enumerate(tiles):
            b = i % 2
            first_k = (i % len(kch) == 0)
            lw = [(s_st[b], s_st[b].v)] if i >= 2 else []
            if ce_f is not None and first_k:
                vr = p.add("sync", (lambda e, t0=t0: e.dma_start(
                    cerow[0:8, :], ce_f[0:8, t0:t0 + t_step])),
                    waits=[(s_v, s_v.v)], inc=s_ld[0], inc_n=16)
                vm = None
                for j in range(t_step // 512):
                    vm = p.add("tensor", (lambda e, j=j: e.matmul(
                        psce[:, j * 512:(j + 1) * 512],
                        G["sel_moe"][0:8, ce_row * 128:(ce_row + 1) * 128],
                        cerow[0:8, j * 512:(j + 1) * 512], start=True, stop=True)),
                        waits=[(s_ld[0], vr)] if j == 0 else (), inc=s_mm)
                vce = p.add("vector", (lambda e: e.tensor_copy(cesb[:, :], psce[:, :])),
                            waits=[(s_mm, vm)], inc=s_v)
            p.add("sync", (lambda e, b=b, k0=k0, ks=ks, t0=t0: e.dma_start(
                at[b][0:ks, :], src_a[a_off + k0:a_off + k0 + ks, t0:t0 + t_step])),
                waits=lw, inc=s_ld[b], inc_n=16)
            vld = p.add("sync", (lambda e, b=b, k0=k0, ks=ks, t0=t0: e.dma_start(
                gt[b][0:ks, :], src_g[g_off + k0:g_off + k0 + ks, t0:t0 + t_step])),
                inc=s_ld[b], inc_n=16)
            va = p.add("scalar", (lambda e, b=b, ks=ks: e.activation(
                ga[b][0:ks, :], gt[b][0:ks, :], act)),
                waits=[(s_ld[b], vld)], inc=s_a)
            vv = p.add("vector", (lambda e, b=b, ks=ks: e.tensor_mul(
                ot[b][0:ks, :], at[b][0:ks, :], ga[b][0:ks, :])),
                waits=[(s_a, va)] + ([(s_v, vce)] if ce_f is not None else []),
                inc=s_v)
            if ce_f is not None:
                vv = p.add("vector", (lambda e, b=b, ks=ks: e.tensor_mul(
                    ot[b][0:ks, :], ot[b][0:ks, :], cesb[0:ks, :])),
                    waits=[(s_v, vv)], inc=s_v)
            p.add("sync", (lambda e, b=b, k0=k0, ks=ks, t0=t0: e.dma_start(
                dst[k0:k0 + ks, t0:t0 + t_step], ot[b][0:ks, :])),
                waits=[(s_v, vv)], inc=s_st[b], inc_n=16)
        p.add("sync", lambda e: e.nop(), waits=[w(s_st[0]), w(s_st[1])])
        p.emit()


def emit_lffn_rot(nc, ctx, G, TAB, IN, l, qkv_f, qr_t, kr_t, qn_t, kn_t, tag):
    with ExitStack() as pctx:
        sb = lambda *a, **k: pctx.enter_context(nc.sbuf_tensor(*a, **k))
        ps = pctx.enter_context(nc.psum_tensor(f"{tag}_ps", [128, 512], F32))
        s_ld = [nc._sb.get(f"sb_l{j}") for j in range(2)]
        s_v = nc._sb.get("sb_v")
        s_a = nc._sb.get("sb_a")
        s_mm = nc._sb.get("sb_mm")
        s_st = [nc._sb.get(f"sb_s{j}") for j in range(2)]
        qt = [sb(f"{tag}_q{j}", [128, 512], BF16, is_transpose_dest=True) for j in range(2)]
        kt = [sb(f"{tag}_k{j}", [128, 512], BF16, is_transpose_dest=True) for j in range(2)]
        xc_ = [[sb(f"{tag}_xc{b}{j}", [128, 32], F32) for j in range(4)]
               for b in range(2)]
        sq = sb(f"{tag}_sq", [128, 512], F32)
        nrm = sb(f"{tag}_n", [128, 32], F32)
        qsr = sb(f"{tag}_qsr", [1, 64], BF16)
        ksr = sb(f"{tag}_ksr", [1, 64], BF16)
        qsb = sb(f"{tag}_qsb", [128, 64], F32)
        ksb = sb(f"{tag}_ksb", [128, 64], F32)
        t1 = sb(f"{tag}_t1", [128, 512], F32)
        t2 = sb(f"{tag}_t2", [128, 512], F32)
        oq = [sb(f"{tag}_oq{j}", [128, 512], BF16) for j in range(2)]
        okk = [sb(f"{tag}_okk{j}", [128, 512], BF16) for j in range(2)]
        nq = [sb(f"{tag}_nq{j}", [128, 8], F32) for j in range(2)]
        nk = [sb(f"{tag}_nk{j}", [128, 8], F32) for j in range(2)]

        def v3(t, off=0, inner=64):
            return bass.AP(t, off, [[512, 128], [64, 8], [1, inner]])

        def v3h(t, off):
            return bass.AP(t, off, [[512, 128], [64, 8], [1, 32]])

        def tabb(t):
            return bass.AP(t, 0, [[32, 128], [0, 8], [1, 32]])

        s_qs = nc._sb.get("sb_qs")
        p = Prog(nc)
        vq = p.add("gpsimd", (lambda e: e.dma_start(
            qsr[0:1, :], IN["L0_ffn_q_scale"][l:l + 1, :])), inc=s_qs, inc_n=16)
        vq = p.add("gpsimd", (lambda e: e.dma_start(
            ksr[0:1, :], IN["L0_ffn_k_scale"][l:l + 1, :])), inc=s_qs, inc_n=16)
        vmm = p.add("tensor", (lambda e: e.matmul(
            ps[:, 0:64], G["bc_all"][0:1, :], qsr[0:1, :], start=True, stop=True)),
            waits=[(s_qs, s_qs.v)], inc=s_mm)
        vmm = p.add("tensor", (lambda e: e.matmul(
            ps[:, 64:128], G["bc_all"][0:1, :], ksr[0:1, :], start=True, stop=True)),
            inc=s_mm)
        vv0 = p.add("vector", lambda e: e.tensor_copy(qsb[:, :], ps[:, 0:64]),
                    waits=[(s_mm, vmm)], inc=s_v)
        vv0 = p.add("vector", lambda e: e.tensor_copy(ksb[:, :], ps[:, 64:128]),
                    inc=s_v)

        for i in range(NT):
            b = i % 2
            srow = (i % (S // 128)) * 128
            lw = [(s_st[b], s_st[b].v)] if i >= 2 else []
            p.add("sync", (lambda e, b=b, i=i: e.dma_start(
                qt[b][:, :], qkv_f[0:512, i * 128:(i + 1) * 128], transpose=True)),
                waits=lw, inc=s_ld[b], inc_n=16)
            vld = p.add("sync", (lambda e, b=b, i=i: e.dma_start(
                kt[b][:, :], qkv_f[512:1024, i * 128:(i + 1) * 128], transpose=True)),
                inc=s_ld[b], inc_n=16)
            for j, tnm in enumerate(["xc_q", "xs_q", "xc_k", "xs_k"]):
                vld = p.add("sync", (lambda e, b=b, j=j, tnm=tnm, srow=srow: e.dma_start(
                    xc_[b][j][:, :], TAB[tnm][srow:srow + 128, :])), inc=s_ld[b], inc_n=16)
            for src_t, scb, ci, si, dst_t, nrm_t in [
                    (qt, qsb, 0, 1, oq, nq), (kt, ksb, 2, 3, okk, nk)]:
                st, dt, nt = src_t[b], dst_t[b], nrm_t[b]
                va = p.add("scalar", (lambda e, st=st: e.activation(
                    sq[:, :], st[:, :], AF.Square)),
                    waits=[(s_ld[b], vld), (s_v, s_v.v)], inc=s_a)
                vv = p.add("vector", (lambda e: e.tensor_reduce(
                    nrm[:, 0:8], v3(sq), AX.X, ALU.add)),
                    waits=[(s_a, va)], inc=s_v)
                va = p.add("scalar", (lambda e: e.activation(
                    nrm[:, 8:16], nrm[:, 0:8], AF.Sqrt)),
                    waits=[(s_v, vv)], inc=s_a)
                vv = p.add("vector", (lambda e: e.tensor_scalar(
                    nrm[:, 16:24], nrm[:, 8:16], 1e-12, None, ALU.max)),
                    waits=[(s_a, va)], inc=s_v)
                vv = p.add("vector", (lambda e: e.reciprocal(
                    nrm[:, 24:32], nrm[:, 16:24])), waits=[(s_v, vv)], inc=s_v)
                vv = p.add("vector", (lambda e, st=st: e.tensor_tensor(
                    v3(t1), v3(st), bass.AP(nrm, 24, [[32, 128], [1, 8], [0, 64]]),
                    ALU.mult)), waits=[(s_v, vv)], inc=s_v)
                vv = p.add("vector", (lambda e, scb=scb: e.tensor_tensor(
                    v3(t1), v3(t1), bass.AP(scb, 0, [[64, 128], [0, 8], [1, 64]]),
                    ALU.mult)), waits=[(s_v, vv)], inc=s_v)
                va = p.add("scalar", (lambda e: e.activation(
                    sq[:, :], t1[:, :], AF.Square)), waits=[(s_v, vv)], inc=s_a)
                vvn = p.add("vector", (lambda e: e.tensor_reduce(
                    nrm[:, 0:8], v3(sq), AX.X, ALU.add)),
                    waits=[(s_a, va)], inc=s_v)
                va = p.add("scalar", (lambda e, nt=nt: e.activation(
                    nt[:, :], nrm[:, 0:8], AF.Sqrt)), waits=[(s_v, vvn)], inc=s_a)
                vv = p.add("vector", (lambda e, b=b, ci=ci: e.tensor_tensor(
                    v3h(t2, 0), v3h(t1, 0), tabb(xc_[b][ci]), ALU.mult)),
                    waits=[(s_v, vv)], inc=s_v)
                vv = p.add("vector", (lambda e, b=b, ci=ci: e.tensor_tensor(
                    v3h(t2, 32), v3h(t1, 32), tabb(xc_[b][ci]), ALU.mult)),
                    waits=[(s_v, vv)], inc=s_v)
                vv = p.add("vector", (lambda e, b=b, si=si, dt=dt: e.tensor_tensor(
                    v3h(dt, 0), v3h(t1, 32), tabb(xc_[b][si]), ALU.mult)),
                    waits=[(s_v, vv)], inc=s_v)
                vv = p.add("vector", (lambda e, dt=dt: e.tensor_tensor(
                    v3h(dt, 0), v3h(t2, 0), v3h(dt, 0), ALU.subtract)),
                    waits=[(s_v, vv)], inc=s_v)
                vv = p.add("vector", (lambda e, b=b, si=si, dt=dt: e.tensor_tensor(
                    v3h(dt, 32), v3h(t1, 0), tabb(xc_[b][si]), ALU.mult)),
                    waits=[(s_v, vv)], inc=s_v)
                vv = p.add("vector", (lambda e, dt=dt: e.tensor_tensor(
                    v3h(dt, 32), v3h(t2, 32), v3h(dt, 32), ALU.add)),
                    waits=[(s_v, vv)], inc=s_v)
            vdone_v, vdone_a = s_v.v, s_a.v
            for dst_hbm, src_sb, wv in [(qr_t, oq[b], (s_v, vdone_v)),
                                        (kr_t, okk[b], (s_v, vdone_v))]:
                p.add("gpsimd", (lambda e, dst_hbm=dst_hbm, src_sb=src_sb, i=i:
                                 e.dma_start(dst_hbm[ts(i, 128), :], src_sb[:, :])),
                      waits=[wv], inc=s_st[b], inc_n=16)
            for dst_hbm, src_sb in [(qn_t, nq[b]), (kn_t, nk[b])]:
                p.add("gpsimd", (lambda e, dst_hbm=dst_hbm, src_sb=src_sb, i=i:
                                 e.dma_start(dst_hbm[ts(i, 128), :], src_sb[:, :])),
                      waits=[(s_a, vdone_a)], inc=s_st[b], inc_n=16)
        p.add("gpsimd", lambda e: e.nop(), waits=[w(s_st[0]), w(s_st[1])])
        p.emit()


def emit_lattn(nc, ctx, G, qr_t, kr_t, qkv_f, qn_t, kn_t, ot_f, tag):
    with ExitStack() as pctx:
        sb = lambda *a, **k: pctx.enter_context(nc.sbuf_tensor(*a, **k))
        psum = lambda *a, **k: pctx.enter_context(nc.psum_tensor(*a, **k))
        qr = [sb(f"{tag}_qr{j}", [128, S], BF16, is_transpose_dest=True) for j in range(4)]
        kr = [sb(f"{tag}_kr{j}", [128, S], BF16, is_transpose_dest=True) for j in range(4)]
        vt = [sb(f"{tag}_vt{j}", [128, 512], BF16, is_transpose_dest=True) for j in range(16)]
        nqt = [sb(f"{tag}_nq{j}", [128, 8], F32) for j in range(16)]
        nkt = [sb(f"{tag}_nk{j}", [128, 8], F32) for j in range(16)]
        qnf = sb(f"{tag}_qnf", [8, S], F32)
        knf = sb(f"{tag}_knf", [8, S], F32)
        wrk = sb(f"{tag}_wrk", [8, 16], F32)
        cbt = sb(f"{tag}_cbt", [8, 4], BF16)
        cb = sb(f"{tag}_cb", [128, 32], F32)
        cbp = sb(f"{tag}_cbp", [128, 32], F32)
        pp2 = [sb(f"{tag}_pq{j}", [128, 512], BF16) for j in range(2)]
        pp = [sb(f"{tag}_pp{j}", [128, 512], BF16) for j in range(2)]
        dsb = sb(f"{tag}_d", [1, 512], F32)
        rcp = [sb(f"{tag}_rc{j}", [1, 512], BF16) for j in range(2)]
        rbc = sb(f"{tag}_rbc", [128, 512], BF16)
        oev = [sb(f"{tag}_o{j}", [128, 512], BF16) for j in range(2)]
        psS = [psum(f"{tag}_pS{j}", [128, 512], F32) for j in range(2)]
        psAb = [psum(f"{tag}_pA{j}", [1, 512], F32) for j in range(2)]
        psO = [psum(f"{tag}_pO{j}", [128, 512], F32) for j in range(2)]
        psR = psum(f"{tag}_pR", [128, 512], F32)
        psTC = psum(f"{tag}_pT", [128, 512], F32)

        s_ld = nc._sb.get("sb_ld")
        s_mm = nc._sb.get("sb_mm")
        s_v = nc._sb.get("sb_v")
        s_a = nc._sb.get("sb_a")
        s_st = [nc._sb.get(f"sb_t{j}") for j in range(2)]

        p = Prog(nc)
        for b in range(B_CORE):
            tb0 = b * S
            ldw = [w(s_mm), w(s_v)] if b > 0 else []
            for j in range(4):
                p.add("sync", (lambda e, j=j, tb0=tb0: e.dma_start(
                    qr[j][:, :], qr_t[tb0:tb0 + S, j * 128:(j + 1) * 128],
                    transpose=True)), waits=ldw if j == 0 else (), inc=s_ld, inc_n=16)
                p.add("sync", (lambda e, j=j, tb0=tb0: e.dma_start(
                    kr[j][:, :], kr_t[tb0:tb0 + S, j * 128:(j + 1) * 128],
                    transpose=True)), inc=s_ld, inc_n=16)
            for j in range(16):
                p.add("sync", (lambda e, j=j, tb0=tb0: e.dma_start(
                    vt[j][:, :], qkv_f[1024:1536, tb0 + j * 128: tb0 + (j + 1) * 128],
                    transpose=True)), inc=s_ld, inc_n=16)
                p.add("sync", (lambda e, j=j, tb0=tb0: e.dma_start(
                    nqt[j][:, :], qn_t[tb0 + j * 128: tb0 + (j + 1) * 128, :])),
                    inc=s_ld, inc_n=16)
                p.add("sync", (lambda e, j=j, tb0=tb0: e.dma_start(
                    nkt[j][:, :], kn_t[tb0 + j * 128: tb0 + (j + 1) * 128, :])),
                    inc=s_ld, inc_n=16)
            vld = s_ld.v
            for j in range(16):
                for src_sb, dst_sb in [(nqt[j], qnf), (nkt[j], knf)]:
                    vmm = p.add("tensor", (lambda e, src_sb=src_sb: e.transpose(
                        psTC[0:8, 0:128], src_sb[:, :], G["ident_f32"][:, :])),
                        waits=[(s_ld, vld), (s_v, s_v.v)], inc=s_mm)
                    p.add("vector", (lambda e, dst_sb=dst_sb, j=j: e.tensor_copy(
                        dst_sb[0:8, j * 128:(j + 1) * 128], psTC[0:8, 0:128])),
                          waits=[(s_mm, vmm)], inc=s_v)
            vv = p.add("vector", (lambda e: e.tensor_reduce(
                wrk[0:8, 0:4], bass.AP(qnf, 0, [[S, 8], [512, 4], [1, 512]]),
                AX.X, ALU.max)), waits=[(s_v, s_v.v)], inc=s_v)
            vv = p.add("vector", (lambda e: e.tensor_reduce(
                wrk[0:8, 4:8], bass.AP(knf, 0, [[S, 8], [512, 4], [1, 512]]),
                AX.X, ALU.max)), waits=[(s_v, vv)], inc=s_v)
            vv = p.add("vector", lambda e: e.tensor_copy(wrk[0:8, 8:9], wrk[0:8, 4:5]),
                       waits=[(s_v, vv)], inc=s_v)
            vv = p.add("vector", lambda e: e.tensor_tensor(
                wrk[0:8, 9:12], wrk[0:8, 4:7], wrk[0:8, 5:8], ALU.max),
                waits=[(s_v, vv)], inc=s_v)
            vv = p.add("vector", lambda e: e.tensor_tensor(
                wrk[0:8, 12:16], wrk[0:8, 0:4], wrk[0:8, 8:12], ALU.mult),
                waits=[(s_v, vv)], inc=s_v)
            vv = p.add("vector", lambda e: e.tensor_scalar(
                cbt[0:8, 0:4], wrk[0:8, 12:16], 8.0, None, ALU.mult),
                waits=[(s_v, vv)], inc=s_v)
            vmm = None
            for h in range(8):
                for ww in range(4):
                    j = h * 4 + ww
                    vmm = p.add("tensor", (lambda e, h=h, ww=ww, j=j: e.matmul(
                        psTC[:, 128 + j:129 + j], G["sel_moe"][0:8, h * 128:(h + 1) * 128],
                        cbt[0:8, ww:ww + 1], start=(j == 0), stop=(j == 31))),
                        waits=[(s_v, vv)] if j == 0 else (), inc=s_mm)
            vv = p.add("vector", lambda e: e.tensor_scalar(
                cb[:, :], psTC[:, 128:160], -1.0, None, ALU.mult),
                waits=[(s_mm, vmm)], inc=s_v)
            vv = p.add("vector", lambda e: e.tensor_scalar(
                cbp[:, :], psTC[:, 128:160], 0.125, None, ALU.mult),
                waits=[(s_v, vv)], inc=s_v)
            vcb = vv
            for hp in range(4):
                for ww in range(4):
                    kcs = list(range(4, 8)) if ww == 0 else list(range(8))
                    for kc in kcs:
                        kcol = 512 * (ww - 1) + 128 * kc
                        sc = kc % 2
                        mmw = [(s_ld, vld), (s_v, s_v.v)]
                        p.add("tensor", (lambda e, hp=hp, kcol=kcol, ww=ww, sc=sc:
                                         e.matmul(psS[sc][:, :],
                                                  kr[hp][0:64, kcol:kcol + 128],
                                                  qr[hp][0:64, ww * 512:(ww + 1) * 512],
                                                  start=True, stop=True,
                                                  tile_position=(0, 0))),
                              waits=mmw, inc=s_mm)
                        vmm = p.add("tensor", (lambda e, hp=hp, kcol=kcol, ww=ww, sc=sc:
                                               e.matmul(psS[sc ^ 1][:, :],
                                                        kr[hp][64:128, kcol:kcol + 128],
                                                        qr[hp][64:128, ww * 512:(ww + 1) * 512],
                                                        start=True, stop=True,
                                                        tile_position=(64, 0))),
                                    inc=s_mm)
                        mtile = G[f"lamu_{kc}"] if kc < 4 else G[f"trilq_{kc - 4}"]
                        for hh in range(2):
                            h = hp * 2 + hh
                            src_ps = psS[sc] if hh == 0 else psS[sc ^ 1]
                            vv = p.add("vector", (lambda e, hh=hh, src_ps=src_ps,
                                                  h=h, ww=ww:
                                                  e.tensor_scalar(pp2[hh][:, :],
                                                                  src_ps[:, :],
                                                                  cbp[:, h * 4 + ww: h * 4 + ww + 1],
                                                                  None, ALU.min)),
                                       waits=[(s_mm, vmm), (s_v, vcb)], inc=s_v)
                            va = p.add("scalar", (lambda e, hh=hh, h=h, ww=ww:
                                                  e.activation(pp[hh][:, :], pp2[hh][:, :],
                                                               AF.Exp, scale=8.0,
                                                               bias=cb[:, h * 4 + ww: h * 4 + ww + 1])),
                                       waits=[(s_v, vv)], inc=s_a)
                            vv = p.add("vector", (lambda e, hh=hh, mtile=mtile:
                                                  e.tensor_mul(pp[hh][:, :], pp[hh][:, :],
                                                               mtile[:, :])),
                                       waits=[(s_a, va)], inc=s_v)
                            p.add("tensor", (lambda e, hh=hh, first=(kc == kcs[0]),
                                             last=(kc == kcs[-1]):
                                             e.matmul(psAb[hh][:, :], G["e0col"][:, 0:1],
                                                      pp[hh][:, :], start=first, stop=last)),
                                  waits=[(s_v, vv)], inc=s_mm)
                            p.add("tensor", (lambda e, hh=hh, h=h, ww=ww, kc=kc,
                                             first=(kc == kcs[0]), last=(kc == kcs[-1]):
                                             e.matmul(psO[hh][hh * 64:(hh + 1) * 64, :],
                                                      vt[(512 * (ww - 1) + 128 * kc) // 128]
                                                      [:, h * 64:(h + 1) * 64],
                                                      pp[hh][:, :], start=first, stop=last,
                                                      tile_position=(0, 64 * hh))),
                                  inc=s_mm)
                    vmm_all = s_mm.v
                    for hh in range(2):
                        vv = p.add("vector", (lambda e, hh=hh: e.tensor_scalar(
                            dsb[0:1, :], psAb[hh][0:1, :], 1e-30, None, ALU.max)),
                            waits=[(s_mm, vmm_all)], inc=s_v)
                        vv = p.add("vector", lambda e: e.reciprocal(dsb[0:1, :], dsb[0:1, :]),
                                   waits=[(s_v, vv)], inc=s_v)
                        vv = p.add("vector", (lambda e, hh=hh: e.tensor_copy(
                            rcp[hh][0:1, :], dsb[0:1, :])), waits=[(s_v, vv)], inc=s_v)
                    vrc = s_v.v
                    vbc = p.add("tensor", (lambda e: e.matmul(
                        psR[:, :], G["bc_lo"][0:1, :], rcp[0][0:1, :], start=True,
                        stop=False)), waits=[(s_v, vrc)], inc=s_mm)
                    vbc = p.add("tensor", (lambda e: e.matmul(
                        psR[:, :], G["bc_hi"][0:1, :], rcp[1][0:1, :], start=False,
                        stop=True)), inc=s_mm)
                    vv = p.add("vector", lambda e: e.tensor_copy(rbc[:, :], psR[:, :]),
                               waits=[(s_mm, vbc)], inc=s_v)
                    e2 = (hp * 4 + ww) % 2
                    stw = [(s_st[e2], s_st[e2].v)]
                    vo = p.add("vector", (lambda e, e2=e2: e.tensor_tensor(
                        oev[e2][0:64, :], psO[0][0:64, :], rbc[0:64, :], ALU.mult)),
                        waits=[(s_v, vv)] + stw, inc=s_v)
                    vo = p.add("vector", (lambda e, e2=e2: e.tensor_tensor(
                        oev[e2][64:128, :], psO[1][64:128, :], rbc[64:128, :],
                        ALU.mult)), waits=[(s_v, vo)], inc=s_v)
                    p.add("sync", (lambda e, e2=e2, hp=hp, ww=ww, tb0=tb0: e.dma_start(
                        ot_f[hp * 128:(hp + 1) * 128, tb0 + ww * 512: tb0 + (ww + 1) * 512],
                        oev[e2][:, :])), waits=[(s_v, vo)], inc=s_st[e2], inc_n=16)
        p.add("sync", lambda e: e.nop(), waits=[w(s_st[0]), w(s_st[1])])
        p.emit()


def emit_combine(nc, ctx, G, tag, *, srcs, dst_f32=None, dst_pad=None,
                 dst_bf=None, mode="resid"):
    with ExitStack() as pctx:
        sb = lambda *a, **k: pctx.enter_context(nc.sbuf_tensor(*a, **k))
        s_ld = [nc._sb.get(f"sb_l{j}") for j in range(2)]
        s_v = nc._sb.get("sb_v")
        s_a = nc._sb.get("sb_a")
        s_st = [nc._sb.get(f"sb_s{j}") for j in range(2)]
        at = [sb(f"{tag}_a{j}", [128, D], F32) for j in range(2)]
        bt = [sb(f"{tag}_b{j}", [128, D], BF16, is_transpose_dest=True) for j in range(2)]
        gt = [sb(f"{tag}_g{j}", [128, D], F32) for j in range(2)]
        sil2 = [sb(f"{tag}_sil{j}", [128, D], F32) for j in range(2)]
        st2 = [sb(f"{tag}_si{j}", [128, D], F32) for j in range(2)]
        of = [sb(f"{tag}_of{j}", [128, D], F32) for j in range(2)]
        ob = [sb(f"{tag}_ob{j}", [128, D], BF16) for j in range(2)]
        nst = (dst_f32 is not None) + (dst_pad is not None) + (dst_bf is not None)
        p = Prog(nc)
        for i in range(NT):
            b = i % 2
            lw = [(s_st[b], s_st[b].v)] if i >= 2 else []
            p.add("sync", (lambda e, b=b, i=i: e.dma_start(
                at[b][:, :], srcs["a"][ts(i, 128), :])), waits=lw, inc=s_ld[b], inc_n=16)
            if mode == "resid":
                vld = p.add("sync", (lambda e, b=b, i=i: e.dma_start(
                    bt[b][:, :], srcs["b"][0:D, i * 128:(i + 1) * 128],
                    transpose=True)), inc=s_ld[b], inc_n=16)
                vv = p.add("vector", (lambda e, b=b: e.tensor_add(
                    of[b][:, :], at[b][:, :], bt[b][:, :])),
                    waits=[(s_ld[b], vld)], inc=s_v)
            else:
                if "g_f" in srcs:
                    p.add("sync", (lambda e, b=b, i=i: e.dma_start(
                        bt[b][:, :], srcs["g_f"][0:D, i * 128:(i + 1) * 128],
                        transpose=True)), inc=s_ld[b], inc_n=16)
                else:
                    p.add("sync", (lambda e, b=b, i=i: e.dma_start(
                        gt[b][:, :], srcs["g"][ts(i, 128), :])), inc=s_ld[b], inc_n=16)
                vld = p.add("sync", (lambda e, b=b, i=i: e.dma_start(
                    st2[b][:, :], srcs["s"][ts(i, 128), :])), inc=s_ld[b], inc_n=16)
                va = p.add("scalar", (lambda e, b=b, gf=("g_f" in srcs): e.activation(
                    sil2[b][:, :], bt[b][:, :] if gf else gt[b][:, :], AF.Silu)),
                    waits=[(s_ld[b], vld), w(s_v)], inc=s_a)
                vv = p.add("vector", (lambda e, b=b: e.tensor_mul(
                    sil2[b][:, :], sil2[b][:, :], st2[b][:, :])), waits=[(s_a, va)], inc=s_v)
                vv = p.add("vector", (lambda e, b=b: e.tensor_add(
                    of[b][:, :], at[b][:, :], sil2[b][:, :])), waits=[(s_v, vv)], inc=s_v)
            if dst_pad is not None or dst_bf is not None:
                vv2 = p.add("vector", (lambda e, b=b: e.tensor_copy(
                    ob[b][:, :], of[b][:, :])), waits=[(s_v, vv)], inc=s_v)
            else:
                vv2 = vv
            if dst_f32 is not None:
                p.add("gpsimd", (lambda e, b=b, i=i: e.dma_start(
                    dst_f32[ts(i, 128), :], of[b][:, :])), waits=[(s_v, vv2)],
                    inc=s_st[b], inc_n=16)
            if dst_pad is not None:
                pr = pad_row(i * 128)
                p.add("gpsimd", (lambda e, b=b, pr=pr: e.dma_start(
                    dst_pad[pr:pr + 128, :], ob[b][:, :])),
                    waits=[(s_v, vv2)] if dst_f32 is None else (),
                    inc=s_st[b], inc_n=16)
            if dst_bf is not None:
                p.add("gpsimd", (lambda e, b=b, i=i: e.dma_start(
                    dst_bf[ts(i, 128), :], ob[b][:, :])),
                    waits=[(s_v, vv2)] if (dst_f32 is None and dst_pad is None) else (),
                    inc=s_st[b], inc_n=16)
        p.add("gpsimd", lambda e: e.nop(), waits=[w(s_st[0]), w(s_st[1])])
        p.emit()


NCH = S // 128


def emit_gla_prep(nc, ctx, G, q_f, k_f, z_f, qa_t, kd_t, kd2_t, ce63r, ce127r, tag):
    with ExitStack() as pctx:
        sb = lambda *a, **k: pctx.enter_context(nc.sbuf_tensor(*a, **k))
        psum = lambda *a, **k: pctx.enter_context(nc.psum_tensor(*a, **k))
        s_ld = [nc._sb.get(f"sb_l{j}") for j in range(2)]
        s_v = nc._sb.get("sb_v")
        s_a = nc._sb.get("sb_a")
        s_mm = nc._sb.get("sb_mm")
        s_st = [nc._sb.get(f"sb_s{j}") for j in range(2)]
        s_row = [nc._sb.get(f"sb_r{j}") for j in range(2)]
        qt = [sb(f"{tag}_q{j}", [128, D], BF16, is_transpose_dest=True) for j in range(2)]
        kt = [sb(f"{tag}_k{j}", [128, D], BF16, is_transpose_dest=True) for j in range(2)]
        zt = [sb(f"{tag}_z{j}", [128, D], BF16, is_transpose_dest=True) for j in range(2)]
        sq = sb(f"{tag}_sq", [128, D], F32)
        sig = sb(f"{tag}_sig", [128, D], F32)
        cum = sb(f"{tag}_cum", [128, D], F32)
        e63 = sb(f"{tag}_e63", [128, D], F32)
        e63i = sb(f"{tag}_e63i", [128, D], F32)
        e127 = sb(f"{tag}_e127", [128, D], F32)
        rowb = [sb(f"{tag}_rowb{j}", [1, D], BF16) for j in range(2)]
        nrm = sb(f"{tag}_n", [128, 64], F32)
        oqa = [sb(f"{tag}_oqa{j}", [128, D], BF16) for j in range(2)]
        okd = [sb(f"{tag}_okd{j}", [128, D], BF16) for j in range(2)]
        okd2 = [sb(f"{tag}_okd2{j}", [128, D], BF16) for j in range(2)]
        psC = psum(f"{tag}_pc", [128, 1024], F32)
        psB = psum(f"{tag}_pb", [128, 1024], F32)

        def v3(t):
            return bass.AP(t, 0, [[D, 128], [64, 16], [1, 64]])

        p = Prog(nc)
        for i in range(NT):
            b = i % 2
            bi, ch = i // NCH, i % NCH
            lw = [(s_st[b], s_st[b].v)] if i >= 2 else []
            p.add("sync", (lambda e, b=b, i=i: e.dma_start(
                qt[b][:, :], q_f[0:D, i * 128:(i + 1) * 128], transpose=True)),
                waits=lw, inc=s_ld[b], inc_n=16)
            p.add("sync", (lambda e, b=b, i=i: e.dma_start(
                kt[b][:, :], k_f[0:D, i * 128:(i + 1) * 128], transpose=True)),
                inc=s_ld[b], inc_n=16)
            vld = p.add("sync", (lambda e, b=b, i=i: e.dma_start(
                zt[b][:, :], z_f[0:D, i * 128:(i + 1) * 128], transpose=True)),
                inc=s_ld[b], inc_n=16)
            va = p.add("scalar", (lambda e, b=b: e.activation(
                sig[:, :], zt[b][:, :], AF.Sigmoid)),
                waits=[(s_ld[b], vld), (s_v, s_v.v)], inc=s_a)
            va = p.add("scalar", (lambda e: e.activation(
                sig[:, :], sig[:, :], AF.Ln)), waits=[(s_a, va)], inc=s_a)
            vmm = p.add("tensor", (lambda e: e.matmul(
                psC[:, 0:512], G["ucum"][:, :], sig[:, 0:512], start=True, stop=True)),
                waits=[(s_a, va)], inc=s_mm)
            vmm = p.add("tensor", (lambda e: e.matmul(
                psC[:, 512:1024], G["ucum"][:, :], sig[:, 512:1024], start=True,
                stop=True)), inc=s_mm)
            vv = p.add("vector", lambda e: e.tensor_copy(cum[:, :], psC[:, :]),
                       waits=[(s_mm, vmm)], inc=s_v)
            vmm = p.add("tensor", (lambda e: e.matmul(
                psB[:, 0:512], G["sel63"][:, :], cum[:, 0:512], start=True, stop=True)),
                waits=[(s_v, vv)], inc=s_mm)
            vmm = p.add("tensor", (lambda e: e.matmul(
                psB[:, 512:1024], G["sel63"][:, :], cum[:, 512:1024], start=True,
                stop=True)), inc=s_mm)
            vv = p.add("vector", (lambda e: e.tensor_sub(e63[:, :], psB[:, :], cum[:, :])),
                       waits=[(s_mm, vmm)], inc=s_v)
            va = p.add("scalar", (lambda e: e.activation(e63i[:, :], e63[:, :],
                                                         AF.Exp, scale=-1.0)),
                       waits=[(s_v, vv)], inc=s_a)
            va = p.add("scalar", (lambda e: e.activation(e63[:, :], e63[:, :], AF.Exp)),
                       waits=[(s_a, va)], inc=s_a)
            va2 = p.add("scalar", (lambda e: e.activation(sq[:, :], psB[:, :],
                                                          AF.Exp)),
                        waits=[(s_a, va)], inc=s_a)
            vv = p.add("vector", (lambda e, b=b: e.tensor_copy(rowb[b][0:1, :], sq[0:1, :])),
                       waits=[(s_a, va2), (s_row[b], s_row[b].v)], inc=s_v)
            vrow = p.add("gpsimd", (lambda e, b=b, bi=bi, ch=ch: e.dma_start(
                ce63r[bi * NCH + ch: bi * NCH + ch + 1, :], rowb[b][0:1, :])),
                waits=[(s_v, vv)], inc=s_row[b], inc_n=16)
            vmm = p.add("tensor", (lambda e: e.matmul(
                psB[:, 0:512], G["sel127"][:, :], cum[:, 0:512], start=True,
                stop=True)), waits=[(s_v, vv), (s_a, va2)], inc=s_mm)
            vmm = p.add("tensor", (lambda e: e.matmul(
                psB[:, 512:1024], G["sel127"][:, :], cum[:, 512:1024], start=True,
                stop=True)), inc=s_mm)
            vv = p.add("vector", (lambda e: e.tensor_sub(e127[:, :], psB[:, :], cum[:, :])),
                       waits=[(s_mm, vmm)], inc=s_v)
            va3 = p.add("scalar", (lambda e: e.activation(e127[:, :], e127[:, :], AF.Exp)),
                        waits=[(s_v, vv)], inc=s_a)
            va3b = p.add("scalar", (lambda e: e.activation(sq[:, :], psB[:, :],
                                                           AF.Exp)),
                         waits=[(s_a, va3)], inc=s_a)
            vv = p.add("vector", (lambda e, b=b: e.tensor_copy(rowb[b][0:1, :], sq[0:1, :])),
                       waits=[(s_a, va3b), (s_row[b], s_row[b].v)], inc=s_v)
            p.add("gpsimd", (lambda e, b=b, bi=bi, ch=ch: e.dma_start(
                ce127r[bi * NCH + ch: bi * NCH + ch + 1, :], rowb[b][0:1, :])),
                waits=[(s_v, vv)], inc=s_row[b], inc_n=16)
            for src_t, dst_list in [(qt, [("qa", oqa, e63i)]),
                                    (kt, [("kd", okd, e63), ("kd2", okd2, e127)])]:
                st = src_t[b]
                va4 = p.add("scalar", (lambda e, st=st: e.activation(
                    sq[:, :], st[:, :], AF.Square)), waits=[(s_v, s_v.v)], inc=s_a)
                vv = p.add("vector", (lambda e: e.tensor_reduce(
                    nrm[:, 0:16], bass.AP(sq, 0, [[D, 128], [64, 16], [1, 64]]),
                    AX.X, ALU.add)), waits=[(s_a, va4)], inc=s_v)
                va4 = p.add("scalar", (lambda e: e.activation(
                    nrm[:, 16:32], nrm[:, 0:16], AF.Sqrt)), waits=[(s_v, vv)], inc=s_a)
                vv = p.add("vector", (lambda e: e.tensor_scalar(
                    nrm[:, 32:48], nrm[:, 16:32], 1e-12, None, ALU.max)),
                    waits=[(s_a, va4)], inc=s_v)
                vv = p.add("vector", (lambda e: e.reciprocal(
                    nrm[:, 48:64], nrm[:, 32:48])), waits=[(s_v, vv)], inc=s_v)
                vv = p.add("vector", (lambda e, st=st: e.tensor_tensor(
                    v3(sq), v3(st), bass.AP(nrm, 48, [[64, 128], [1, 16], [0, 64]]),
                    ALU.mult)), waits=[(s_v, vv)], inc=s_v)
                for _, dst_t, fac in dst_list:
                    vv = p.add("vector", (lambda e, dst_t=dst_t, fac=fac, b=b:
                                          e.tensor_mul(dst_t[b][:, :], sq[:, :],
                                                       fac[:, :])),
                               waits=[(s_v, vv)], inc=s_v)
            vdone = s_v.v
            for dst_hbm, src_list in [(qa_t, oqa), (kd_t, okd), (kd2_t, okd2)]:
                p.add("gpsimd", (lambda e, dst_hbm=dst_hbm, src_list=src_list,
                                 b=b, i=i: e.dma_start(
                    dst_hbm[ts(i, 128), :], src_list[b][:, :])),
                    waits=[(s_v, vdone)], inc=s_st[b], inc_n=16)
        p.add("gpsimd", lambda e: e.nop(),
              waits=[w(s_st[0]), w(s_st[1]), w(s_row[0]), w(s_row[1])])
        p.emit()


def emit_gla_scan(nc, ctx, G, qa_t, kd_t, kd2_t, v_f, ce63r, ce127r, og_f, tag):
    import os as _os
    CUT = set((_os.environ.get("GS_CUT") or "").split(","))
    SPACE = "space" in CUT
    MAXC = int(_os.environ.get("GS_MAXC") or NCH)
    MAXHP = int(_os.environ.get("GS_MAXHP") or 8)
    MAXB = int(_os.environ.get("GS_MAXB") or B_CORE)
    with ExitStack() as pctx:
        sb = lambda *a, **k: pctx.enter_context(nc.sbuf_tensor(*a, **k))
        psum = lambda *a, **k: pctx.enter_context(nc.psum_tensor(*a, **k))
        s_ld = nc._sb.get("sb_ld")
        s_ld2 = [nc._sb.get(f"sb_l{j}") for j in range(2)]
        s_v = nc._sb.get("sb_v")
        s_mm = nc._sb.get("sb_mm")
        s_st = [nc._sb.get(f"sb_y{j}") for j in range(2)]
        c63 = [sb(f"{tag}_c63_{hp}", [128, NCH], F32) for hp in range(8)]
        c127 = [sb(f"{tag}_c127_{hp}", [128, NCH], F32) for hp in range(8)]
        c63b = [sb(f"{tag}_c63b_{hp}", [128, NCH], BF16, is_transpose_dest=True)
                for hp in range(8)]
        c127b = [sb(f"{tag}_c127b_{hp}", [128, NCH], BF16, is_transpose_dest=True)
                 for hp in range(8)]
        qa = [sb(f"{tag}_qa{j}", [128, 512], BF16, is_transpose_dest=True)
              for j in range(16)]
        kd = [sb(f"{tag}_kd{j}", [128, 512], BF16, is_transpose_dest=True)
              for j in range(16)]
        kd2 = [sb(f"{tag}_k2{j}", [128, D], BF16) for j in range(2)]
        vt = [sb(f"{tag}_vt{j}", [128, D], BF16, is_transpose_dest=True) for j in range(2)]
        Sf = [sb(f"{tag}_Sf{hp}", [128, 64], F32) for hp in range(8)]
        Sb = [sb(f"{tag}_Sb{hp}", [128, 64], BF16) for hp in range(8)]
        Spp = sb(f"{tag}_Spp", [128, 64], F32)
        Sppb = sb(f"{tag}_Sppb", [128, 64], BF16)
        smx = [sb(f"{tag}_sm{j}", [128, 128], BF16) for j in range(2)]
        oev = [sb(f"{tag}_oev{j}", [128, 128], BF16) for j in range(2)]
        psS = [psum(f"{tag}_pS{j}", [128, 512], F32) for j in range(2)]
        psO = [[psum(f"{tag}_pO{j}{h}", [128, 512], F32) for h in range(2)]
               for j in range(2)]
        psA1 = [psum(f"{tag}_pA{h}", [128, 512], F32) for h in range(2)]
        psA = [psA1, psA1]

        p = Prog(nc, space=SPACE)
        for b in range(MAXB):
            tb0 = b * S
            ldw = [w(s_mm), w(s_v)] if b > 0 else []
            if "noce" not in CUT:
                for hp in range(8):
                    v1 = p.add("sync", (lambda e, hp=hp, b=b: e.dma_start(
                        c63b[hp][:, :], ce63r[b * NCH:(b + 1) * NCH,
                                              hp * 128:(hp + 1) * 128], transpose=True)),
                        waits=ldw if hp == 0 else (), inc=s_ld2[0], inc_n=16)
                    v1 = p.add("sync", (lambda e, hp=hp, b=b: e.dma_start(
                        c127b[hp][:, :], ce127r[b * NCH:(b + 1) * NCH,
                                                hp * 128:(hp + 1) * 128], transpose=True)),
                        inc=s_ld2[0], inc_n=16)
                vce = s_ld2[0].v
                for hp in range(8):
                    p.add("vector", (lambda e, hp=hp: e.tensor_copy(c63[hp][:, :],
                                                                    c63b[hp][:, :])),
                          waits=[(s_ld2[0], vce)] if hp == 0 else (), inc=s_v)
                    p.add("vector", (lambda e, hp=hp: e.tensor_copy(c127[hp][:, :],
                                                                    c127b[hp][:, :])),
                          inc=s_v)
            else:
                for hp in range(8):
                    p.add("vector", (lambda e, hp=hp: e.memset(c63[hp][:, :], 0.5)),
                          waits=ldw if hp == 0 else (), inc=s_v)
                    p.add("vector", (lambda e, hp=hp: e.memset(c127[hp][:, :], 0.5)),
                          inc=s_v)
            vcec = s_v.v
            for hp in range(8):
                p.add("vector", (lambda e, hp=hp: e.memset(Sf[hp][:, :], 0.0)), inc=s_v)
                p.add("vector", (lambda e, hp=hp: e.memset(Sb[hp][:, :], 0.0)), inc=s_v)
            for c in range(MAXC):
                cb2 = c % 2
                g4 = c // 4
                if c % 4 == 0 and "noqa" not in CUT:
                    gw = [w(s_mm)] if (c > 0 or b > 0) else []
                    for hp in range(8):
                        p.add("sync", (lambda e, hp=hp, g4=g4, tb0=tb0: e.dma_start(
                            qa[hp * 2 + g4 % 2][:, :],
                            qa_t[tb0 + g4 * 512: tb0 + (g4 + 1) * 512,
                                 hp * 128:(hp + 1) * 128], transpose=True)),
                            waits=gw if hp == 0 else (), inc=s_ld2[1], inc_n=16)
                        p.add("sync", (lambda e, hp=hp, g4=g4, tb0=tb0: e.dma_start(
                            kd[hp * 2 + g4 % 2][:, :],
                            kd_t[tb0 + g4 * 512: tb0 + (g4 + 1) * 512,
                                 hp * 128:(hp + 1) * 128], transpose=True)),
                            inc=s_ld2[1], inc_n=16)
                vqa = s_ld2[1].v
                if "nokv" not in CUT:
                    lw2 = [(s_mm, s_mm.v)] if (c >= 2 or b > 0) else []
                    p.add("sync", (lambda e, c=c, cb2=cb2, tb0=tb0: e.dma_start(
                        kd2[cb2][:, :], kd2_t[tb0 + c * 128: tb0 + (c + 1) * 128, :])),
                        waits=lw2, inc=s_ld2[1], inc_n=16)
                    vk2 = p.add("sync", (lambda e, c=c, cb2=cb2, tb0=tb0: e.dma_start(
                        vt[cb2][:, :], v_f[0:D, tb0 + c * 128: tb0 + (c + 1) * 128],
                        transpose=True)), inc=s_ld2[1], inc_n=16)
                else:
                    vk2 = s_ld2[1].v
                for hp in range(0 if "nobody" in CUT else MAXHP):
                    gi = hp * 2 + g4 % 2
                    coff = (c % 4) * 128
                    mw = [(s_ld2[1], vk2), (s_v, s_v.v)]
                    p.add("tensor", (lambda e, gi=gi, coff=coff:
                                     e.matmul(psS[0][:, 0:128],
                                              kd[gi][0:64, coff:coff + 128],
                                              qa[gi][0:64, coff:coff + 128],
                                              start=True, stop=True,
                                              tile_position=(0, 0))),
                          waits=mw, inc=s_mm)
                    vmm = p.add("tensor", (lambda e, gi=gi, coff=coff:
                                           e.matmul(psS[1][:, 0:128],
                                                    kd[gi][64:128, coff:coff + 128],
                                                    qa[gi][64:128, coff:coff + 128],
                                                    start=True, stop=True,
                                                    tile_position=(64, 0))),
                                inc=s_mm)
                    if "nospp" not in CUT:
                        vv = p.add("vector", (lambda e, hp=hp, c=c: e.tensor_scalar(
                            Spp[:, :], Sf[hp][:, :], c63[hp][:, c:c + 1], None, ALU.mult)),
                            waits=[(s_v, s_v.v), (s_mm, s_mm.v)], inc=s_v)
                        vsp = p.add("vector", (lambda e: e.tensor_copy(Sppb[:, :], Spp[:, :])),
                                   waits=[(s_v, vv)], inc=s_v)
                    else:
                        vsp = s_v.v
                    vs0 = p.add("vector", (lambda e: e.tensor_tensor(
                        smx[0][:, :], psS[0][:, 0:128], G["tril_inc"][:, :],
                        ALU.mult)), waits=[(s_mm, vmm)], inc=s_v)
                    vs1 = p.add("vector", (lambda e: e.tensor_tensor(
                        smx[1][:, :], psS[1][:, 0:128], G["tril_inc"][:, :],
                        ALU.mult)), waits=[(s_v, vs0)], inc=s_v)
                    nointer = "nointer" in CUT
                    nopv = "nopv" in CUT
                    if not nopv:
                        p.add("tensor", (lambda e, cb2=cb2, hp=hp, st=nointer: e.matmul(
                            psO[cb2][0][0:64, 0:128], vt[cb2][:, hp * 128: hp * 128 + 64],
                            smx[0][:, :], start=True, stop=st, tile_position=(0, 0))),
                            waits=[(s_v, vs1)], inc=s_mm)
                        vo = p.add("tensor", (lambda e, cb2=cb2, hp=hp, st=nointer: e.matmul(
                            psO[cb2][1][64:128, 0:128], vt[cb2][:, hp * 128 + 64: hp * 128 + 128],
                            smx[1][:, :], start=True, stop=st, tile_position=(0, 64))),
                            inc=s_mm)
                    else:
                        vo = s_mm.v
                    if not nointer and not nopv:
                        p.add("tensor", (lambda e, cb2=cb2, gi=gi, coff=coff: e.matmul(
                            psO[cb2][0][0:64, 0:128], Sppb[0:64, :],
                            qa[gi][0:64, coff:coff + 128], start=False, stop=True,
                            tile_position=(0, 0))),
                            waits=[(s_v, vsp)], inc=s_mm)
                        vo = p.add("tensor", (lambda e, cb2=cb2, gi=gi, coff=coff: e.matmul(
                            psO[cb2][1][64:128, 0:128], Sppb[64:128, :],
                            qa[gi][64:128, coff:coff + 128], start=False, stop=True,
                            tile_position=(64, 64))),
                            inc=s_mm)
                    if "nosadd" in CUT:
                        vsa = s_mm.v
                    vsa = s_mm.v if "nosadd" in CUT else p.add("tensor", (lambda e, cb2=cb2, hp=hp: e.matmul(
                        psA[cb2][0][0:64, 0:64], kd2[cb2][:, hp * 128: hp * 128 + 64],
                        vt[cb2][:, hp * 128: hp * 128 + 64], start=True, stop=True,
                        tile_position=(0, 0))), inc=s_mm)
                    if "nosadd" not in CUT:
                        vsa = p.add("tensor", (lambda e, cb2=cb2, hp=hp: e.matmul(
                            psA[cb2][1][64:128, 0:64], kd2[cb2][:, hp * 128 + 64: hp * 128 + 128],
                            vt[cb2][:, hp * 128 + 64: hp * 128 + 128], start=True, stop=True,
                            tile_position=(0, 64))), inc=s_mm)
                        vv = p.add("vector", (lambda e, hp=hp, c=c, cb2=cb2:
                                              e.scalar_tensor_tensor(
                            Sf[hp][0:64, :], Sf[hp][0:64, :], c127[hp][0:64, c:c + 1],
                            psA[cb2][0][0:64, 0:64], ALU.mult, ALU.add)),
                            waits=[(s_mm, vsa), (s_v, vcec)], inc=s_v)
                        vv = p.add("vector", (lambda e, hp=hp, c=c, cb2=cb2:
                                              e.scalar_tensor_tensor(
                            Sf[hp][64:128, :], Sf[hp][64:128, :], c127[hp][64:128, c:c + 1],
                            psA[cb2][1][64:128, 0:64], ALU.mult, ALU.add)),
                            waits=[(s_v, vv)], inc=s_v)
                        vv = p.add("vector", (lambda e, hp=hp: e.tensor_copy(
                            Sb[hp][:, :], Sf[hp][:, :])), waits=[(s_v, vv)], inc=s_v)
                    e2 = hp % 2
                    if nopv:
                        continue
                    vo2 = p.add("vector", (lambda e, cb2=cb2, e2=e2: e.tensor_copy(
                        oev[e2][0:64, :], psO[cb2][0][0:64, 0:128])),
                        waits=[(s_mm, vo), (s_st[e2], s_st[e2].v)], inc=s_v)
                    vo2 = p.add("vector", (lambda e, cb2=cb2, e2=e2: e.tensor_copy(
                        oev[e2][64:128, :], psO[cb2][1][64:128, 0:128])),
                        waits=[(s_v, vo2)], inc=s_v)
                    if "nost" not in CUT:
                        p.add("sync", (lambda e, hp=hp, c=c, tb0=tb0, e2=e2: e.dma_start(
                            og_f[hp * 128:(hp + 1) * 128, tb0 + c * 128: tb0 + (c + 1) * 128],
                            oev[e2][:, :])), waits=[(s_v, vo2)], inc=s_st[e2], inc_n=16)
        p.add("sync", lambda e: e.nop(), waits=[w(s_st[0]), w(s_st[1])])
        p.emit()


def emit_moe_cast(nc, ctx, G, w1_in, w2_in, w1bf, w2bf, tag):
    with ExitStack() as pctx:
        sb = lambda *a, **k: pctx.enter_context(nc.sbuf_tensor(*a, **k))
        s_ld = [nc._sb.get(f"sb_w{j}") for j in range(2)]
        s_st = [nc._sb.get(f"sb_y{j}") for j in range(2)]
        t = [sb(f"{tag}_t{j}", [128, D], BF16) for j in range(2)]
        p = Prog(nc)
        jobs = []
        for e in range(8):
            for r in range(16):
                jobs.append((w1_in, w1bf, e, r, 2048))
            for r in range(8):
                jobs.append((w2_in, w2bf, e, r, 1024))
        for i, (src, dst, ee, r, rows) in enumerate(jobs):
            b = i % 2
            lw = [(s_st[b], s_st[b].v)] if i >= 2 else []
            vld = p.add("gpsimd", (lambda e_, src=src, ee=ee, r=r, b=b: e_.dma_start(
                t[b][:, :], src[ee, r * 128:(r + 1) * 128, :])),
                waits=lw, inc=s_ld[b], inc_n=16)
            p.add("sync", (lambda e_, dst=dst, ee=ee, r=r, b=b: e_.dma_start(
                dst[ee, r * 128:(r + 1) * 128, :], t[b][:, :])),
                waits=[(s_ld[b], vld)], inc=s_st[b], inc_n=16)
        p.add("sync", lambda e_: e_.nop(), waits=[w(s_st[0]), w(s_st[1])])
        p.emit()


def wget_xbar3(wbf, e, Krows):
    def f(eng, dst, k0, ks, m0, ms):
        return eng.dma_start(dst[0:ks, 0:ms],
                             wbf[e, m0:m0 + ms, k0:k0 + ks], transpose=True)
    return f


def emit_moe_gate(nc, ctx, G, IN, xn1_t, ce_f, tag):
    with ExitStack() as pctx:
        sb = lambda *a, **k: pctx.enter_context(nc.sbuf_tensor(*a, **k))
        psum = lambda *a, **k: pctx.enter_context(nc.psum_tensor(*a, **k))
        s_ld = [nc._sb.get(f"sb_l{j}") for j in range(2)]
        s_w = nc._sb.get("sb_qs")
        s_v = nc._sb.get("sb_v")
        s_a = nc._sb.get("sb_a")
        s_mm = nc._sb.get("sb_mm")
        s_st = [nc._sb.get(f"sb_y{j}") for j in range(2)]
        wgt = sb(f"{tag}_wg", [8, D], BF16)
        wgT = sb(f"{tag}_wgT", [128, 64], BF16)
        bgr = sb(f"{tag}_bg", [1, 8], BF16)
        bgb = sb(f"{tag}_bgb", [128, 8], F32)
        xnf = [sb(f"{tag}_x{j}", [128, 128], BF16, is_transpose_dest=True)
               for j in range(8)]
        lg = sb(f"{tag}_lg", [128, 8], F32)
        pr = sb(f"{tag}_pr", [128, 8], F32)
        t8 = sb(f"{tag}_t8", [128, 8], F32)
        smm = sb(f"{tag}_smm", [128, 16], F32)
        ceb = [sb(f"{tag}_ce{j}", [128, 8], BF16) for j in range(2)]
        cet = [sb(f"{tag}_cet{j}", [8, 128], BF16) for j in range(2)]
        psL = psum(f"{tag}_pL", [128, 512], F32)
        psT = psum(f"{tag}_pT", [128, 512], F32)
        psTb = psum(f"{tag}_pTb", [128, 512], BF16)

        p = Prog(nc)
        vw = p.add("gpsimd", (lambda e: e.dma_start(wgt[0:8, :], IN["L1_ffn_Wg"][:, :])),
                   inc=s_w, inc_n=16)
        vw2 = p.add("gpsimd", (lambda e: e.dma_start(bgr[0:1, :], IN["L1_ffn_bg"][:, :])),
                    inc=s_w, inc_n=16)
        vt_ = None
        for j in range(8):
            vt_ = p.add("tensor", (lambda e, j=j: e.transpose(
                psTb[0:128, j * 8:(j + 1) * 8], wgt[0:8, j * 128:(j + 1) * 128],
                G["ident_bf16"][0:8, 0:8])),
                waits=[(s_w, vw2)] if j == 0 else (), inc=s_mm)
        vv = p.add("vector", (lambda e: e.tensor_copy(wgT[:, :], psTb[:, 0:64])),
                   waits=[(s_mm, vt_)], inc=s_v)
        vm = p.add("tensor", (lambda e: e.matmul(
            psT[:, 64:72], G["bc_all"][0:1, :], bgr[0:1, :], start=True, stop=True)),
            waits=[(s_v, vv)], inc=s_mm)
        vv = p.add("vector", (lambda e: e.tensor_copy(bgb[:, :], psT[:, 64:72])),
                   waits=[(s_mm, vm)], inc=s_v)
        vpre = vv
        for i in range(NT):
            b = i % 2
            lw = ([(s_st[b], s_st[b].v)] if i >= 2 else [(s_v, vpre)]) + \
                 ([(s_mm, s_mm.v)] if i >= 1 else [])
            vld = None
            for j in range(8):
                vld = p.add("sync", (lambda e, j=j, i=i: e.dma_start(
                    xnf[j][:, :], xn1_t[i * 128:(i + 1) * 128,
                                        j * 128:(j + 1) * 128], transpose=True)),
                    waits=lw if j == 0 else (), inc=s_ld[b], inc_n=16)
            vm = None
            for j in range(8):
                vm = p.add("tensor", (lambda e, j=j: e.matmul(
                    psL[:, 0:8], xnf[j][:, :], wgT[:, j * 8:(j + 1) * 8],
                    start=(j == 0), stop=(j == 7))),
                    waits=[(s_ld[b], vld), (s_v, s_v.v)] if j == 0 else (),
                    inc=s_mm)
            vv = p.add("vector", (lambda e: e.tensor_add(lg[:, :], psL[:, 0:8],
                                                         bgb[:, :])),
                       waits=[(s_mm, vm)], inc=s_v)
            vv = p.add("vector", (lambda e: e.reduce_max(smm[:, 0:1], lg[:, :], AX.X,
                                                         negate=True)),
                       waits=[(s_v, vv)], inc=s_v)
            va = p.add("scalar", (lambda e: e.activation(pr[:, :], lg[:, :], AF.Exp,
                                                         bias=smm[:, 0:1])),
                       waits=[(s_v, vv)], inc=s_a)
            vv = p.add("vector", (lambda e: e.reduce_sum(smm[:, 1:2], pr[:, :], AX.X)),
                       waits=[(s_a, va)], inc=s_v)
            vv = p.add("vector", (lambda e: e.reciprocal(smm[:, 2:3], smm[:, 1:2])),
                       waits=[(s_v, vv)], inc=s_v)
            vv = p.add("vector", (lambda e: e.tensor_scalar(pr[:, :], pr[:, :],
                                                            smm[:, 2:3], None,
                                                            ALU.mult)),
                       waits=[(s_v, vv)], inc=s_v)
            vv = p.add("vector", (lambda e: e.reduce_max(smm[:, 3:4], pr[:, :], AX.X)),
                       waits=[(s_v, vv)], inc=s_v)
            vv = p.add("vector", (lambda e: e.tensor_scalar(t8[:, :], pr[:, :],
                                                            smm[:, 3:4], None,
                                                            ALU.is_ge)),
                       waits=[(s_v, vv)], inc=s_v)
            vv = p.add("vector", (lambda e: e.reduce_sum(smm[:, 4:5], t8[:, :], AX.X)),
                       waits=[(s_v, vv)], inc=s_v)
            vv = p.add("vector", (lambda e: e.scalar_tensor_tensor(
                t8[:, :], t8[:, :], -2.0, pr[:, :], ALU.mult, ALU.add)),
                waits=[(s_v, vv)], inc=s_v)
            vv = p.add("vector", (lambda e: e.reduce_max(smm[:, 5:6], t8[:, :], AX.X)),
                       waits=[(s_v, vv)], inc=s_v)
            vv = p.add("vector", (lambda e: e.tensor_scalar(smm[:, 4:5], smm[:, 4:5],
                                                            2.0, None, ALU.is_ge)),
                       waits=[(s_v, vv)], inc=s_v)
            vv = p.add("vector", (lambda e: e.tensor_sub(smm[:, 6:7], smm[:, 3:4],
                                                         smm[:, 5:6])),
                       waits=[(s_v, vv)], inc=s_v)
            vv = p.add("vector", (lambda e: e.scalar_tensor_tensor(
                smm[:, 7:8], smm[:, 4:5], smm[:, 6:7], smm[:, 5:6], ALU.mult,
                ALU.add)), waits=[(s_v, vv)], inc=s_v)
            vv = p.add("vector", (lambda e: e.tensor_scalar(t8[:, :], pr[:, :],
                                                            smm[:, 7:8], None,
                                                            ALU.is_ge)),
                       waits=[(s_v, vv)], inc=s_v)
            vv = p.add("vector", (lambda e: e.tensor_mul(t8[:, :], t8[:, :], pr[:, :])),
                       waits=[(s_v, vv)], inc=s_v)
            vv = p.add("vector", (lambda e: e.reduce_sum(smm[:, 8:9], t8[:, :], AX.X)),
                       waits=[(s_v, vv)], inc=s_v)
            vv = p.add("vector", (lambda e: e.reciprocal(smm[:, 9:10], smm[:, 8:9])),
                       waits=[(s_v, vv)], inc=s_v)
            vv = p.add("vector", (lambda e, b=b: e.tensor_scalar(
                ceb[b][:, :], t8[:, :], smm[:, 9:10], None, ALU.mult)),
                waits=[(s_v, vv)], inc=s_v)
            vm = p.add("tensor", (lambda e, b=b: e.transpose(
                psTb[0:8, 128:256], ceb[b][:, :], G["ident_bf16"][:, :])),
                waits=[(s_v, vv)], inc=s_mm)
            vv = p.add("vector", (lambda e, b=b: e.tensor_copy(cet[b][0:8, :],
                                                               psTb[0:8, 128:256])),
                       waits=[(s_mm, vm)], inc=s_v)
            p.add("sync", (lambda e, b=b, i=i: e.dma_start(
                ce_f[0:8, i * 128:(i + 1) * 128], cet[b][0:8, :])),
                waits=[(s_v, vv)], inc=s_st[b], inc_n=16)
        p.add("sync", lambda e: e.nop(), waits=[w(s_st[0]), w(s_st[1])])
        p.emit()


def make_in_maps(inputs):
    tabs = host_tables()
    x = np.ascontiguousarray(np.asarray(inputs["L0_attn_x"], dtype=np.float32))
    in_maps = []
    for c in range(8):
        m = {"x_in": x[2 * c:2 * c + 2].reshape(N, D)}
        for k, v in inputs.items():
            if k == "L0_attn_x":
                continue
            a = np.asarray(v, dtype=np.float32)
            if k in ("L1_attn_g_attn", "L1_attn_g_ffn", "L1_ffn_bg"):
                a = a.reshape(1, -1)
            m[k] = np.ascontiguousarray(a)
        for k, v in tabs.items():
            m["tab_" + k] = v
        in_maps.append(m)
    return in_maps


def kernel(**inputs):
    nc = build_nc()
    in_maps = make_in_maps(inputs)
    res = run_bass_kernel_spmd(nc, in_maps, core_ids=list(range(8)))
    return np.concatenate(
        [np.asarray(r["out"]).reshape(2, S, D) for r in res.results], axis=0)


# version 02
# speedup vs baseline: 1.0647x; 1.0647x over previous
import math
from contextlib import ExitStack

import numpy as np
import ml_dtypes
import concourse.bass as bass
import concourse.mybir as mybir
from concourse.bass_utils import run_bass_kernel_spmd

F32 = mybir.dt.float32
BF16 = mybir.dt.bfloat16
AF = mybir.ActivationFunctionType
ALU = mybir.AluOpType
AX = mybir.AxisListType
bf = ml_dtypes.bfloat16
ts = bass.ts

B_CORE = 2
S = 2048
N = B_CORE * S
D = 1024
NT = N // 128
H0 = 16
DH0 = 64

TAPS: list = []
STAGES = {"ln0", "rproj", "rrot", "rattn", "rfin", "ffn0", "comb0",
          "ln1", "gla", "moe", "comb1"}


class Sem:
    def __init__(self, handle):
        self.h = handle
        self.v = 0


class Prog:
    ENGINES = ("sync", "gpsimd", "tensor", "vector", "scalar")

    def __init__(self, nc, space=False):
        self.nc = nc
        self.space = space
        self.items = {e: [] for e in self.ENGINES}

    def add(self, eng, fn, waits=(), inc=None, inc_n=1):
        w = [(s.h, v) for (s, v) in waits]
        out = None
        if inc is not None:
            inc.v += inc_n
            out = inc.v
        self.items[eng].append((fn, w, inc.h if inc else None, inc_n))
        if self.space and eng in ("tensor", "vector") and inc is not None:
            self.items[eng].append((lambda e: e.nop(), [], None, 1))
        return out

    def emit(self):
        with self.nc.Block() as blk:
            for ename in self.ENGINES:
                items = self.items[ename]
                if not items:
                    continue
                def body(eng, items=items):
                    for fn, waits, inch, incn in items:
                        for h, v in waits:
                            eng.wait_ge(h, v)
                        inst = fn(eng)
                        if inch is not None:
                            inst.then_inc(inch, incn)
                getattr(blk, ename)(body)
        self.items = {e: [] for e in self.ENGINES}


def w(sem):
    return (sem, sem.v)


class SemBank:
    def __init__(self, nc, ctx):
        self.nc = nc
        self.ctx = ctx
        self.sems = {}

    def get(self, name):
        if name not in self.sems:
            self.sems[name] = Sem(self.ctx.enter_context(self.nc.semaphore(name)))
        return self.sems[name]


def _geo_rowsum(decay, q):
    return (1.0 - np.exp(decay * (q + 1))) / (1.0 - np.exp(decay))


def host_tables():
    t = {}
    kk = np.arange(128)[:, None]
    qq = np.arange(512)[None, :]
    t["ident_f32"] = np.eye(128, dtype=np.float32)
    t["ident_bf16"] = np.eye(128, dtype=np.float32).astype(bf)
    a = 1.0 / 10000.0 ** np.linspace(0.0, 1.0, DH0 // 2)
    idx = np.arange(S, dtype=np.float64)
    cos2 = np.cos(idx[:, None] * a[None, :])
    sin2 = np.sin(idx[:, None] * a[None, :])
    decay = np.log(1.0 - 2.0 ** (-5.0 - np.arange(H0, dtype=np.float64)))
    rowsum = _geo_rowsum(decay[None, :], idx[:, None])
    dq = np.exp(decay[None, :] * idx[:, None]) / np.sqrt(rowsum)
    dk = np.exp(-decay[None, :] * idx[:, None])
    t["rcq"] = (cos2[:, None, :] * dq[:, :, None]).reshape(S, 512).astype(np.float32)
    t["rsq"] = (sin2[:, None, :] * dq[:, :, None]).reshape(S, 512).astype(np.float32)
    t["rck"] = (cos2[:, None, :] * dk[:, :, None] / 32.0).reshape(S, 512).astype(np.float32)
    t["rsk"] = (sin2[:, None, :] * dk[:, :, None] / 32.0).reshape(S, 512).astype(np.float32)
    for j in range(4):
        t[f"trilq_{j}"] = (qq >= kk + 128 * j).astype(np.float32).astype(bf)
    for j in range(4):
        t[f"lamu_{j}"] = (qq <= kk + 128 * j).astype(np.float32).astype(bf)
    t["e0col"] = np.ones((128, 1), dtype=np.float32).astype(bf)
    r0 = np.zeros((1, 128), dtype=np.float32); r0[0, :64] = 1
    r1 = np.zeros((1, 128), dtype=np.float32); r1[0, 64:] = 1
    t["bc_lo"] = r0.astype(bf)
    t["bc_hi"] = r1.astype(bf)
    t["bc_all"] = np.ones((1, 128), dtype=np.float32).astype(bf)
    DHF, WINDOW = 64, 512
    half = np.arange(0, DHF, 2, dtype=np.float64)
    inv_freq = 1.0 / 10000.0 ** (half / DHF)
    tt = np.arange(S, dtype=np.float64)
    freqs = tt[:, None] * inv_freq[None, :]
    base = (half + 0.4 * DHF) / (1.4 * DHF)
    power = (tt - S // 2) / float(WINDOW // 2)
    scale = base[None, :] ** power[:, None]
    t["xc_q"] = (np.cos(freqs) * scale).astype(np.float32)
    t["xs_q"] = (np.sin(freqs) * scale).astype(np.float32)
    t["xc_k"] = (np.cos(freqs) / scale).astype(np.float32)
    t["xs_k"] = (np.sin(freqs) / scale).astype(np.float32)
    ss = np.arange(128)[:, None]
    t2 = np.arange(128)[None, :]
    t["tril_inc"] = (t2 >= ss).astype(np.float32).astype(bf)
    t["ucum"] = (t2 >= ss).astype(np.float32)
    t["sel63"] = (ss == 63).astype(np.float32) @ np.ones((1, 128), np.float32)
    t["sel127"] = (ss == 127).astype(np.float32) @ np.ones((1, 128), np.float32)
    t["ones_row"] = np.ones((1, 512), dtype=np.float32).astype(bf)
    t["zrow"] = np.zeros((1, 1024), dtype=np.float32).astype(bf)
    sel = np.zeros((8, 8, 128), dtype=np.float32)
    for e in range(8):
        sel[e, e, :] = 1.0
    t["sel_moe"] = np.concatenate(list(sel), axis=1).astype(bf)
    return t


def build_nc():
    nc = bass.Bass("TRN2", target_bir_lowering=False)
    tabs = host_tables()

    IN = {}
    def extin(name, shape, dt=F32):
        IN[name] = nc.dram_tensor(name, list(shape), dt, kind="ExternalInput")
        return IN[name]

    x_in = extin("x_in", (N, D))
    for nm, shp in [
        ("L0_attn_Wq", (D, D)), ("L0_attn_Wk", (D, D)), ("L0_attn_Wv", (D, D)),
        ("L0_attn_Wg", (D, D)),
        ("L0_ffn_wqkv", (2, D, 1536)), ("L0_ffn_q_scale", (2, 64)),
        ("L0_ffn_k_scale", (2, 64)), ("L0_ffn_wo", (2, 512, D)),
        ("L0_ffn_ln_g", (2, D)), ("L0_ffn_ln_b", (2, D)),
        ("L0_ffn_w1", (2, D, 5460)), ("L0_ffn_b1", (2, 5460)),
        ("L0_ffn_w2", (2, 2730, D)), ("L0_ffn_b2", (2, D)),
        ("L1_attn_Wq", (D, D)), ("L1_attn_Wk", (D, D)), ("L1_attn_Wv", (D, D)),
        ("L1_attn_Wg", (D, D)), ("L1_attn_Wo", (D, D)),
        ("L1_attn_w1", (D, 2736)), ("L1_attn_w2", (D, 2736)),
        ("L1_attn_w3", (2736, D)),
        ("L1_attn_g_attn", (1, D)), ("L1_attn_g_ffn", (1, D)),
        ("L1_ffn_Wg", (8, D)), ("L1_ffn_bg", (1, 8)),
        ("L1_ffn_w1", (8, 2048, D)), ("L1_ffn_w2", (8, D, 1024)),
    ]:
        extin(nm, shp)
    TAB = {}
    for nm, arr in tabs.items():
        dt = BF16 if arr.dtype == bf else F32
        TAB[nm] = nc.dram_tensor("tab_" + nm, list(arr.shape), dt, kind="ExternalInput")

    out = nc.dram_tensor("out", [N, D], F32, kind="ExternalOutput")

    SC = {}
    def scr(name, shape, dt):
        kind = "ExternalOutput" if name in TAPS else "Internal"
        SC[name] = nc.dram_tensor(name, list(shape), dt, kind=kind)
        return SC[name]

    xn_t = scr("XN_T", (N, D), BF16)
    xn_f32 = scr("XN_F32", (N, D), F32)
    q0_f = scr("Q0_F", (D, N), BF16)
    k0_f = scr("K0_F", (D, N), BF16)
    v0_f = scr("V0_F", (D, N), BF16)
    g0_f = scr("G0_F", (D, N), BF16)
    qr0_t = scr("QR0_T", (N, D), BF16)
    kr0_t = scr("KR0_T", (N, D), BF16)
    ot0_f = scr("OT0_F", (D, N), BF16)
    s0_t = scr("S0_T", (N, D), F32)
    scr("XN_PAD", (B_CORE * PAD_S, D), BF16)
    for l in range(2):
        scr(f"QKV{l}_F", (1536, N), BF16)
        scr(f"QRX{l}_T", (N, 512), BF16)
        scr(f"KRX{l}_T", (N, 512), BF16)
        scr(f"QN{l}_T", (N, 8), F32)
        scr(f"KN{l}_T", (N, 8), F32)
        scr(f"OT1{l}_F", (512, N), BF16)
        scr(f"WO{l}_F", (D, N), BF16)
        scr(f"XF1{l}_F32", (N, D), F32)
        scr(f"XF1{l}_PAD", (B_CORE * PAD_S, D), BF16)
        scr(f"H2N{l}_T", (N, D), BF16)
        scr(f"UA{l}_F", (2730, N), BF16)
        scr(f"UG{l}_F", (2730, N), BF16)
        scr(f"H3{l}_F", (2730, N), BF16)
        scr(f"W2O{l}_F", (D, N), BF16)
        scr(f"XF2{l}_F32", (N, D), F32)
    scr("XF20_PAD", (B_CORE * PAD_S, D), BF16)
    scr("X1_T", (N, D), F32)
    scr("XN1_T", (N, D), BF16)
    scr("XN1_F32", (N, D), F32)
    scr("HA_T", (N, D), BF16)
    for nm in ["QG_F", "KG_F", "VG_F", "ZG_F"]:
        scr(nm, (D, N), BF16)
    scr("QA_T", (N, D), BF16)
    scr("KD_T", (N, D), BF16)
    scr("KD2_T", (N, D), BF16)
    scr("CE63R", (B_CORE * NCH, D), BF16)
    scr("CE127R", (B_CORE * NCH, D), BF16)
    scr("OG_F", (D, N), BF16)
    scr("WOG_F", (D, N), BF16)
    scr("XA_T", (N, D), F32)
    scr("HF_T", (N, D), BF16)
    scr("U1_F", (2736, N), BF16)
    scr("U2_F", (2736, N), BF16)
    scr("FF1_F", (2736, N), BF16)
    scr("W3O_F", (D, N), BF16)
    scr("S1_T", (N, D), F32)
    scr("W1BF", (8, 2048, D), BF16)
    scr("W2BF", (8, D, 1024), BF16)
    scr("CE_F", (8, N), BF16)
    scr("HE_F", (2048, N), BF16)
    for e in range(8):
        scr(f"ACE{e}_F", (D, N), BF16)
    scr("MOE_F", (D, N), BF16)

    with ExitStack() as gctx:
        nc._sb = SemBank(nc, gctx)
        G = load_const_tiles(nc, gctx, TAB)

        final_src = x_in
        if "ln0" in STAGES:
            emit_ln(nc, gctx, G, x_in, xn_t, xn_f32, eps=1e-5, tag="ln0",
                    dst_pad=SC["XN_PAD"])
            final_src = xn_f32
        if "rproj" in STAGES:
            for nm, dst in [("L0_attn_Wq", q0_f), ("L0_attn_Wk", k0_f),
                            ("L0_attn_Wv", v0_f), ("L0_attn_Wg", g0_f)]:
                emit_mm(nc, gctx, G, tag="p0" + nm[-2:],
                        Ktot=D, Mtot=D,
                        wbatch=wbatch_plain(IN[nm]),
                        rhs_load=rhs_xbar(xn_t),
                        out_store=store_F(dst))
        if "rrot" in STAGES:
            emit_ret_rot(nc, gctx, G, TAB, q0_f, k0_f, qr0_t, kr0_t, tag="rrot")
        if "rattn" in STAGES:
            emit_ret_attn(nc, gctx, G, qr0_t, kr0_t, v0_f, ot0_f, tag="rat")
        if "rfin" in STAGES:
            emit_ret_final(nc, gctx, G, ot0_f, g0_f, s0_t, tag="rfin")
            final_src = s0_t

        if "ffn0" in STAGES:
            xn_pad = SC["XN_PAD"]
            emit_zero_pad_rows(nc, gctx, G, xn_pad, tag="zp0")
            stream_pad, stream_f32 = xn_pad, xn_f32
            for l in range(2):
                qkvf = SC[f"QKV{l}_F"]
                emit_mm(nc, gctx, G, tag=f"fq{l}", Ktot=D, Mtot=1536,
                        wbatch=wbatch_plain(IN["L0_ffn_wqkv"], l=l),
                        rhs_load=rhs_xbar_pad(stream_pad),
                        out_store=store_F(qkvf))
                emit_lffn_rot(nc, gctx, G, TAB, IN, l, qkvf,
                              SC[f"QRX{l}_T"], SC[f"KRX{l}_T"],
                              SC[f"QN{l}_T"], SC[f"KN{l}_T"], tag=f"fr{l}")
                emit_lattn(nc, gctx, G, SC[f"QRX{l}_T"], SC[f"KRX{l}_T"], qkvf,
                           SC[f"QN{l}_T"], SC[f"KN{l}_T"], SC[f"OT1{l}_F"],
                           tag=f"fa{l}")
                emit_mm(nc, gctx, G, tag=f"fo{l}", Ktot=512, Mtot=D,
                        wbatch=wbatch_plain(IN["L0_ffn_wo"], l=l),
                        rhs_load=rhs_direct(SC[f"OT1{l}_F"]),
                        out_store=store_F(SC[f"WO{l}_F"]))
                emit_zero_pad_rows(nc, gctx, G, SC[f"XF1{l}_PAD"], tag=f"zp1{l}")
                emit_combine(nc, gctx, G, tag=f"fc1{l}",
                             srcs={"a": stream_f32, "b": SC[f"WO{l}_F"]},
                             dst_f32=SC[f"XF1{l}_F32"], dst_pad=SC[f"XF1{l}_PAD"],
                             mode="resid")
                emit_ln(nc, gctx, G, None, SC[f"H2N{l}_T"], None, eps=1e-5,
                        tag=f"fl{l}", gain=IN["L0_ffn_ln_g"][l:l + 1, :],
                        bias_row=IN["L0_ffn_ln_b"][l:l + 1, :],
                        src_pad=SC[f"XF1{l}_PAD"])
                emit_mm(nc, gctx, G, tag=f"fwa{l}", Ktot=D, Mtot=2730,
                        wbatch=wbatch_plain(IN["L0_ffn_w1"], l=l, c0=0),
                        bias_get=bias_cols(IN["L0_ffn_b1"], l, 0),
                        rhs_load=rhs_xbar(SC[f"H2N{l}_T"]),
                        out_store=store_F(SC[f"UA{l}_F"]))
                emit_mm(nc, gctx, G, tag=f"fwg{l}", Ktot=D, Mtot=2730,
                        wbatch=wbatch_plain(IN["L0_ffn_w1"], l=l, c0=2730),
                        bias_get=bias_cols(IN["L0_ffn_b1"], l, 2730),
                        rhs_load=rhs_xbar(SC[f"H2N{l}_T"]),
                        out_store=store_F(SC[f"UG{l}_F"]))
                emit_ew_F(nc, gctx, G, tag=f"fgg{l}", Ktot=2730,
                          src_a=SC[f"UA{l}_F"], src_g=SC[f"UG{l}_F"],
                          dst=SC[f"H3{l}_F"], act=AF.Gelu)
                emit_mm(nc, gctx, G, tag=f"fw2{l}", Ktot=2730, Mtot=D,
                        wbatch=wbatch_plain(IN["L0_ffn_w2"], l=l),
                        bias_get=bias_cols(IN["L0_ffn_b2"], l, 0),
                        rhs_load=rhs_direct(SC[f"H3{l}_F"]),
                        out_store=store_F(SC[f"W2O{l}_F"]), t_ext=2048)
                if l == 0:
                    emit_zero_pad_rows(nc, gctx, G, SC["XF20_PAD"], tag="zp20")
                emit_combine(nc, gctx, G, tag=f"fc2{l}",
                             srcs={"a": SC[f"XF1{l}_F32"], "b": SC[f"W2O{l}_F"]},
                             dst_f32=SC[f"XF2{l}_F32"],
                             dst_pad=SC["XF20_PAD"] if l == 0 else None,
                             mode="resid")
                stream_pad = SC["XF20_PAD"] if l == 0 else None
                stream_f32 = SC[f"XF2{l}_F32"]
            final_src = SC["XF21_F32"]

        if "comb0" in STAGES:
            emit_combine(nc, gctx, G, tag="cmb0",
                         srcs={"a": x_in, "s": s0_t, "g": SC["XF21_F32"]},
                         dst_f32=SC["X1_T"], mode="gate")
            final_src = SC["X1_T"]

        if "gla_dbg" in STAGES:
            STAGES.add("gla")
        if "ln1" in STAGES:
            emit_ln(nc, gctx, G, SC["X1_T"], SC["XN1_T"], SC["XN1_F32"],
                    eps=1e-5, tag="ln1")
            final_src = SC["XN1_F32"]
        if "gla" in STAGES:
            import os as _os
            _gskip = set((_os.environ.get("GLA_SKIP") or "").split(","))
            emit_ln(nc, gctx, G, SC["XN1_F32"], SC["HA_T"], None, eps=1e-6,
                    tag="grms", gain=IN["L1_attn_g_attn"][0:1, :], rms=True)
            for nm, dst in [("L1_attn_Wq", "QG_F"), ("L1_attn_Wk", "KG_F"),
                            ("L1_attn_Wv", "VG_F"), ("L1_attn_Wg", "ZG_F")]:
                if "proj" not in _gskip:
                    emit_mm(nc, gctx, G, tag="g" + nm[-2:], Ktot=D, Mtot=D,
                            wbatch=wbatch_plain(IN[nm]), rhs_load=rhs_xbar(SC["HA_T"]),
                            out_store=store_F(SC[dst]))
            if "prep" not in _gskip:
                emit_gla_prep(nc, gctx, G, SC["QG_F"], SC["KG_F"], SC["ZG_F"],
                              SC["QA_T"], SC["KD_T"], SC["KD2_T"],
                              SC["CE63R"], SC["CE127R"], tag="gp")
            if "scan" not in _gskip:
                emit_gla_scan(nc, gctx, G, SC["QA_T"], SC["KD_T"], SC["KD2_T"],
                              SC["VG_F"], SC["CE63R"], SC["CE127R"], SC["OG_F"],
                              tag="gs")
            emit_mm(nc, gctx, G, tag="gwo", Ktot=D, Mtot=D,
                    wbatch=wbatch_plain(IN["L1_attn_Wo"]),
                    rhs_load=rhs_direct(SC["OG_F"]),
                    out_store=store_F(SC["WOG_F"]))
            emit_combine(nc, gctx, G, tag="gxa",
                         srcs={"a": SC["XN1_F32"], "b": SC["WOG_F"]},
                         dst_f32=SC["XA_T"], mode="resid")
            emit_ln(nc, gctx, G, SC["XA_T"], SC["HF_T"], None, eps=1e-6,
                    tag="grm2", gain=IN["L1_attn_g_ffn"][0:1, :], rms=True)
            emit_mm(nc, gctx, G, tag="gw1", Ktot=D, Mtot=2736,
                    wbatch=wbatch_plain(IN["L1_attn_w1"]),
                    rhs_load=rhs_xbar(SC["HF_T"]), out_store=store_F(SC["U1_F"]))
            emit_mm(nc, gctx, G, tag="gw2", Ktot=D, Mtot=2736,
                    wbatch=wbatch_plain(IN["L1_attn_w2"]),
                    rhs_load=rhs_xbar(SC["HF_T"]), out_store=store_F(SC["U2_F"]))
            emit_ew_F(nc, gctx, G, tag="gsm", Ktot=2736, src_a=SC["U2_F"],
                      src_g=SC["U1_F"], dst=SC["FF1_F"], act=AF.Silu)
            emit_mm(nc, gctx, G, tag="gw3", Ktot=2736, Mtot=D,
                    wbatch=wbatch_plain(IN["L1_attn_w3"]),
                    rhs_load=rhs_direct(SC["FF1_F"]),
                    out_store=store_F(SC["W3O_F"]), t_ext=2048)
            emit_combine(nc, gctx, G, tag="gs1",
                         srcs={"a": SC["XA_T"], "b": SC["W3O_F"]},
                         dst_f32=SC["S1_T"], mode="resid")
            final_src = SC["S1_T"]

        if "moe" in STAGES:
            emit_moe_cast(nc, gctx, G, IN["L1_ffn_w1"], IN["L1_ffn_w2"],
                          SC["W1BF"], SC["W2BF"], tag="mc")
            emit_moe_gate(nc, gctx, G, IN, SC["XN1_T"], SC["CE_F"], tag="mg")
            for e in range(8):
                emit_mm(nc, gctx, G, tag=f"m1e{e}", Ktot=D, Mtot=2048,
                        wget=wget_xbar3(SC["W1BF"], e, D),
                        rhs_load=rhs_xbar(SC["XN1_T"]),
                        out_store=store_F(SC["HE_F"]), wq="sync")
                emit_ew_F(nc, gctx, G, tag=f"ma{e}", Ktot=D,
                          src_a=SC["HE_F"], a_off=1024, src_g=SC["HE_F"], g_off=0,
                          dst=SC[f"ACE{e}_F"], act=AF.Silu,
                          ce_f=SC["CE_F"], ce_row=e)
            def s2_wget(eng, dst, k0, ks, m0, ms):
                e_, kk = k0 // 1024, k0 % 1024
                return eng.dma_start(dst[0:ks, 0:ms],
                                     SC["W2BF"][e_, m0:m0 + ms, kk:kk + ks],
                                     transpose=True)
            def s2_rhs(dst, k0, ks, t0, tsz):
                e_, kk = k0 // 1024, k0 % 1024
                return [lambda eng: eng.dma_start(
                    dst[0:ks, 0:tsz], SC[f"ACE{e_}_F"][kk:kk + ks, t0:t0 + tsz])]
            emit_mm(nc, gctx, G, tag="m2", Ktot=8 * D, Mtot=D,
                    wget=s2_wget, rhs_load=s2_rhs,
                    out_store=store_F(SC["MOE_F"]), wq="sync", t_ext=1024)

        if "comb1" in STAGES:
            emit_combine(nc, gctx, G, tag="cmb1",
                         srcs={"a": SC["X1_T"], "s": SC["S1_T"],
                               "g_f": SC["MOE_F"]},
                         dst_f32=out, mode="gate")
            final_src = None

        if "lnaff_dbg" in STAGES:
            emit_ln(nc, gctx, G, None, SC["H2N0_T"], None, eps=1e-5,
                    tag="fl0", gain=IN["L0_ffn_ln_g"][0:1, :],
                    bias_row=IN["L0_ffn_ln_b"][0:1, :],
                    src_pad=SC["XF10_PAD"])
            final_src = xn_f32

        if final_src is not None:
            emit_copy_out(nc, gctx, final_src, out, tag="cout")

    nc.finalize()
    return nc


def load_const_tiles(nc, ctx, TAB):
    sb = lambda *a, **k: ctx.enter_context(nc.sbuf_tensor(*a, **k))
    G = {}
    specs = [
        ("ident_bf16", [128, 128], BF16), ("ident_f32", [128, 128], F32),
        ("e0col", [128, 1], BF16), ("bc_lo", [1, 128], BF16),
        ("bc_hi", [1, 128], BF16), ("bc_all", [1, 128], BF16),
        ("ones_row", [1, 512], BF16),
        ("tril_inc", [128, 128], BF16), ("ucum", [128, 128], F32),
        ("sel63", [128, 128], F32), ("sel127", [128, 128], F32),
        ("sel_moe", [8, 1024], BF16), ("zrow", [1, 1024], BF16),
    ] + [(f"trilq_{j}", [128, 512], BF16) for j in range(4)] \
      + [(f"lamu_{j}", [128, 512], BF16) for j in range(4)]
    s_c = nc._sb.get("sb_const")
    p = Prog(nc)
    for nm, shape, dt in specs:
        G[nm] = sb("g_" + nm, shape, dt)
        p.add("sync", lambda e, g=G[nm], t=TAB[nm], sh=shape: e.dma_start(
            g[tuple(slice(0, x) for x in sh)], t[tuple(slice(0, x) for x in sh)]),
            inc=s_c, inc_n=16)
    p.add("sync", lambda e: e.nop(), waits=[w(s_c)])
    p.emit()
    return G


def wget_plain(wt):
    def f(eng, dst, k0, ks, m0, ms):
        return eng.dma_start(dst[0:ks, 0:ms], wt[k0:k0 + ks, m0:m0 + ms])
    return f


def wbatch_plain(wt, l=None, c0=0):
    shape = list(wt.shape)
    M_full = shape[-1]
    K_full = shape[-2]
    base0 = (l * K_full * M_full) if l is not None else 0

    def f(dst_t, m0, ms, kch):
        outs = []
        nfull = sum(1 for _, ks in kch if ks == 128)
        rowlen = len(kch) * 128
        if nfull:
            src = bass.AP(wt, base0 + c0 + m0,
                          [[M_full, 128], [128 * M_full, nfull], [1, ms]])
            dst = bass.AP(dst_t, 0, [[rowlen, 128], [128, nfull], [1, ms]])
            outs.append(lambda e, src=src, dst=dst: e.dma_start(dst, src))
        if nfull < len(kch):
            k0, ks = kch[nfull]
            src2 = bass.AP(wt, base0 + c0 + m0 + k0 * M_full,
                           [[M_full, ks], [1, ms]])
            dst2 = bass.AP(dst_t, nfull * 128, [[rowlen, ks], [1, ms]])
            outs.append(lambda e, src2=src2, dst2=dst2: e.dma_start(dst2, src2))
        return outs
    return f


def wget_plain3(wt, l):
    def f(eng, dst, k0, ks, m0, ms):
        return eng.dma_start(dst[0:ks, 0:ms], wt[l, k0:k0 + ks, m0:m0 + ms])
    return f


def wget_cols(wt, l, c0):
    def f(eng, dst, k0, ks, m0, ms):
        return eng.dma_start(dst[0:ks, 0:ms],
                             wt[l, k0:k0 + ks, c0 + m0:c0 + m0 + ms])
    return f


def bias_cols(bv, l, c0):
    def f(eng, dst, m0, ms):
        return eng.dma_start(dst, bv[l:l + 1, c0 + m0:c0 + m0 + ms])
    return f


def rhs_xbar(src_t, col_off=0, row_off=0):
    def f(dst, k0, ks, t0, tsz):
        return [lambda e: e.dma_start(dst[0:ks, 0:tsz],
                                      src_t[row_off + t0: row_off + t0 + tsz,
                                            col_off + k0: col_off + k0 + ks],
                                      transpose=True)]
    return f


def rhs_direct(src_f, row_off=0):
    def f(dst, k0, ks, t0, tsz):
        return [lambda e: e.dma_start(dst[0:ks, 0:tsz],
                                      src_f[row_off + k0: row_off + k0 + ks,
                                            t0:t0 + tsz])]
    return f


def rhs_xbar_pad(src_pad, shift_ge=512):
    def f(dst, k0, ks, t0, tsz):
        outs = []
        sh = 0 if k0 + ks <= shift_ge else 1
        assert k0 >= shift_ge or k0 + ks <= shift_ge
        for b0 in range(t0 // S, (t0 + tsz - 1) // S + 1):
            lo = max(t0, b0 * S)
            hi = min(t0 + tsz, (b0 + 1) * S)
            row = b0 * (S + 1) + (1 - sh) + (lo - b0 * S)
            outs.append(lambda e, lo=lo, hi=hi, row=row: e.dma_start(
                dst[0:ks, lo - t0: hi - t0],
                src_pad[row: row + (hi - lo), k0:k0 + ks], transpose=True))
        return outs
    return f


def store_F(dst_f, row_off=0):
    def f(eng, src, m0, ms, t0, tsz):
        return eng.dma_start(dst_f[row_off + m0: row_off + m0 + ms, t0:t0 + tsz], src)
    return f


def emit_mm(nc, ctx, G, tag, *, Ktot, Mtot, wget=None, rhs_load, out_store,
            bias_get=None, t_ext=None, evac=None, wq="gpsimd", wbatch=None):
    with ExitStack() as pctx:
        sb = lambda *a, **k: pctx.enter_context(nc.sbuf_tensor(*a, **k))
        psum = lambda *a, **k: pctx.enter_context(nc.psum_tensor(*a, **k))
        if t_ext is None:
            t_ext = N if Ktot <= 1100 else 2048
        kch = _chunks(Ktot, 128)
        mch = _chunks(Mtot, 128)
        groups = _chunks(N, t_ext)
        ntb = t_ext // 512
        nwpm = len(kch) + (1 if bias_get else 0)

        rhs_sb = [sb(f"{tag}_r{j}", [128, t_ext], BF16, is_transpose_dest=True)
                  for j in range(len(kch))]
        wsb_w = (len(kch) * 128) if wbatch is not None else 128
        wsb = [sb(f"{tag}_w{j}", [128, wsb_w], BF16) for j in range(2)]
        ev = [sb(f"{tag}_e{j}", [128, t_ext], BF16) for j in range(2)]
        ps = psum(f"{tag}_ps", [128, t_ext], F32)

        s_rhs = nc._sb.get("sb_rhs")
        s_w = [nc._sb.get(f"sb_w{j}" if wq == "gpsimd" else f"sb_x{j}")
               for j in range(2)]
        s_mm = nc._sb.get("sb_mm")
        s_ev = nc._sb.get("sb_ev")
        s_st = [nc._sb.get(f"sb_st{j}") for j in range(2)]

        p = Prog(nc)
        jw = 0
        wtile_done = []
        ev_done = []
        st_done = [[], []]
        rhs_done = []

        for g, (t0, text) in enumerate(groups):
            waits = [w(s_mm)] if g > 0 else []
            for j, (k0, ks) in enumerate(kch):
                for fi, fn in enumerate(rhs_load(rhs_sb[j], k0, ks, t0, text)):
                    p.add("sync", fn, waits=waits if (j == 0 and fi == 0) else (),
                          inc=s_rhs, inc_n=16)
            rhs_done.append(s_rhs.v)
            for mi, (m0, ms) in enumerate(mch):
                gm = g * len(mch) + mi
                e2 = gm % 2
                if wbatch is not None:
                    wb = jw % 2
                    wwaits = []
                    if jw >= 2:
                        wwaits.append((s_mm, wtile_done[jw - 2]))
                    for fi, fn in enumerate(wbatch(wsb[wb], m0, ms, kch)):
                        p.add(wq, fn, waits=wwaits if fi == 0 else (),
                              inc=s_w[wb], inc_n=16)
                    mm_waits = [w(s_w[wb]), (s_rhs, rhs_done[g])]
                    if gm > 0:
                        mm_waits.append((s_ev, ev_done[gm - 1]))
                    for ki, (k0, ks) in enumerate(kch):
                        for tb in range(ntb):
                            p.add("tensor", (lambda e, tb=tb, wb=wb, ki=ki, ks=ks,
                                             ms=ms, last=(ki == len(kch) - 1
                                                          and bias_get is None):
                                             e.matmul(ps[0:ms, tb * 512:(tb + 1) * 512],
                                                      wsb[wb][0:ks, ki * 128: ki * 128 + ms],
                                                      rhs_sb[ki][0:ks, tb * 512:(tb + 1) * 512],
                                                      start=(ki == 0), stop=last)),
                                  waits=mm_waits if (ki == 0 and tb == 0) else (),
                                  inc=s_mm if (ki == len(kch) - 1 and tb == ntb - 1) else None)
                    wtile_done.append(s_mm.v)
                    jw += 1
                else:
                  for ki, (k0, ks) in enumerate(kch):
                    wb = jw % 2
                    wwaits = []
                    if jw >= 2:
                        wwaits.append((s_mm, wtile_done[jw - 2]))
                    p.add(wq, (lambda e, dst=wsb[wb], k0=k0, ks=ks, m0=m0,
                               ms=ms, wg=wget: wg(e, dst, k0, ks, m0, ms)),
                          waits=wwaits, inc=s_w[wb], inc_n=16)
                    mm_waits = [w(s_w[wb]), (s_rhs, rhs_done[g])]
                    if gm > 0:
                        mm_waits.append((s_ev, ev_done[gm - 1]))
                    for tb in range(ntb):
                        p.add("tensor", (lambda e, tb=tb, wb=wb, ki=ki, ks=ks,
                                         ms=ms, j=ki, last=(ki == len(kch) - 1
                                                            and bias_get is None):
                                         e.matmul(ps[0:ms, tb * 512:(tb + 1) * 512],
                                                  wsb[wb][0:ks, 0:ms],
                                                  rhs_sb[j][0:ks, tb * 512:(tb + 1) * 512],
                                                  start=(ki == 0), stop=last)),
                              waits=mm_waits if tb == 0 else (),
                              inc=s_mm if tb == ntb - 1 else None)
                    wtile_done.append(s_mm.v)
                    jw += 1
                if bias_get is not None:
                    wb = jw % 2
                    wwaits = []
                    if jw >= 2:
                        wwaits.append((s_mm, wtile_done[jw - 2]))
                    p.add("gpsimd", (lambda e, dst=wsb[wb], m0=m0, ms=ms,
                                     bg=bias_get: bg(e, dst[0:1, 0:ms], m0, ms)),
                          waits=wwaits, inc=s_w[wb], inc_n=16)
                    for tb in range(ntb):
                        p.add("tensor", (lambda e, tb=tb, wb=wb, ms=ms:
                                         e.matmul(ps[0:ms, tb * 512:(tb + 1) * 512],
                                                  wsb[wb][0:1, 0:ms],
                                                  G["ones_row"][0:1, :],
                                                  start=False, stop=True)),
                              waits=[w(s_w[wb])] if tb == 0 else (),
                              inc=s_mm if tb == ntb - 1 else None)
                    wtile_done.append(s_mm.v)
                    jw += 1
                evw = [w(s_mm)]
                if gm >= 2:
                    evw.append((s_st[e2], st_done[e2][-1]))
                if evac is None:
                    p.add("vector", (lambda e, e2=e2, ms=ms, text=text:
                                     e.tensor_copy(ev[e2][0:ms, 0:text],
                                                   ps[0:ms, 0:text])),
                          waits=evw, inc=s_ev)
                else:
                    evac(p, evw, s_ev, ps, ev[e2], m0, ms, t0, text)
                ev_done.append(s_ev.v)
                p.add("sync", (lambda e, e2=e2, m0=m0, ms=ms, t0=t0, text=text,
                               os=out_store: os(e, ev[e2][0:ms, 0:text], m0, ms,
                                                t0, text)),
                      waits=[(s_ev, ev_done[gm])], inc=s_st[e2], inc_n=16)
                st_done[e2].append(s_st[e2].v)
        p.add("sync", lambda e: e.nop(), waits=[w(s_st[0]), w(s_st[1])])
        p.emit()


def emit_ln(nc, ctx, G, src, dst_bf, dst_f32, eps, tag, gain=None, bias_row=None,
            rms=False, src_pad=None, dst_pad=None):
    with ExitStack() as pctx:
        sb = lambda *a, **k: pctx.enter_context(nc.sbuf_tensor(*a, **k))
        s_ld = [nc._sb.get(f"sb_l{j}") for j in range(2)]
        s_v = nc._sb.get("sb_v")
        s_a = nc._sb.get("sb_a")
        s_st = [nc._sb.get(f"sb_s{j}") for j in range(2)]
        s_g = nc._sb.get("sb_g")
        xt = [sb(f"{tag}_x{j}", [128, D], BF16 if src_pad is not None else F32)
              for j in range(2)]
        xc = [sb(f"{tag}_c{j}", [128, D], F32) for j in range(2)]
        sq = sb(f"{tag}_sq", [128, D], F32)
        ob = [sb(f"{tag}_ob{j}", [128, D], BF16) for j in range(2)]
        of = [sb(f"{tag}_of{j}", [128, D], F32) for j in range(2)]
        sm = sb(f"{tag}_sm", [128, 8], F32)
        gb = sb(f"{tag}_gb", [128, D], F32) if gain is not None else None
        bb = sb(f"{tag}_bb", [128, D], F32) if bias_row is not None else None
        p = Prog(nc)
        if gain is not None or bias_row is not None:
            psb = pctx.enter_context(nc.psum_tensor(f"{tag}_pg", [128, 1024], F32))
            row = sb(f"{tag}_row", [1, D], BF16)
            rowf = sb(f"{tag}_rowf", [1, D], F32)
            for which, (vec, dstt) in enumerate([(gain, gb), (bias_row, bb)]):
                if vec is None:
                    continue
                v1 = p.add("sync", (lambda e, vec=vec: e.dma_start(
                    rowf[0:1, :], vec)), waits=[w(s_v)], inc=s_g, inc_n=16)
                v2 = p.add("vector", lambda e: e.tensor_copy(row[0:1, :], rowf[0:1, :]),
                           waits=[(s_g, v1)], inc=s_v)
                smm = nc._sb.get("sb_bcmm")
                for h in range(2):
                    p.add("tensor", (lambda e, h=h: e.matmul(
                        psb[:, h * 512:(h + 1) * 512], G["bc_all"][0:1, :],
                        row[0:1, h * 512:(h + 1) * 512],
                        start=True, stop=True)),
                        waits=[(s_v, v2)] if h == 0 else (), inc=smm)
                p.add("vector", (lambda e, dstt=dstt: e.tensor_copy(dstt[:, :], psb[:, :])),
                      waits=[w(smm)], inc=s_v)
        for i in range(NT):
            b = i % 2
            lw = []
            if i >= 2:
                lw.append((s_st[b], s_st[b].v))
            if src_pad is None:
                vld = p.add("sync", (lambda e, b=b, i=i: e.dma_start(
                    xt[b][:, :], src[ts(i, 128), :])), waits=lw, inc=s_ld[b], inc_n=16)
            else:
                pr = pad_row(i * 128)
                p.add("sync", (lambda e, b=b, pr=pr: e.dma_start(
                    xt[b][:, 0:512], src_pad[pr:pr + 128, 0:512])),
                    waits=lw, inc=s_ld[b], inc_n=16)
                vld = p.add("sync", (lambda e, b=b, pr=pr: e.dma_start(
                    xt[b][:, 512:1024], src_pad[pr - 1:pr + 127, 512:1024])),
                    inc=s_ld[b], inc_n=16)
            v1 = p.add("vector", (lambda e, b=b: e.reduce_sum(sm[:, 0:1], xt[b][:, :], AX.X)),
                       waits=[(s_ld[b], vld)], inc=s_v)
            if not rms:
                v2 = p.add("vector", lambda e: e.tensor_scalar(
                    sm[:, 1:2], sm[:, 0:1], -1.0 / D, None, ALU.mult),
                    waits=[(s_v, v1)], inc=s_v)
                v3 = p.add("vector", (lambda e, b=b: e.tensor_scalar(
                    xc[b][:, :], xt[b][:, :], sm[:, 1:2], None, ALU.add)),
                    waits=[(s_v, v2)], inc=s_v)
            else:
                v3 = p.add("vector", (lambda e, b=b: e.tensor_copy(xc[b][:, :], xt[b][:, :])),
                           waits=[(s_v, v1)], inc=s_v)
            va = p.add("scalar", (lambda e, b=b: e.activation(
                sq[:, :], xc[b][:, :], AF.Square, accum_out=sm[:, 2:3])),
                waits=[(s_v, v3)], inc=s_a)
            v4 = p.add("vector", lambda e: e.tensor_scalar(
                sm[:, 3:4], sm[:, 2:3], 1.0 / D, eps, ALU.mult, ALU.add),
                waits=[(s_a, va)], inc=s_v)
            v5 = p.add("vector", lambda e: e.reciprocal(sm[:, 4:5], sm[:, 3:4]),
                       waits=[(s_v, v4)], inc=s_v)
            va2 = p.add("scalar", lambda e: e.activation(sm[:, 5:6], sm[:, 4:5], AF.Sqrt),
                        waits=[(s_v, v5)], inc=s_a)
            if gain is None:
                v6 = p.add("vector", (lambda e, b=b: e.tensor_scalar(
                    ob[b][:, :], xc[b][:, :], sm[:, 5:6], None, ALU.mult)),
                    waits=[(s_a, va2)], inc=s_v)
                v7 = p.add("vector", (lambda e, b=b: e.tensor_scalar(
                    of[b][:, :], xc[b][:, :], sm[:, 5:6], None, ALU.mult)),
                    inc=s_v) if dst_f32 is not None else v6
            else:
                v6a = p.add("vector", (lambda e, b=b: e.tensor_scalar(
                    xc[b][:, :], xc[b][:, :], sm[:, 5:6], None, ALU.mult)),
                    waits=[(s_a, va2)], inc=s_v)
                if bias_row is not None:
                    v6b = p.add("vector", (lambda e, b=b: e.tensor_mul(
                        xc[b][:, :], xc[b][:, :], gb[:, :])),
                        waits=[(s_v, v6a)], inc=s_v)
                    v6 = p.add("vector", (lambda e, b=b: e.tensor_add(
                        xc[b][:, :], xc[b][:, :], bb[:, :])), waits=[(s_v, v6b)], inc=s_v)
                else:
                    v6 = p.add("vector", (lambda e, b=b: e.tensor_mul(
                        xc[b][:, :], xc[b][:, :], gb[:, :])), waits=[(s_v, v6a)], inc=s_v)
                v6 = p.add("vector", (lambda e, b=b: e.tensor_copy(ob[b][:, :], xc[b][:, :])),
                           waits=[(s_v, v6)], inc=s_v)
                v7 = p.add("vector", (lambda e, b=b: e.tensor_copy(of[b][:, :], xc[b][:, :])),
                           inc=s_v) if dst_f32 is not None else v6
            last_v = max(v6, v7 if dst_f32 is not None else v6)
            if dst_bf is not None:
                p.add("scalar", (lambda e, b=b, i=i: e.dma_start(
                    dst_bf[ts(i, 128), :], ob[b][:, :])), waits=[(s_v, last_v)],
                    inc=s_st[b], inc_n=16)
            if dst_f32 is not None:
                p.add("scalar", (lambda e, b=b, i=i: e.dma_start(
                    dst_f32[ts(i, 128), :], of[b][:, :])), waits=[(s_v, last_v)] if dst_bf is None else (),
                    inc=s_st[b], inc_n=16)
            if dst_pad is not None:
                pr = pad_row(i * 128)
                p.add("scalar", (lambda e, b=b, pr=pr: e.dma_start(
                    dst_pad[pr:pr + 128, :], ob[b][:, :])),
                    waits=[(s_v, last_v)] if (dst_bf is None and dst_f32 is None) else (),
                    inc=s_st[b], inc_n=16)
        p.add("scalar", lambda e: e.nop(), waits=[w(s_st[0]), w(s_st[1])])
        p.emit()


def emit_copy_out(nc, ctx, src, dst, tag):
    with ExitStack() as pctx:
        sb = lambda *a, **k: pctx.enter_context(nc.sbuf_tensor(*a, **k))
        s_ld = [nc._sb.get(f"sb_l{j}") for j in range(2)]
        s_st = [nc._sb.get(f"sb_y{j}") for j in range(2)]
        t = [sb(f"{tag}_t{j}", [128, D], F32) for j in range(2)]
        p = Prog(nc)
        for i in range(NT):
            b = i % 2
            lw = [(s_st[b], s_st[b].v)] if i >= 2 else []
            vld = p.add("sync", (lambda e, b=b, i=i: e.dma_start(
                t[b][:, :], src[ts(i, 128), :])), waits=lw, inc=s_ld[b], inc_n=16)
            p.add("sync", (lambda e, b=b, i=i: e.dma_start(
                dst[ts(i, 128), :], t[b][:, :])), waits=[(s_ld[b], vld)],
                inc=s_st[b], inc_n=16)
        p.add("sync", lambda e: e.nop(), waits=[w(s_st[0]), w(s_st[1])])
        p.emit()


def _chunks(total, step):
    out = []
    o = 0
    while o < total:
        out.append((o, min(step, total - o)))
        o += step
    return out


def emit_ret_rot(nc, ctx, G, TAB, q_f, k_f, qr_t, kr_t, tag):
    with ExitStack() as pctx:
        sb = lambda *a, **k: pctx.enter_context(nc.sbuf_tensor(*a, **k))
        s_ld = [nc._sb.get(f"sb_l{j}") for j in range(2)]
        s_v = nc._sb.get("sb_v")
        s_st = [nc._sb.get(f"sb_s{j}") for j in range(2)]
        qt = [sb(f"{tag}_q{j}", [128, D], BF16, is_transpose_dest=True) for j in range(2)]
        kt = [sb(f"{tag}_k{j}", [128, D], BF16, is_transpose_dest=True) for j in range(2)]
        rc = [sb(f"{tag}_rc{j}", [128, 512], F32) for j in range(2)]
        rs = [sb(f"{tag}_rs{j}", [128, 512], F32) for j in range(2)]
        rck = [sb(f"{tag}_rck{j}", [128, 512], F32) for j in range(2)]
        rsk = [sb(f"{tag}_rsk{j}", [128, 512], F32) for j in range(2)]
        tmp = sb(f"{tag}_tmp", [128, D], F32)
        oq = [sb(f"{tag}_oq{j}", [128, D], BF16) for j in range(2)]
        ok = [sb(f"{tag}_ok{j}", [128, D], BF16) for j in range(2)]

        def ev3(t, par):
            return bass.AP(t, par, [[D, 128], [64, 16], [2, 32]])

        def tb3(t):
            return bass.AP(t, 0, [[512, 128], [32, 16], [1, 32]])

        p = Prog(nc)
        for i in range(NT):
            b = i % 2
            srow = (i % (S // 128)) * 128
            lw = [(s_st[b], s_st[b].v)] if i >= 2 else []
            p.add("sync", (lambda e, b=b, i=i: e.dma_start(
                qt[b][:, :], q_f[0:D, i * 128:(i + 1) * 128], transpose=True)),
                waits=lw, inc=s_ld[b], inc_n=16)
            p.add("sync", (lambda e, b=b, i=i: e.dma_start(
                kt[b][:, :], k_f[0:D, i * 128:(i + 1) * 128], transpose=True)),
                inc=s_ld[b], inc_n=16)
            for dst, tnm in [(rc[b], "rcq"), (rs[b], "rsq"), (rck[b], "rck"),
                             (rsk[b], "rsk")]:
                p.add("sync", (lambda e, dst=dst, tnm=tnm, srow=srow: e.dma_start(
                    dst[:, :], TAB[tnm][srow:srow + 128, :])), inc=s_ld[b], inc_n=16)
            vld = s_ld[b].v
            for src_t, ctab, stab, dst_t in [(qt, rc, rs, oq), (kt, rck, rsk, ok)]:
                st, dt, ct, sb_ = src_t[b], dst_t[b], ctab[b], stab[b]
                tmp3 = lambda par: bass.AP(tmp, par, [[D, 128], [64, 16], [2, 32]])
                dst3 = lambda par, dt=dt: bass.AP(dt, par, [[D, 128], [64, 16], [2, 32]])
                p.add("vector", (lambda e, st=st, ct=ct, tmp3=tmp3:
                                 e.tensor_tensor(tmp3(0), ev3(st, 0), tb3(ct), ALU.mult)),
                      waits=[(s_ld[b], vld)], inc=s_v)
                p.add("vector", (lambda e, st=st, ct=ct, tmp3=tmp3:
                                 e.tensor_tensor(tmp3(1), ev3(st, 1), tb3(ct), ALU.mult)),
                      inc=s_v)
                a3 = p.add("vector", (lambda e, st=st, sb_=sb_, dst3=dst3:
                                      e.tensor_tensor(dst3(0), ev3(st, 1), tb3(sb_), ALU.mult)),
                           inc=s_v)
                p.add("vector", (lambda e, dst3=dst3, tmp3=tmp3:
                                 e.tensor_tensor(dst3(0), tmp3(0), dst3(0), ALU.subtract)),
                      waits=[(s_v, a3)], inc=s_v)
                a5 = p.add("vector", (lambda e, st=st, sb_=sb_, dst3=dst3:
                                      e.tensor_tensor(dst3(1), ev3(st, 0), tb3(sb_), ALU.mult)),
                           inc=s_v)
                p.add("vector", (lambda e, dst3=dst3, tmp3=tmp3:
                                 e.tensor_tensor(dst3(1), tmp3(1), dst3(1), ALU.add)),
                      waits=[(s_v, a5)], inc=s_v)
            vdone = s_v.v
            p.add("scalar", (lambda e, b=b, i=i: e.dma_start(
                qr_t[ts(i, 128), :], oq[b][:, :])), waits=[(s_v, vdone)],
                inc=s_st[b], inc_n=16)
            p.add("scalar", (lambda e, b=b, i=i: e.dma_start(
                kr_t[ts(i, 128), :], ok[b][:, :])), inc=s_st[b], inc_n=16)
        p.add("scalar", lambda e: e.nop(), waits=[w(s_st[0]), w(s_st[1])])
        p.emit()


def emit_ret_attn(nc, ctx, G, qr_t, kr_t, v_f, ot_f, tag):
    with ExitStack() as pctx:
        sb = lambda *a, **k: pctx.enter_context(nc.sbuf_tensor(*a, **k))
        psum = lambda *a, **k: pctx.enter_context(nc.psum_tensor(*a, **k))
        qr = [sb(f"{tag}_qr{j}", [128, S], BF16, is_transpose_dest=True) for j in range(8)]
        kr = [sb(f"{tag}_kr{j}", [128, S], BF16, is_transpose_dest=True) for j in range(8)]
        vt = [sb(f"{tag}_vt{j}", [128, D], BF16, is_transpose_dest=True) for j in range(16)]
        ssb = [sb(f"{tag}_s{j}", [128, 512], BF16) for j in range(2)]
        sab = [sb(f"{tag}_a{j}", [128, 512], BF16) for j in range(2)]
        dsb = sb(f"{tag}_d", [1, 512], F32)
        dsb2 = sb(f"{tag}_d2", [1, 512], F32)
        rcp = [sb(f"{tag}_rc{j}", [1, 512], BF16) for j in range(2)]
        rbc = sb(f"{tag}_rbc", [128, 512], BF16)
        oev = [sb(f"{tag}_o{j}", [128, 512], BF16) for j in range(2)]
        psS = [psum(f"{tag}_pS{j}", [128, 512], F32) for j in range(2)]
        psAb = [psum(f"{tag}_pA{j}", [1, 512], F32) for j in range(2)]
        psO = [psum(f"{tag}_pO{j}", [128, 512], F32) for j in range(2)]
        psR = psum(f"{tag}_pR", [128, 512], F32)

        s_ld = nc._sb.get("sb_ld")
        s_mm = nc._sb.get("sb_mm")
        s_v = nc._sb.get("sb_v")
        s_a = nc._sb.get("sb_a")
        s_st = [nc._sb.get(f"sb_t{j}") for j in range(2)]

        p = Prog(nc)
        for b in range(B_CORE):
            tb0 = b * S
            ldw = [w(s_mm)] if b > 0 else []
            for j in range(8):
                p.add("sync", (lambda e, j=j, tb0=tb0: e.dma_start(
                    qr[j][:, :], qr_t[tb0:tb0 + S, j * 128:(j + 1) * 128],
                    transpose=True)), waits=ldw if j == 0 else (), inc=s_ld, inc_n=16)
                p.add("sync", (lambda e, j=j, tb0=tb0: e.dma_start(
                    kr[j][:, :], kr_t[tb0:tb0 + S, j * 128:(j + 1) * 128],
                    transpose=True)), inc=s_ld, inc_n=16)
            for j in range(16):
                p.add("sync", (lambda e, j=j, tb0=tb0: e.dma_start(
                    vt[j][:, :], v_f[0:D, tb0 + j * 128: tb0 + (j + 1) * 128],
                    transpose=True)), inc=s_ld, inc_n=16)
            vld = s_ld.v
            for hp in range(8):
                h = 2 * hp
                for qc in range(4):
                    kmax = 4 * qc + 4
                    ev_prev_o = s_st
                    for kc in range(kmax):
                        diag = kc - 4 * qc
                        sc = kc % 2
                        mmw = [(s_ld, vld)]
                        mmw.append((s_v, s_v.v))
                        vmm = p.add("tensor", (lambda e, hp=hp, kc=kc, qc=qc, sc=sc:
                                               e.matmul(psS[sc][:, :],
                                                        kr[hp][0:64, kc * 128:(kc + 1) * 128],
                                                        qr[hp][0:64, qc * 512:(qc + 1) * 512],
                                                        start=True, stop=True,
                                                        tile_position=(0, 0))),
                                    waits=mmw, inc=s_mm)
                        vmm = p.add("tensor", (lambda e, hp=hp, kc=kc, qc=qc, sc=sc:
                                               e.matmul(psS[sc ^ 1][:, :],
                                                        kr[hp][64:128, kc * 128:(kc + 1) * 128],
                                                        qr[hp][64:128, qc * 512:(qc + 1) * 512],
                                                        start=True, stop=True,
                                                        tile_position=(64, 0))),
                                    inc=s_mm)
                        for hh in range(2):
                            src_ps = psS[sc] if hh == 0 else psS[sc ^ 1]
                            if diag >= 0:
                                vv = p.add("vector", (lambda e, hh=hh, src_ps=src_ps, diag=diag:
                                                      e.tensor_tensor(ssb[hh][:, :], src_ps[:, :],
                                                                      G[f"trilq_{diag}"][:, :],
                                                                      ALU.mult)),
                                           waits=[(s_mm, vmm)], inc=s_v)
                            else:
                                vv = p.add("vector", (lambda e, hh=hh, src_ps=src_ps:
                                                      e.tensor_copy(ssb[hh][:, :], src_ps[:, :])),
                                           waits=[(s_mm, vmm)], inc=s_v)
                            vaa = p.add("scalar", (lambda e, hh=hh: e.activation(
                                sab[hh][:, :], ssb[hh][:, :], AF.Abs)),
                                waits=[(s_v, vv)], inc=s_a)
                            p.add("tensor", (lambda e, hh=hh, kc=kc:
                                             e.matmul(psAb[hh][:, :], G["e0col"][:, 0:1],
                                                      sab[hh][:, :], start=(kc == 0),
                                                      stop=(kc == kmax - 1))),
                                  waits=[(s_a, vaa)], inc=s_mm)
                            p.add("tensor", (lambda e, hh=hh, kc=kc, hp=hp, kmax=kmax:
                                             e.matmul(psO[hh][hh * 64:(hh + 1) * 64, :],
                                                      vt[kc][:, (2 * hp + hh) * 64:(2 * hp + hh + 1) * 64],
                                                      ssb[hh][:, :], start=(kc == 0),
                                                      stop=(kc == kmax - 1),
                                                      tile_position=(0, 64 * hh))),
                                  waits=[(s_v, vv)], inc=s_mm)
                    vmm_all = s_mm.v
                    for hh in range(2):
                        vv = p.add("vector", (lambda e, hh=hh: e.tensor_scalar(
                            dsb[0:1, :], psAb[hh][0:1, :], 1.0, 50000.0, ALU.max, ALU.min)),
                            waits=[(s_mm, vmm_all)], inc=s_v)
                        vv = p.add("vector", lambda e: e.reciprocal(dsb2[0:1, :], dsb[0:1, :]),
                                   waits=[(s_v, vv)], inc=s_v)
                        vv = p.add("vector", (lambda e, hh=hh: e.tensor_copy(
                            rcp[hh][0:1, :], dsb2[0:1, :])), waits=[(s_v, vv)], inc=s_v)
                    vrc = s_v.v
                    vbc = p.add("tensor", (lambda e: e.matmul(
                        psR[:, :], G["bc_lo"][0:1, :], rcp[0][0:1, :], start=True,
                        stop=False)), waits=[(s_v, vrc)], inc=s_mm)
                    vbc = p.add("tensor", (lambda e: e.matmul(
                        psR[:, :], G["bc_hi"][0:1, :], rcp[1][0:1, :], start=False,
                        stop=True)), inc=s_mm)
                    vv = p.add("vector", lambda e: e.tensor_copy(rbc[:, :], psR[:, :]),
                               waits=[(s_mm, vbc)], inc=s_v)
                    e2 = (hp * 4 + qc) % 2
                    stw = [(s_st[e2], s_st[e2].v)]
                    vo = p.add("vector", (lambda e, e2=e2: e.tensor_tensor(
                        oev[e2][0:64, :], psO[0][0:64, :], rbc[0:64, :], ALU.mult)),
                        waits=[(s_v, vv)] + stw, inc=s_v)
                    vo = p.add("vector", (lambda e, e2=e2: e.tensor_tensor(
                        oev[e2][64:128, :], psO[1][64:128, :], rbc[64:128, :],
                        ALU.mult)), waits=[(s_v, vo)], inc=s_v)
                    p.add("sync", (lambda e, e2=e2, hp=hp, qc=qc, tb0=tb0: e.dma_start(
                        ot_f[hp * 128:(hp + 1) * 128, tb0 + qc * 512: tb0 + (qc + 1) * 512],
                        oev[e2][:, :])), waits=[(s_v, vo)], inc=s_st[e2], inc_n=16)
        p.add("sync", lambda e: e.nop(), waits=[w(s_st[0]), w(s_st[1])])
        p.emit()


def emit_ret_final(nc, ctx, G, ot_f, g_f, s0_t, tag):
    with ExitStack() as pctx:
        sb = lambda *a, **k: pctx.enter_context(nc.sbuf_tensor(*a, **k))
        s_ld = [nc._sb.get(f"sb_l{j}") for j in range(2)]
        s_v = nc._sb.get("sb_v")
        s_a = nc._sb.get("sb_a")
        s_st = [nc._sb.get(f"sb_s{j}") for j in range(2)]
        ot = [sb(f"{tag}_o{j}", [128, D], BF16, is_transpose_dest=True) for j in range(2)]
        gt = [sb(f"{tag}_g{j}", [128, D], BF16, is_transpose_dest=True) for j in range(2)]
        sq = sb(f"{tag}_sq", [128, D], F32)
        sil = sb(f"{tag}_sil", [128, D], F32)
        grp = sb(f"{tag}_grp", [128, 16 * 4], F32)
        o1 = sb(f"{tag}_onorm", [128, D], F32)
        of = [sb(f"{tag}_of{j}", [128, D], F32) for j in range(2)]

        def g3(col):
            return bass.AP(grp, col * 16, [[64, 128], [1, 16]])

        p = Prog(nc)
        for i in range(NT):
            b = i % 2
            lw = [(s_st[b], s_st[b].v)] if i >= 2 else []
            v1 = p.add("sync", (lambda e, b=b, i=i: e.dma_start(
                ot[b][:, :], ot_f[0:D, i * 128:(i + 1) * 128], transpose=True)),
                waits=lw, inc=s_ld[b], inc_n=16)
            p.add("sync", (lambda e, b=b, i=i: e.dma_start(
                gt[b][:, :], g_f[0:D, i * 128:(i + 1) * 128], transpose=True)),
                inc=s_ld[b], inc_n=16)
            vld = s_ld[b].v
            va = p.add("scalar", (lambda e, b=b: e.activation(sq[:, :], ot[b][:, :],
                                                              AF.Square)),
                       waits=[(s_ld[b], vld)], inc=s_a)
            vv = p.add("vector", lambda e: e.tensor_reduce(
                g3(0), bass.AP(sq, 0, [[D, 128], [64, 16], [1, 64]]), AX.X, ALU.add),
                waits=[(s_a, va)], inc=s_v)
            vv = p.add("vector", lambda e: e.tensor_scalar(
                g3(1), g3(0), 1.0 / 64, 1e-6, ALU.mult, ALU.add),
                waits=[(s_v, vv)], inc=s_v)
            vv = p.add("vector", lambda e: e.reciprocal(g3(2), g3(1)),
                       waits=[(s_v, vv)], inc=s_v)
            va2 = p.add("scalar", lambda e: e.activation(g3(3), g3(2), AF.Sqrt),
                        waits=[(s_v, vv)], inc=s_a)
            va3 = p.add("scalar", (lambda e, b=b: e.activation(sil[:, :], gt[b][:, :],
                                                               AF.Silu)),
                        waits=[(s_ld[b], vld)], inc=s_a)
            vv = p.add("vector", (lambda e, b=b: e.tensor_tensor(
                bass.AP(o1, 0, [[D, 128], [64, 16], [1, 64]]),
                bass.AP(ot[b], 0, [[D, 128], [64, 16], [1, 64]]),
                bass.AP(grp, 48, [[64, 128], [1, 16], [0, 64]]), ALU.mult)),
                waits=[(s_a, va2)], inc=s_v)
            vv = p.add("vector", (lambda e, b=b: e.tensor_mul(of[b][:, :], o1[:, :],
                                                              sil[:, :])),
                       waits=[(s_v, vv), (s_a, va3)], inc=s_v)
            p.add("scalar", (lambda e, b=b, i=i: e.dma_start(
                s0_t[ts(i, 128), :], of[b][:, :])), waits=[(s_v, vv)],
                inc=s_st[b], inc_n=16)
        p.add("scalar", lambda e: e.nop(), waits=[w(s_st[0]), w(s_st[1])])
        p.emit()


PAD_S = S + 1


def pad_row(tok):
    b0, s0 = tok // S, tok % S
    return b0 * PAD_S + 1 + s0


def emit_zero_pad_rows(nc, ctx, G, dst_pad, tag):
    s_z = nc._sb.get("sb_z")
    p = Prog(nc)
    for b0 in range(B_CORE):
        p.add("sync", (lambda e, b0=b0: e.dma_start(
            dst_pad[b0 * PAD_S: b0 * PAD_S + 1, :], G["zrow"][0:1, :])),
            inc=s_z, inc_n=16)
    p.add("sync", lambda e: e.nop(), waits=[w(s_z)])
    p.emit()


def emit_ew_F(nc, ctx, G, tag, *, Ktot, src_a, src_g, dst, act=AF.Gelu,
              t_step=2048, a_off=0, g_off=0, ce_f=None, ce_row=0):
    with ExitStack() as pctx:
        sb = lambda *a, **k: pctx.enter_context(nc.sbuf_tensor(*a, **k))
        psum = lambda *a, **k: pctx.enter_context(nc.psum_tensor(*a, **k))
        s_ld = [nc._sb.get(f"sb_l{j}") for j in range(2)]
        s_v = nc._sb.get("sb_v")
        s_a = nc._sb.get("sb_a")
        s_mm = nc._sb.get("sb_mm")
        s_st = [nc._sb.get(f"sb_y{j}") for j in range(2)]
        at = [sb(f"{tag}_at{j}", [128, t_step], BF16) for j in range(2)]
        gt = [sb(f"{tag}_gt{j}", [128, t_step], BF16) for j in range(2)]
        ga = [sb(f"{tag}_ga{j}", [128, t_step], BF16) for j in range(2)]
        ot = [sb(f"{tag}_ot{j}", [128, t_step], BF16) for j in range(2)]
        if ce_f is not None:
            cesb = sb(f"{tag}_ce", [128, t_step], BF16)
            cerow = sb(f"{tag}_cer", [8, t_step], BF16)
            psce = psum(f"{tag}_pce", [128, t_step], F32)
        p = Prog(nc)
        kch = _chunks(Ktot, 128)
        tiles = [(k0, ks, t0) for (t0, _) in _chunks(N, t_step)
                 for (k0, ks) in kch]
        vce = 0
        for i, (k0, ks, t0) in enumerate(tiles):
            b = i % 2
            first_k = (i % len(kch) == 0)
            lw = [(s_st[b], s_st[b].v)] if i >= 2 else []
            if ce_f is not None and first_k:
                vr = p.add("sync", (lambda e, t0=t0: e.dma_start(
                    cerow[0:8, :], ce_f[0:8, t0:t0 + t_step])),
                    waits=[(s_v, s_v.v)], inc=s_ld[0], inc_n=16)
                vm = None
                for j in range(t_step // 512):
                    vm = p.add("tensor", (lambda e, j=j: e.matmul(
                        psce[:, j * 512:(j + 1) * 512],
                        G["sel_moe"][0:8, ce_row * 128:(ce_row + 1) * 128],
                        cerow[0:8, j * 512:(j + 1) * 512], start=True, stop=True)),
                        waits=[(s_ld[0], vr)] if j == 0 else (), inc=s_mm)
                vce = p.add("vector", (lambda e: e.tensor_copy(cesb[:, :], psce[:, :])),
                            waits=[(s_mm, vm)], inc=s_v)
            p.add("sync", (lambda e, b=b, k0=k0, ks=ks, t0=t0: e.dma_start(
                at[b][0:ks, :], src_a[a_off + k0:a_off + k0 + ks, t0:t0 + t_step])),
                waits=lw, inc=s_ld[b], inc_n=16)
            vld = p.add("sync", (lambda e, b=b, k0=k0, ks=ks, t0=t0: e.dma_start(
                gt[b][0:ks, :], src_g[g_off + k0:g_off + k0 + ks, t0:t0 + t_step])),
                inc=s_ld[b], inc_n=16)
            va = p.add("scalar", (lambda e, b=b, ks=ks: e.activation(
                ga[b][0:ks, :], gt[b][0:ks, :], act)),
                waits=[(s_ld[b], vld)], inc=s_a)
            vv = p.add("vector", (lambda e, b=b, ks=ks: e.tensor_mul(
                ot[b][0:ks, :], at[b][0:ks, :], ga[b][0:ks, :])),
                waits=[(s_a, va)] + ([(s_v, vce)] if ce_f is not None else []),
                inc=s_v)
            if ce_f is not None:
                vv = p.add("vector", (lambda e, b=b, ks=ks: e.tensor_mul(
                    ot[b][0:ks, :], ot[b][0:ks, :], cesb[0:ks, :])),
                    waits=[(s_v, vv)], inc=s_v)
            p.add("sync", (lambda e, b=b, k0=k0, ks=ks, t0=t0: e.dma_start(
                dst[k0:k0 + ks, t0:t0 + t_step], ot[b][0:ks, :])),
                waits=[(s_v, vv)], inc=s_st[b], inc_n=16)
        p.add("sync", lambda e: e.nop(), waits=[w(s_st[0]), w(s_st[1])])
        p.emit()


def emit_lffn_rot(nc, ctx, G, TAB, IN, l, qkv_f, qr_t, kr_t, qn_t, kn_t, tag):
    with ExitStack() as pctx:
        sb = lambda *a, **k: pctx.enter_context(nc.sbuf_tensor(*a, **k))
        ps = pctx.enter_context(nc.psum_tensor(f"{tag}_ps", [128, 512], F32))
        s_ld = [nc._sb.get(f"sb_l{j}") for j in range(2)]
        s_v = nc._sb.get("sb_v")
        s_a = nc._sb.get("sb_a")
        s_mm = nc._sb.get("sb_mm")
        s_st = [nc._sb.get(f"sb_s{j}") for j in range(2)]
        qt = [sb(f"{tag}_q{j}", [128, 512], BF16, is_transpose_dest=True) for j in range(2)]
        kt = [sb(f"{tag}_k{j}", [128, 512], BF16, is_transpose_dest=True) for j in range(2)]
        xc_ = [[sb(f"{tag}_xc{b}{j}", [128, 32], F32) for j in range(4)]
               for b in range(2)]
        sq = sb(f"{tag}_sq", [128, 512], F32)
        nrm = sb(f"{tag}_n", [128, 32], F32)
        qsr = sb(f"{tag}_qsr", [1, 64], BF16)
        ksr = sb(f"{tag}_ksr", [1, 64], BF16)
        qsb = sb(f"{tag}_qsb", [128, 64], F32)
        ksb = sb(f"{tag}_ksb", [128, 64], F32)
        t1 = sb(f"{tag}_t1", [128, 512], F32)
        t2 = sb(f"{tag}_t2", [128, 512], F32)
        oq = [sb(f"{tag}_oq{j}", [128, 512], BF16) for j in range(2)]
        okk = [sb(f"{tag}_okk{j}", [128, 512], BF16) for j in range(2)]
        nq = [sb(f"{tag}_nq{j}", [128, 8], F32) for j in range(2)]
        nk = [sb(f"{tag}_nk{j}", [128, 8], F32) for j in range(2)]

        def v3(t, off=0, inner=64):
            return bass.AP(t, off, [[512, 128], [64, 8], [1, inner]])

        def v3h(t, off):
            return bass.AP(t, off, [[512, 128], [64, 8], [1, 32]])

        def tabb(t):
            return bass.AP(t, 0, [[32, 128], [0, 8], [1, 32]])

        s_qs = nc._sb.get("sb_qs")
        p = Prog(nc)
        vq = p.add("gpsimd", (lambda e: e.dma_start(
            qsr[0:1, :], IN["L0_ffn_q_scale"][l:l + 1, :])), inc=s_qs, inc_n=16)
        vq = p.add("gpsimd", (lambda e: e.dma_start(
            ksr[0:1, :], IN["L0_ffn_k_scale"][l:l + 1, :])), inc=s_qs, inc_n=16)
        vmm = p.add("tensor", (lambda e: e.matmul(
            ps[:, 0:64], G["bc_all"][0:1, :], qsr[0:1, :], start=True, stop=True)),
            waits=[(s_qs, s_qs.v)], inc=s_mm)
        vmm = p.add("tensor", (lambda e: e.matmul(
            ps[:, 64:128], G["bc_all"][0:1, :], ksr[0:1, :], start=True, stop=True)),
            inc=s_mm)
        vv0 = p.add("vector", lambda e: e.tensor_copy(qsb[:, :], ps[:, 0:64]),
                    waits=[(s_mm, vmm)], inc=s_v)
        vv0 = p.add("vector", lambda e: e.tensor_copy(ksb[:, :], ps[:, 64:128]),
                    inc=s_v)

        for i in range(NT):
            b = i % 2
            srow = (i % (S // 128)) * 128
            lw = [(s_st[b], s_st[b].v)] if i >= 2 else []
            p.add("sync", (lambda e, b=b, i=i: e.dma_start(
                qt[b][:, :], qkv_f[0:512, i * 128:(i + 1) * 128], transpose=True)),
                waits=lw, inc=s_ld[b], inc_n=16)
            vld = p.add("sync", (lambda e, b=b, i=i: e.dma_start(
                kt[b][:, :], qkv_f[512:1024, i * 128:(i + 1) * 128], transpose=True)),
                inc=s_ld[b], inc_n=16)
            for j, tnm in enumerate(["xc_q", "xs_q", "xc_k", "xs_k"]):
                vld = p.add("sync", (lambda e, b=b, j=j, tnm=tnm, srow=srow: e.dma_start(
                    xc_[b][j][:, :], TAB[tnm][srow:srow + 128, :])), inc=s_ld[b], inc_n=16)
            for src_t, scb, ci, si, dst_t, nrm_t in [
                    (qt, qsb, 0, 1, oq, nq), (kt, ksb, 2, 3, okk, nk)]:
                st, dt, nt = src_t[b], dst_t[b], nrm_t[b]
                va = p.add("scalar", (lambda e, st=st: e.activation(
                    sq[:, :], st[:, :], AF.Square)),
                    waits=[(s_ld[b], vld), (s_v, s_v.v)], inc=s_a)
                vv = p.add("vector", (lambda e: e.tensor_reduce(
                    nrm[:, 0:8], v3(sq), AX.X, ALU.add)),
                    waits=[(s_a, va)], inc=s_v)
                va = p.add("scalar", (lambda e: e.activation(
                    nrm[:, 8:16], nrm[:, 0:8], AF.Sqrt)),
                    waits=[(s_v, vv)], inc=s_a)
                vv = p.add("vector", (lambda e: e.tensor_scalar(
                    nrm[:, 16:24], nrm[:, 8:16], 1e-12, None, ALU.max)),
                    waits=[(s_a, va)], inc=s_v)
                vv = p.add("vector", (lambda e: e.reciprocal(
                    nrm[:, 24:32], nrm[:, 16:24])), waits=[(s_v, vv)], inc=s_v)
                vv = p.add("vector", (lambda e, st=st: e.tensor_tensor(
                    v3(t1), v3(st), bass.AP(nrm, 24, [[32, 128], [1, 8], [0, 64]]),
                    ALU.mult)), waits=[(s_v, vv)], inc=s_v)
                vv = p.add("vector", (lambda e, scb=scb: e.tensor_tensor(
                    v3(t1), v3(t1), bass.AP(scb, 0, [[64, 128], [0, 8], [1, 64]]),
                    ALU.mult)), waits=[(s_v, vv)], inc=s_v)
                va = p.add("scalar", (lambda e: e.activation(
                    sq[:, :], t1[:, :], AF.Square)), waits=[(s_v, vv)], inc=s_a)
                vvn = p.add("vector", (lambda e: e.tensor_reduce(
                    nrm[:, 0:8], v3(sq), AX.X, ALU.add)),
                    waits=[(s_a, va)], inc=s_v)
                va = p.add("scalar", (lambda e, nt=nt: e.activation(
                    nt[:, :], nrm[:, 0:8], AF.Sqrt)), waits=[(s_v, vvn)], inc=s_a)
                vv = p.add("vector", (lambda e, b=b, ci=ci: e.tensor_tensor(
                    v3h(t2, 0), v3h(t1, 0), tabb(xc_[b][ci]), ALU.mult)),
                    waits=[(s_v, vv)], inc=s_v)
                vv = p.add("vector", (lambda e, b=b, ci=ci: e.tensor_tensor(
                    v3h(t2, 32), v3h(t1, 32), tabb(xc_[b][ci]), ALU.mult)),
                    waits=[(s_v, vv)], inc=s_v)
                vv = p.add("vector", (lambda e, b=b, si=si, dt=dt: e.tensor_tensor(
                    v3h(dt, 0), v3h(t1, 32), tabb(xc_[b][si]), ALU.mult)),
                    waits=[(s_v, vv)], inc=s_v)
                vv = p.add("vector", (lambda e, dt=dt: e.tensor_tensor(
                    v3h(dt, 0), v3h(t2, 0), v3h(dt, 0), ALU.subtract)),
                    waits=[(s_v, vv)], inc=s_v)
                vv = p.add("vector", (lambda e, b=b, si=si, dt=dt: e.tensor_tensor(
                    v3h(dt, 32), v3h(t1, 0), tabb(xc_[b][si]), ALU.mult)),
                    waits=[(s_v, vv)], inc=s_v)
                vv = p.add("vector", (lambda e, dt=dt: e.tensor_tensor(
                    v3h(dt, 32), v3h(t2, 32), v3h(dt, 32), ALU.add)),
                    waits=[(s_v, vv)], inc=s_v)
            vdone_v, vdone_a = s_v.v, s_a.v
            for dst_hbm, src_sb, wv in [(qr_t, oq[b], (s_v, vdone_v)),
                                        (kr_t, okk[b], (s_v, vdone_v))]:
                p.add("scalar", (lambda e, dst_hbm=dst_hbm, src_sb=src_sb, i=i:
                                 e.dma_start(dst_hbm[ts(i, 128), :], src_sb[:, :])),
                      waits=[wv], inc=s_st[b], inc_n=16)
            for dst_hbm, src_sb in [(qn_t, nq[b]), (kn_t, nk[b])]:
                p.add("scalar", (lambda e, dst_hbm=dst_hbm, src_sb=src_sb, i=i:
                                 e.dma_start(dst_hbm[ts(i, 128), :], src_sb[:, :])),
                      waits=[(s_a, vdone_a)], inc=s_st[b], inc_n=16)
        p.add("scalar", lambda e: e.nop(), waits=[w(s_st[0]), w(s_st[1])])
        p.emit()


def emit_lattn(nc, ctx, G, qr_t, kr_t, qkv_f, qn_t, kn_t, ot_f, tag):
    with ExitStack() as pctx:
        sb = lambda *a, **k: pctx.enter_context(nc.sbuf_tensor(*a, **k))
        psum = lambda *a, **k: pctx.enter_context(nc.psum_tensor(*a, **k))
        qr = [sb(f"{tag}_qr{j}", [128, S], BF16, is_transpose_dest=True) for j in range(4)]
        kr = [sb(f"{tag}_kr{j}", [128, S], BF16, is_transpose_dest=True) for j in range(4)]
        vt = [sb(f"{tag}_vt{j}", [128, 512], BF16, is_transpose_dest=True) for j in range(16)]
        nqt = [sb(f"{tag}_nq{j}", [128, 8], F32) for j in range(16)]
        nkt = [sb(f"{tag}_nk{j}", [128, 8], F32) for j in range(16)]
        qnf = sb(f"{tag}_qnf", [8, S], F32)
        knf = sb(f"{tag}_knf", [8, S], F32)
        wrk = sb(f"{tag}_wrk", [8, 16], F32)
        cbt = sb(f"{tag}_cbt", [8, 4], BF16)
        cb = sb(f"{tag}_cb", [128, 32], F32)
        cbp = sb(f"{tag}_cbp", [128, 32], F32)
        pp2 = [sb(f"{tag}_pq{j}", [128, 512], BF16) for j in range(2)]
        pp = [sb(f"{tag}_pp{j}", [128, 512], BF16) for j in range(2)]
        dsb = sb(f"{tag}_d", [1, 512], F32)
        rcp = [sb(f"{tag}_rc{j}", [1, 512], BF16) for j in range(2)]
        rbc = sb(f"{tag}_rbc", [128, 512], BF16)
        oev = [sb(f"{tag}_o{j}", [128, 512], BF16) for j in range(2)]
        psS = [psum(f"{tag}_pS{j}", [128, 512], F32) for j in range(2)]
        psAb = [psum(f"{tag}_pA{j}", [1, 512], F32) for j in range(2)]
        psO = [psum(f"{tag}_pO{j}", [128, 512], F32) for j in range(2)]
        psR = psum(f"{tag}_pR", [128, 512], F32)
        psTC = psum(f"{tag}_pT", [128, 512], F32)

        s_ld = nc._sb.get("sb_ld")
        s_mm = nc._sb.get("sb_mm")
        s_v = nc._sb.get("sb_v")
        s_a = nc._sb.get("sb_a")
        s_st = [nc._sb.get(f"sb_t{j}") for j in range(2)]

        p = Prog(nc)
        for b in range(B_CORE):
            tb0 = b * S
            ldw = [w(s_mm), w(s_v)] if b > 0 else []
            for j in range(4):
                p.add("sync", (lambda e, j=j, tb0=tb0: e.dma_start(
                    qr[j][:, :], qr_t[tb0:tb0 + S, j * 128:(j + 1) * 128],
                    transpose=True)), waits=ldw if j == 0 else (), inc=s_ld, inc_n=16)
                p.add("sync", (lambda e, j=j, tb0=tb0: e.dma_start(
                    kr[j][:, :], kr_t[tb0:tb0 + S, j * 128:(j + 1) * 128],
                    transpose=True)), inc=s_ld, inc_n=16)
            for j in range(16):
                p.add("sync", (lambda e, j=j, tb0=tb0: e.dma_start(
                    vt[j][:, :], qkv_f[1024:1536, tb0 + j * 128: tb0 + (j + 1) * 128],
                    transpose=True)), inc=s_ld, inc_n=16)
                p.add("sync", (lambda e, j=j, tb0=tb0: e.dma_start(
                    nqt[j][:, :], qn_t[tb0 + j * 128: tb0 + (j + 1) * 128, :])),
                    inc=s_ld, inc_n=16)
                p.add("sync", (lambda e, j=j, tb0=tb0: e.dma_start(
                    nkt[j][:, :], kn_t[tb0 + j * 128: tb0 + (j + 1) * 128, :])),
                    inc=s_ld, inc_n=16)
            vld = s_ld.v
            for j in range(16):
                for src_sb, dst_sb in [(nqt[j], qnf), (nkt[j], knf)]:
                    vmm = p.add("tensor", (lambda e, src_sb=src_sb: e.transpose(
                        psTC[0:8, 0:128], src_sb[:, :], G["ident_f32"][:, :])),
                        waits=[(s_ld, vld), (s_v, s_v.v)], inc=s_mm)
                    p.add("vector", (lambda e, dst_sb=dst_sb, j=j: e.tensor_copy(
                        dst_sb[0:8, j * 128:(j + 1) * 128], psTC[0:8, 0:128])),
                          waits=[(s_mm, vmm)], inc=s_v)
            vv = p.add("vector", (lambda e: e.tensor_reduce(
                wrk[0:8, 0:4], bass.AP(qnf, 0, [[S, 8], [512, 4], [1, 512]]),
                AX.X, ALU.max)), waits=[(s_v, s_v.v)], inc=s_v)
            vv = p.add("vector", (lambda e: e.tensor_reduce(
                wrk[0:8, 4:8], bass.AP(knf, 0, [[S, 8], [512, 4], [1, 512]]),
                AX.X, ALU.max)), waits=[(s_v, vv)], inc=s_v)
            vv = p.add("vector", lambda e: e.tensor_copy(wrk[0:8, 8:9], wrk[0:8, 4:5]),
                       waits=[(s_v, vv)], inc=s_v)
            vv = p.add("vector", lambda e: e.tensor_tensor(
                wrk[0:8, 9:12], wrk[0:8, 4:7], wrk[0:8, 5:8], ALU.max),
                waits=[(s_v, vv)], inc=s_v)
            vv = p.add("vector", lambda e: e.tensor_tensor(
                wrk[0:8, 12:16], wrk[0:8, 0:4], wrk[0:8, 8:12], ALU.mult),
                waits=[(s_v, vv)], inc=s_v)
            vv = p.add("vector", lambda e: e.tensor_scalar(
                cbt[0:8, 0:4], wrk[0:8, 12:16], 8.0, None, ALU.mult),
                waits=[(s_v, vv)], inc=s_v)
            vmm = None
            for h in range(8):
                for ww in range(4):
                    j = h * 4 + ww
                    vmm = p.add("tensor", (lambda e, h=h, ww=ww, j=j: e.matmul(
                        psTC[:, 128 + j:129 + j], G["sel_moe"][0:8, h * 128:(h + 1) * 128],
                        cbt[0:8, ww:ww + 1], start=(j == 0), stop=(j == 31))),
                        waits=[(s_v, vv)] if j == 0 else (), inc=s_mm)
            vv = p.add("vector", lambda e: e.tensor_scalar(
                cb[:, :], psTC[:, 128:160], -1.0, None, ALU.mult),
                waits=[(s_mm, vmm)], inc=s_v)
            vv = p.add("vector", lambda e: e.tensor_scalar(
                cbp[:, :], psTC[:, 128:160], 0.125, None, ALU.mult),
                waits=[(s_v, vv)], inc=s_v)
            vcb = vv
            for hp in range(4):
                for ww in range(4):
                    kcs = list(range(4, 8)) if ww == 0 else list(range(8))
                    for kc in kcs:
                        kcol = 512 * (ww - 1) + 128 * kc
                        sc = kc % 2
                        mmw = [(s_ld, vld), (s_v, s_v.v)]
                        p.add("tensor", (lambda e, hp=hp, kcol=kcol, ww=ww, sc=sc:
                                         e.matmul(psS[sc][:, :],
                                                  kr[hp][0:64, kcol:kcol + 128],
                                                  qr[hp][0:64, ww * 512:(ww + 1) * 512],
                                                  start=True, stop=True,
                                                  tile_position=(0, 0))),
                              waits=mmw, inc=s_mm)
                        vmm = p.add("tensor", (lambda e, hp=hp, kcol=kcol, ww=ww, sc=sc:
                                               e.matmul(psS[sc ^ 1][:, :],
                                                        kr[hp][64:128, kcol:kcol + 128],
                                                        qr[hp][64:128, ww * 512:(ww + 1) * 512],
                                                        start=True, stop=True,
                                                        tile_position=(64, 0))),
                                    inc=s_mm)
                        mtile = G[f"lamu_{kc}"] if kc < 4 else G[f"trilq_{kc - 4}"]
                        for hh in range(2):
                            h = hp * 2 + hh
                            src_ps = psS[sc] if hh == 0 else psS[sc ^ 1]
                            vv = p.add("vector", (lambda e, hh=hh, src_ps=src_ps,
                                                  h=h, ww=ww:
                                                  e.tensor_scalar(pp2[hh][:, :],
                                                                  src_ps[:, :],
                                                                  cbp[:, h * 4 + ww: h * 4 + ww + 1],
                                                                  None, ALU.min)),
                                       waits=[(s_mm, vmm), (s_v, vcb)], inc=s_v)
                            va = p.add("scalar", (lambda e, hh=hh, h=h, ww=ww:
                                                  e.activation(pp[hh][:, :], pp2[hh][:, :],
                                                               AF.Exp, scale=8.0,
                                                               bias=cb[:, h * 4 + ww: h * 4 + ww + 1])),
                                       waits=[(s_v, vv)], inc=s_a)
                            vv = p.add("vector", (lambda e, hh=hh, mtile=mtile:
                                                  e.tensor_mul(pp[hh][:, :], pp[hh][:, :],
                                                               mtile[:, :])),
                                       waits=[(s_a, va)], inc=s_v)
                            p.add("tensor", (lambda e, hh=hh, first=(kc == kcs[0]),
                                             last=(kc == kcs[-1]):
                                             e.matmul(psAb[hh][:, :], G["e0col"][:, 0:1],
                                                      pp[hh][:, :], start=first, stop=last)),
                                  waits=[(s_v, vv)], inc=s_mm)
                            p.add("tensor", (lambda e, hh=hh, h=h, ww=ww, kc=kc,
                                             first=(kc == kcs[0]), last=(kc == kcs[-1]):
                                             e.matmul(psO[hh][hh * 64:(hh + 1) * 64, :],
                                                      vt[(512 * (ww - 1) + 128 * kc) // 128]
                                                      [:, h * 64:(h + 1) * 64],
                                                      pp[hh][:, :], start=first, stop=last,
                                                      tile_position=(0, 64 * hh))),
                                  inc=s_mm)
                    vmm_all = s_mm.v
                    for hh in range(2):
                        vv = p.add("vector", (lambda e, hh=hh: e.tensor_scalar(
                            dsb[0:1, :], psAb[hh][0:1, :], 1e-30, None, ALU.max)),
                            waits=[(s_mm, vmm_all)], inc=s_v)
                        vv = p.add("vector", lambda e: e.reciprocal(dsb[0:1, :], dsb[0:1, :]),
                                   waits=[(s_v, vv)], inc=s_v)
                        vv = p.add("vector", (lambda e, hh=hh: e.tensor_copy(
                            rcp[hh][0:1, :], dsb[0:1, :])), waits=[(s_v, vv)], inc=s_v)
                    vrc = s_v.v
                    vbc = p.add("tensor", (lambda e: e.matmul(
                        psR[:, :], G["bc_lo"][0:1, :], rcp[0][0:1, :], start=True,
                        stop=False)), waits=[(s_v, vrc)], inc=s_mm)
                    vbc = p.add("tensor", (lambda e: e.matmul(
                        psR[:, :], G["bc_hi"][0:1, :], rcp[1][0:1, :], start=False,
                        stop=True)), inc=s_mm)
                    vv = p.add("vector", lambda e: e.tensor_copy(rbc[:, :], psR[:, :]),
                               waits=[(s_mm, vbc)], inc=s_v)
                    e2 = (hp * 4 + ww) % 2
                    stw = [(s_st[e2], s_st[e2].v)]
                    vo = p.add("vector", (lambda e, e2=e2: e.tensor_tensor(
                        oev[e2][0:64, :], psO[0][0:64, :], rbc[0:64, :], ALU.mult)),
                        waits=[(s_v, vv)] + stw, inc=s_v)
                    vo = p.add("vector", (lambda e, e2=e2: e.tensor_tensor(
                        oev[e2][64:128, :], psO[1][64:128, :], rbc[64:128, :],
                        ALU.mult)), waits=[(s_v, vo)], inc=s_v)
                    p.add("sync", (lambda e, e2=e2, hp=hp, ww=ww, tb0=tb0: e.dma_start(
                        ot_f[hp * 128:(hp + 1) * 128, tb0 + ww * 512: tb0 + (ww + 1) * 512],
                        oev[e2][:, :])), waits=[(s_v, vo)], inc=s_st[e2], inc_n=16)
        p.add("sync", lambda e: e.nop(), waits=[w(s_st[0]), w(s_st[1])])
        p.emit()


def emit_combine(nc, ctx, G, tag, *, srcs, dst_f32=None, dst_pad=None,
                 dst_bf=None, mode="resid"):
    with ExitStack() as pctx:
        sb = lambda *a, **k: pctx.enter_context(nc.sbuf_tensor(*a, **k))
        s_ld = [nc._sb.get(f"sb_l{j}") for j in range(2)]
        s_v = nc._sb.get("sb_v")
        s_a = nc._sb.get("sb_a")
        s_st = [nc._sb.get(f"sb_s{j}") for j in range(2)]
        at = [sb(f"{tag}_a{j}", [128, D], F32) for j in range(2)]
        bt = [sb(f"{tag}_b{j}", [128, D], BF16, is_transpose_dest=True) for j in range(2)]
        gt = [sb(f"{tag}_g{j}", [128, D], F32) for j in range(2)]
        sil2 = [sb(f"{tag}_sil{j}", [128, D], F32) for j in range(2)]
        st2 = [sb(f"{tag}_si{j}", [128, D], F32) for j in range(2)]
        of = [sb(f"{tag}_of{j}", [128, D], F32) for j in range(2)]
        ob = [sb(f"{tag}_ob{j}", [128, D], BF16) for j in range(2)]
        nst = (dst_f32 is not None) + (dst_pad is not None) + (dst_bf is not None)
        p = Prog(nc)
        for i in range(NT):
            b = i % 2
            lw = [(s_st[b], s_st[b].v)] if i >= 2 else []
            p.add("sync", (lambda e, b=b, i=i: e.dma_start(
                at[b][:, :], srcs["a"][ts(i, 128), :])), waits=lw, inc=s_ld[b], inc_n=16)
            if mode == "resid":
                vld = p.add("sync", (lambda e, b=b, i=i: e.dma_start(
                    bt[b][:, :], srcs["b"][0:D, i * 128:(i + 1) * 128],
                    transpose=True)), inc=s_ld[b], inc_n=16)
                vv = p.add("vector", (lambda e, b=b: e.tensor_add(
                    of[b][:, :], at[b][:, :], bt[b][:, :])),
                    waits=[(s_ld[b], vld)], inc=s_v)
            else:
                if "g_f" in srcs:
                    p.add("sync", (lambda e, b=b, i=i: e.dma_start(
                        bt[b][:, :], srcs["g_f"][0:D, i * 128:(i + 1) * 128],
                        transpose=True)), inc=s_ld[b], inc_n=16)
                else:
                    p.add("sync", (lambda e, b=b, i=i: e.dma_start(
                        gt[b][:, :], srcs["g"][ts(i, 128), :])), inc=s_ld[b], inc_n=16)
                vld = p.add("sync", (lambda e, b=b, i=i: e.dma_start(
                    st2[b][:, :], srcs["s"][ts(i, 128), :])), inc=s_ld[b], inc_n=16)
                va = p.add("scalar", (lambda e, b=b, gf=("g_f" in srcs): e.activation(
                    sil2[b][:, :], bt[b][:, :] if gf else gt[b][:, :], AF.Silu)),
                    waits=[(s_ld[b], vld), w(s_v)], inc=s_a)
                vv = p.add("vector", (lambda e, b=b: e.tensor_mul(
                    sil2[b][:, :], sil2[b][:, :], st2[b][:, :])), waits=[(s_a, va)], inc=s_v)
                vv = p.add("vector", (lambda e, b=b: e.tensor_add(
                    of[b][:, :], at[b][:, :], sil2[b][:, :])), waits=[(s_v, vv)], inc=s_v)
            if dst_pad is not None or dst_bf is not None:
                vv2 = p.add("vector", (lambda e, b=b: e.tensor_copy(
                    ob[b][:, :], of[b][:, :])), waits=[(s_v, vv)], inc=s_v)
            else:
                vv2 = vv
            if dst_f32 is not None:
                p.add("scalar", (lambda e, b=b, i=i: e.dma_start(
                    dst_f32[ts(i, 128), :], of[b][:, :])), waits=[(s_v, vv2)],
                    inc=s_st[b], inc_n=16)
            if dst_pad is not None:
                pr = pad_row(i * 128)
                p.add("scalar", (lambda e, b=b, pr=pr: e.dma_start(
                    dst_pad[pr:pr + 128, :], ob[b][:, :])),
                    waits=[(s_v, vv2)] if dst_f32 is None else (),
                    inc=s_st[b], inc_n=16)
            if dst_bf is not None:
                p.add("scalar", (lambda e, b=b, i=i: e.dma_start(
                    dst_bf[ts(i, 128), :], ob[b][:, :])),
                    waits=[(s_v, vv2)] if (dst_f32 is None and dst_pad is None) else (),
                    inc=s_st[b], inc_n=16)
        p.add("scalar", lambda e: e.nop(), waits=[w(s_st[0]), w(s_st[1])])
        p.emit()


NCH = S // 128


def emit_gla_prep(nc, ctx, G, q_f, k_f, z_f, qa_t, kd_t, kd2_t, ce63r, ce127r, tag):
    with ExitStack() as pctx:
        sb = lambda *a, **k: pctx.enter_context(nc.sbuf_tensor(*a, **k))
        psum = lambda *a, **k: pctx.enter_context(nc.psum_tensor(*a, **k))
        s_ld = [nc._sb.get(f"sb_l{j}") for j in range(2)]
        s_v = nc._sb.get("sb_v")
        s_a = nc._sb.get("sb_a")
        s_mm = nc._sb.get("sb_mm")
        s_st = [nc._sb.get(f"sb_s{j}") for j in range(2)]
        s_row = [nc._sb.get(f"sb_r{j}") for j in range(2)]
        qt = [sb(f"{tag}_q{j}", [128, D], BF16, is_transpose_dest=True) for j in range(2)]
        kt = [sb(f"{tag}_k{j}", [128, D], BF16, is_transpose_dest=True) for j in range(2)]
        zt = [sb(f"{tag}_z{j}", [128, D], BF16, is_transpose_dest=True) for j in range(2)]
        sq = sb(f"{tag}_sq", [128, D], F32)
        sig = sb(f"{tag}_sig", [128, D], F32)
        cum = sb(f"{tag}_cum", [128, D], F32)
        e63 = sb(f"{tag}_e63", [128, D], F32)
        e63i = sb(f"{tag}_e63i", [128, D], F32)
        e127 = sb(f"{tag}_e127", [128, D], F32)
        rowb = [sb(f"{tag}_rowb{j}", [1, D], BF16) for j in range(2)]
        nrm = sb(f"{tag}_n", [128, 64], F32)
        oqa = [sb(f"{tag}_oqa{j}", [128, D], BF16) for j in range(2)]
        okd = [sb(f"{tag}_okd{j}", [128, D], BF16) for j in range(2)]
        okd2 = [sb(f"{tag}_okd2{j}", [128, D], BF16) for j in range(2)]
        psC = psum(f"{tag}_pc", [128, 1024], F32)
        psB = psum(f"{tag}_pb", [128, 1024], F32)

        def v3(t):
            return bass.AP(t, 0, [[D, 128], [64, 16], [1, 64]])

        p = Prog(nc)
        for i in range(NT):
            b = i % 2
            bi, ch = i // NCH, i % NCH
            lw = [(s_st[b], s_st[b].v)] if i >= 2 else []
            p.add("sync", (lambda e, b=b, i=i: e.dma_start(
                qt[b][:, :], q_f[0:D, i * 128:(i + 1) * 128], transpose=True)),
                waits=lw, inc=s_ld[b], inc_n=16)
            p.add("sync", (lambda e, b=b, i=i: e.dma_start(
                kt[b][:, :], k_f[0:D, i * 128:(i + 1) * 128], transpose=True)),
                inc=s_ld[b], inc_n=16)
            vld = p.add("sync", (lambda e, b=b, i=i: e.dma_start(
                zt[b][:, :], z_f[0:D, i * 128:(i + 1) * 128], transpose=True)),
                inc=s_ld[b], inc_n=16)
            va = p.add("scalar", (lambda e, b=b: e.activation(
                sig[:, :], zt[b][:, :], AF.Sigmoid)),
                waits=[(s_ld[b], vld), (s_v, s_v.v)], inc=s_a)
            va = p.add("scalar", (lambda e: e.activation(
                sig[:, :], sig[:, :], AF.Ln)), waits=[(s_a, va)], inc=s_a)
            vmm = p.add("tensor", (lambda e: e.matmul(
                psC[:, 0:512], G["ucum"][:, :], sig[:, 0:512], start=True, stop=True)),
                waits=[(s_a, va)], inc=s_mm)
            vmm = p.add("tensor", (lambda e: e.matmul(
                psC[:, 512:1024], G["ucum"][:, :], sig[:, 512:1024], start=True,
                stop=True)), inc=s_mm)
            vv = p.add("vector", lambda e: e.tensor_copy(cum[:, :], psC[:, :]),
                       waits=[(s_mm, vmm)], inc=s_v)
            vmm = p.add("tensor", (lambda e: e.matmul(
                psB[:, 0:512], G["sel63"][:, :], cum[:, 0:512], start=True, stop=True)),
                waits=[(s_v, vv)], inc=s_mm)
            vmm = p.add("tensor", (lambda e: e.matmul(
                psB[:, 512:1024], G["sel63"][:, :], cum[:, 512:1024], start=True,
                stop=True)), inc=s_mm)
            vv = p.add("vector", (lambda e: e.tensor_sub(e63[:, :], psB[:, :], cum[:, :])),
                       waits=[(s_mm, vmm)], inc=s_v)
            va = p.add("scalar", (lambda e: e.activation(e63i[:, :], e63[:, :],
                                                         AF.Exp, scale=-1.0)),
                       waits=[(s_v, vv)], inc=s_a)
            va = p.add("scalar", (lambda e: e.activation(e63[:, :], e63[:, :], AF.Exp)),
                       waits=[(s_a, va)], inc=s_a)
            va2 = p.add("scalar", (lambda e: e.activation(sq[:, :], psB[:, :],
                                                          AF.Exp)),
                        waits=[(s_a, va)], inc=s_a)
            vv = p.add("vector", (lambda e, b=b: e.tensor_copy(rowb[b][0:1, :], sq[0:1, :])),
                       waits=[(s_a, va2), (s_row[b], s_row[b].v)], inc=s_v)
            vrow = p.add("scalar", (lambda e, b=b, bi=bi, ch=ch: e.dma_start(
                ce63r[bi * NCH + ch: bi * NCH + ch + 1, :], rowb[b][0:1, :])),
                waits=[(s_v, vv)], inc=s_row[b], inc_n=16)
            vmm = p.add("tensor", (lambda e: e.matmul(
                psB[:, 0:512], G["sel127"][:, :], cum[:, 0:512], start=True,
                stop=True)), waits=[(s_v, vv), (s_a, va2)], inc=s_mm)
            vmm = p.add("tensor", (lambda e: e.matmul(
                psB[:, 512:1024], G["sel127"][:, :], cum[:, 512:1024], start=True,
                stop=True)), inc=s_mm)
            vv = p.add("vector", (lambda e: e.tensor_sub(e127[:, :], psB[:, :], cum[:, :])),
                       waits=[(s_mm, vmm)], inc=s_v)
            va3 = p.add("scalar", (lambda e: e.activation(e127[:, :], e127[:, :], AF.Exp)),
                        waits=[(s_v, vv)], inc=s_a)
            va3b = p.add("scalar", (lambda e: e.activation(sq[:, :], psB[:, :],
                                                           AF.Exp)),
                         waits=[(s_a, va3)], inc=s_a)
            vv = p.add("vector", (lambda e, b=b: e.tensor_copy(rowb[b][0:1, :], sq[0:1, :])),
                       waits=[(s_a, va3b), (s_row[b], s_row[b].v)], inc=s_v)
            p.add("scalar", (lambda e, b=b, bi=bi, ch=ch: e.dma_start(
                ce127r[bi * NCH + ch: bi * NCH + ch + 1, :], rowb[b][0:1, :])),
                waits=[(s_v, vv)], inc=s_row[b], inc_n=16)
            for src_t, dst_list in [(qt, [("qa", oqa, e63i)]),
                                    (kt, [("kd", okd, e63), ("kd2", okd2, e127)])]:
                st = src_t[b]
                va4 = p.add("scalar", (lambda e, st=st: e.activation(
                    sq[:, :], st[:, :], AF.Square)), waits=[(s_v, s_v.v)], inc=s_a)
                vv = p.add("vector", (lambda e: e.tensor_reduce(
                    nrm[:, 0:16], bass.AP(sq, 0, [[D, 128], [64, 16], [1, 64]]),
                    AX.X, ALU.add)), waits=[(s_a, va4)], inc=s_v)
                va4 = p.add("scalar", (lambda e: e.activation(
                    nrm[:, 16:32], nrm[:, 0:16], AF.Sqrt)), waits=[(s_v, vv)], inc=s_a)
                vv = p.add("vector", (lambda e: e.tensor_scalar(
                    nrm[:, 32:48], nrm[:, 16:32], 1e-12, None, ALU.max)),
                    waits=[(s_a, va4)], inc=s_v)
                vv = p.add("vector", (lambda e: e.reciprocal(
                    nrm[:, 48:64], nrm[:, 32:48])), waits=[(s_v, vv)], inc=s_v)
                vv = p.add("vector", (lambda e, st=st: e.tensor_tensor(
                    v3(sq), v3(st), bass.AP(nrm, 48, [[64, 128], [1, 16], [0, 64]]),
                    ALU.mult)), waits=[(s_v, vv)], inc=s_v)
                for _, dst_t, fac in dst_list:
                    vv = p.add("vector", (lambda e, dst_t=dst_t, fac=fac, b=b:
                                          e.tensor_mul(dst_t[b][:, :], sq[:, :],
                                                       fac[:, :])),
                               waits=[(s_v, vv)], inc=s_v)
            vdone = s_v.v
            for dst_hbm, src_list in [(qa_t, oqa), (kd_t, okd), (kd2_t, okd2)]:
                p.add("scalar", (lambda e, dst_hbm=dst_hbm, src_list=src_list,
                                 b=b, i=i: e.dma_start(
                    dst_hbm[ts(i, 128), :], src_list[b][:, :])),
                    waits=[(s_v, vdone)], inc=s_st[b], inc_n=16)
        p.add("scalar", lambda e: e.nop(),
              waits=[w(s_st[0]), w(s_st[1]), w(s_row[0]), w(s_row[1])])
        p.emit()


def emit_gla_scan(nc, ctx, G, qa_t, kd_t, kd2_t, v_f, ce63r, ce127r, og_f, tag):
    import os as _os
    CUT = set((_os.environ.get("GS_CUT") or "").split(","))
    SPACE = "space" in CUT
    MAXC = int(_os.environ.get("GS_MAXC") or NCH)
    MAXHP = int(_os.environ.get("GS_MAXHP") or 8)
    MAXB = int(_os.environ.get("GS_MAXB") or B_CORE)
    with ExitStack() as pctx:
        sb = lambda *a, **k: pctx.enter_context(nc.sbuf_tensor(*a, **k))
        psum = lambda *a, **k: pctx.enter_context(nc.psum_tensor(*a, **k))
        s_ld = nc._sb.get("sb_ld")
        s_ld2 = [nc._sb.get(f"sb_l{j}") for j in range(2)]
        s_v = nc._sb.get("sb_v")
        s_mm = nc._sb.get("sb_mm")
        s_st = [nc._sb.get(f"sb_y{j}") for j in range(2)]
        c63 = [sb(f"{tag}_c63_{hp}", [128, NCH], F32) for hp in range(8)]
        c127 = [sb(f"{tag}_c127_{hp}", [128, NCH], F32) for hp in range(8)]
        c63b = [sb(f"{tag}_c63b_{hp}", [128, NCH], BF16, is_transpose_dest=True)
                for hp in range(8)]
        c127b = [sb(f"{tag}_c127b_{hp}", [128, NCH], BF16, is_transpose_dest=True)
                 for hp in range(8)]
        qa = [sb(f"{tag}_qa{j}", [128, 512], BF16, is_transpose_dest=True)
              for j in range(16)]
        kd = [sb(f"{tag}_kd{j}", [128, 512], BF16, is_transpose_dest=True)
              for j in range(16)]
        kd2 = [sb(f"{tag}_k2{j}", [128, D], BF16) for j in range(2)]
        vt = [sb(f"{tag}_vt{j}", [128, D], BF16, is_transpose_dest=True) for j in range(2)]
        Sf = [sb(f"{tag}_Sf{hp}", [128, 64], F32) for hp in range(8)]
        Sb = [sb(f"{tag}_Sb{hp}", [128, 64], BF16) for hp in range(8)]
        Spp = sb(f"{tag}_Spp", [128, 64], F32)
        Sppb = sb(f"{tag}_Sppb", [128, 64], BF16)
        smx = [sb(f"{tag}_sm{j}", [128, 128], BF16) for j in range(2)]
        oev = [sb(f"{tag}_oev{j}", [128, 128], BF16) for j in range(2)]
        psS = [psum(f"{tag}_pS{j}", [128, 512], F32) for j in range(2)]
        psO = [[psum(f"{tag}_pO{j}{h}", [128, 512], F32) for h in range(2)]
               for j in range(2)]
        psA1 = [psum(f"{tag}_pA{h}", [128, 512], F32) for h in range(2)]
        psA = [psA1, psA1]

        p = Prog(nc, space=SPACE)
        for b in range(MAXB):
            tb0 = b * S
            ldw = [w(s_mm), w(s_v)] if b > 0 else []
            if "noce" not in CUT:
                for hp in range(8):
                    v1 = p.add("sync", (lambda e, hp=hp, b=b: e.dma_start(
                        c63b[hp][:, :], ce63r[b * NCH:(b + 1) * NCH,
                                              hp * 128:(hp + 1) * 128], transpose=True)),
                        waits=ldw if hp == 0 else (), inc=s_ld2[0], inc_n=16)
                    v1 = p.add("sync", (lambda e, hp=hp, b=b: e.dma_start(
                        c127b[hp][:, :], ce127r[b * NCH:(b + 1) * NCH,
                                                hp * 128:(hp + 1) * 128], transpose=True)),
                        inc=s_ld2[0], inc_n=16)
                vce = s_ld2[0].v
                for hp in range(8):
                    p.add("vector", (lambda e, hp=hp: e.tensor_copy(c63[hp][:, :],
                                                                    c63b[hp][:, :])),
                          waits=[(s_ld2[0], vce)] if hp == 0 else (), inc=s_v)
                    p.add("vector", (lambda e, hp=hp: e.tensor_copy(c127[hp][:, :],
                                                                    c127b[hp][:, :])),
                          inc=s_v)
            else:
                for hp in range(8):
                    p.add("vector", (lambda e, hp=hp: e.memset(c63[hp][:, :], 0.5)),
                          waits=ldw if hp == 0 else (), inc=s_v)
                    p.add("vector", (lambda e, hp=hp: e.memset(c127[hp][:, :], 0.5)),
                          inc=s_v)
            vcec = s_v.v
            for hp in range(8):
                p.add("vector", (lambda e, hp=hp: e.memset(Sf[hp][:, :], 0.0)), inc=s_v)
                p.add("vector", (lambda e, hp=hp: e.memset(Sb[hp][:, :], 0.0)), inc=s_v)
            for c in range(MAXC):
                cb2 = c % 2
                g4 = c // 4
                if c % 4 == 0 and "noqa" not in CUT:
                    gw = [w(s_mm)] if (c > 0 or b > 0) else []
                    for hp in range(8):
                        p.add("sync", (lambda e, hp=hp, g4=g4, tb0=tb0: e.dma_start(
                            qa[hp * 2 + g4 % 2][:, :],
                            qa_t[tb0 + g4 * 512: tb0 + (g4 + 1) * 512,
                                 hp * 128:(hp + 1) * 128], transpose=True)),
                            waits=gw if hp == 0 else (), inc=s_ld2[1], inc_n=16)
                        p.add("sync", (lambda e, hp=hp, g4=g4, tb0=tb0: e.dma_start(
                            kd[hp * 2 + g4 % 2][:, :],
                            kd_t[tb0 + g4 * 512: tb0 + (g4 + 1) * 512,
                                 hp * 128:(hp + 1) * 128], transpose=True)),
                            inc=s_ld2[1], inc_n=16)
                vqa = s_ld2[1].v
                if "nokv" not in CUT:
                    lw2 = [(s_mm, s_mm.v)] if (c >= 2 or b > 0) else []
                    p.add("sync", (lambda e, c=c, cb2=cb2, tb0=tb0: e.dma_start(
                        kd2[cb2][:, :], kd2_t[tb0 + c * 128: tb0 + (c + 1) * 128, :])),
                        waits=lw2, inc=s_ld2[1], inc_n=16)
                    vk2 = p.add("sync", (lambda e, c=c, cb2=cb2, tb0=tb0: e.dma_start(
                        vt[cb2][:, :], v_f[0:D, tb0 + c * 128: tb0 + (c + 1) * 128],
                        transpose=True)), inc=s_ld2[1], inc_n=16)
                else:
                    vk2 = s_ld2[1].v
                for hp in range(0 if "nobody" in CUT else MAXHP):
                    gi = hp * 2 + g4 % 2
                    coff = (c % 4) * 128
                    mw = [(s_ld2[1], vk2), (s_v, s_v.v)]
                    p.add("tensor", (lambda e, gi=gi, coff=coff:
                                     e.matmul(psS[0][:, 0:128],
                                              kd[gi][0:64, coff:coff + 128],
                                              qa[gi][0:64, coff:coff + 128],
                                              start=True, stop=True,
                                              tile_position=(0, 0))),
                          waits=mw, inc=s_mm)
                    vmm = p.add("tensor", (lambda e, gi=gi, coff=coff:
                                           e.matmul(psS[1][:, 0:128],
                                                    kd[gi][64:128, coff:coff + 128],
                                                    qa[gi][64:128, coff:coff + 128],
                                                    start=True, stop=True,
                                                    tile_position=(64, 0))),
                                inc=s_mm)
                    if "nospp" not in CUT:
                        vv = p.add("vector", (lambda e, hp=hp, c=c: e.tensor_scalar(
                            Spp[:, :], Sf[hp][:, :], c63[hp][:, c:c + 1], None, ALU.mult)),
                            waits=[(s_v, s_v.v), (s_mm, s_mm.v)], inc=s_v)
                        vsp = p.add("vector", (lambda e: e.tensor_copy(Sppb[:, :], Spp[:, :])),
                                   waits=[(s_v, vv)], inc=s_v)
                    else:
                        vsp = s_v.v
                    vs0 = p.add("vector", (lambda e: e.tensor_tensor(
                        smx[0][:, :], psS[0][:, 0:128], G["tril_inc"][:, :],
                        ALU.mult)), waits=[(s_mm, vmm)], inc=s_v)
                    vs1 = p.add("vector", (lambda e: e.tensor_tensor(
                        smx[1][:, :], psS[1][:, 0:128], G["tril_inc"][:, :],
                        ALU.mult)), waits=[(s_v, vs0)], inc=s_v)
                    nointer = "nointer" in CUT
                    nopv = "nopv" in CUT
                    if not nopv:
                        p.add("tensor", (lambda e, cb2=cb2, hp=hp, st=nointer: e.matmul(
                            psO[cb2][0][0:64, 0:128], vt[cb2][:, hp * 128: hp * 128 + 64],
                            smx[0][:, :], start=True, stop=st, tile_position=(0, 0))),
                            waits=[(s_v, vs1)], inc=s_mm)
                        vo = p.add("tensor", (lambda e, cb2=cb2, hp=hp, st=nointer: e.matmul(
                            psO[cb2][1][64:128, 0:128], vt[cb2][:, hp * 128 + 64: hp * 128 + 128],
                            smx[1][:, :], start=True, stop=st, tile_position=(0, 64))),
                            inc=s_mm)
                    else:
                        vo = s_mm.v
                    if not nointer and not nopv:
                        p.add("tensor", (lambda e, cb2=cb2, gi=gi, coff=coff: e.matmul(
                            psO[cb2][0][0:64, 0:128], Sppb[0:64, :],
                            qa[gi][0:64, coff:coff + 128], start=False, stop=True,
                            tile_position=(0, 0))),
                            waits=[(s_v, vsp)], inc=s_mm)
                        vo = p.add("tensor", (lambda e, cb2=cb2, gi=gi, coff=coff: e.matmul(
                            psO[cb2][1][64:128, 0:128], Sppb[64:128, :],
                            qa[gi][64:128, coff:coff + 128], start=False, stop=True,
                            tile_position=(64, 64))),
                            inc=s_mm)
                    if "nosadd" in CUT:
                        vsa = s_mm.v
                    vsa = s_mm.v if "nosadd" in CUT else p.add("tensor", (lambda e, cb2=cb2, hp=hp: e.matmul(
                        psA[cb2][0][0:64, 0:64], kd2[cb2][:, hp * 128: hp * 128 + 64],
                        vt[cb2][:, hp * 128: hp * 128 + 64], start=True, stop=True,
                        tile_position=(0, 0))), inc=s_mm)
                    if "nosadd" not in CUT:
                        vsa = p.add("tensor", (lambda e, cb2=cb2, hp=hp: e.matmul(
                            psA[cb2][1][64:128, 0:64], kd2[cb2][:, hp * 128 + 64: hp * 128 + 128],
                            vt[cb2][:, hp * 128 + 64: hp * 128 + 128], start=True, stop=True,
                            tile_position=(0, 64))), inc=s_mm)
                        vv = p.add("vector", (lambda e, hp=hp, c=c, cb2=cb2:
                                              e.scalar_tensor_tensor(
                            Sf[hp][0:64, :], Sf[hp][0:64, :], c127[hp][0:64, c:c + 1],
                            psA[cb2][0][0:64, 0:64], ALU.mult, ALU.add)),
                            waits=[(s_mm, vsa), (s_v, vcec)], inc=s_v)
                        vv = p.add("vector", (lambda e, hp=hp, c=c, cb2=cb2:
                                              e.scalar_tensor_tensor(
                            Sf[hp][64:128, :], Sf[hp][64:128, :], c127[hp][64:128, c:c + 1],
                            psA[cb2][1][64:128, 0:64], ALU.mult, ALU.add)),
                            waits=[(s_v, vv)], inc=s_v)
                        vv = p.add("vector", (lambda e, hp=hp: e.tensor_copy(
                            Sb[hp][:, :], Sf[hp][:, :])), waits=[(s_v, vv)], inc=s_v)
                    e2 = hp % 2
                    if nopv:
                        continue
                    vo2 = p.add("vector", (lambda e, cb2=cb2, e2=e2: e.tensor_copy(
                        oev[e2][0:64, :], psO[cb2][0][0:64, 0:128])),
                        waits=[(s_mm, vo), (s_st[e2], s_st[e2].v)], inc=s_v)
                    vo2 = p.add("vector", (lambda e, cb2=cb2, e2=e2: e.tensor_copy(
                        oev[e2][64:128, :], psO[cb2][1][64:128, 0:128])),
                        waits=[(s_v, vo2)], inc=s_v)
                    if "nost" not in CUT:
                        p.add("sync", (lambda e, hp=hp, c=c, tb0=tb0, e2=e2: e.dma_start(
                            og_f[hp * 128:(hp + 1) * 128, tb0 + c * 128: tb0 + (c + 1) * 128],
                            oev[e2][:, :])), waits=[(s_v, vo2)], inc=s_st[e2], inc_n=16)
        p.add("sync", lambda e: e.nop(), waits=[w(s_st[0]), w(s_st[1])])
        p.emit()


def emit_moe_cast(nc, ctx, G, w1_in, w2_in, w1bf, w2bf, tag):
    with ExitStack() as pctx:
        sb = lambda *a, **k: pctx.enter_context(nc.sbuf_tensor(*a, **k))
        s_ld = [nc._sb.get(f"sb_w{j}") for j in range(2)]
        s_st = [nc._sb.get(f"sb_y{j}") for j in range(2)]
        t = [sb(f"{tag}_t{j}", [128, D], BF16) for j in range(2)]
        p = Prog(nc)
        jobs = []
        for e in range(8):
            for r in range(16):
                jobs.append((w1_in, w1bf, e, r, 2048))
            for r in range(8):
                jobs.append((w2_in, w2bf, e, r, 1024))
        for i, (src, dst, ee, r, rows) in enumerate(jobs):
            b = i % 2
            lw = [(s_st[b], s_st[b].v)] if i >= 2 else []
            vld = p.add("gpsimd", (lambda e_, src=src, ee=ee, r=r, b=b: e_.dma_start(
                t[b][:, :], src[ee, r * 128:(r + 1) * 128, :])),
                waits=lw, inc=s_ld[b], inc_n=16)
            p.add("sync", (lambda e_, dst=dst, ee=ee, r=r, b=b: e_.dma_start(
                dst[ee, r * 128:(r + 1) * 128, :], t[b][:, :])),
                waits=[(s_ld[b], vld)], inc=s_st[b], inc_n=16)
        p.add("sync", lambda e_: e_.nop(), waits=[w(s_st[0]), w(s_st[1])])
        p.emit()


def wget_xbar3(wbf, e, Krows):
    def f(eng, dst, k0, ks, m0, ms):
        return eng.dma_start(dst[0:ks, 0:ms],
                             wbf[e, m0:m0 + ms, k0:k0 + ks], transpose=True)
    return f


def emit_moe_gate(nc, ctx, G, IN, xn1_t, ce_f, tag):
    with ExitStack() as pctx:
        sb = lambda *a, **k: pctx.enter_context(nc.sbuf_tensor(*a, **k))
        psum = lambda *a, **k: pctx.enter_context(nc.psum_tensor(*a, **k))
        s_ld = [nc._sb.get(f"sb_l{j}") for j in range(2)]
        s_w = nc._sb.get("sb_qs")
        s_v = nc._sb.get("sb_v")
        s_a = nc._sb.get("sb_a")
        s_mm = nc._sb.get("sb_mm")
        s_st = [nc._sb.get(f"sb_y{j}") for j in range(2)]
        wgt = sb(f"{tag}_wg", [8, D], BF16)
        wgT = sb(f"{tag}_wgT", [128, 64], BF16)
        bgr = sb(f"{tag}_bg", [1, 8], BF16)
        bgb = sb(f"{tag}_bgb", [128, 8], F32)
        xnf = [sb(f"{tag}_x{j}", [128, 128], BF16, is_transpose_dest=True)
               for j in range(8)]
        lg = sb(f"{tag}_lg", [128, 8], F32)
        pr = sb(f"{tag}_pr", [128, 8], F32)
        t8 = sb(f"{tag}_t8", [128, 8], F32)
        smm = sb(f"{tag}_smm", [128, 16], F32)
        ceb = [sb(f"{tag}_ce{j}", [128, 8], BF16) for j in range(2)]
        cet = [sb(f"{tag}_cet{j}", [8, 128], BF16) for j in range(2)]
        psL = psum(f"{tag}_pL", [128, 512], F32)
        psT = psum(f"{tag}_pT", [128, 512], F32)
        psTb = psum(f"{tag}_pTb", [128, 512], BF16)

        p = Prog(nc)
        vw = p.add("gpsimd", (lambda e: e.dma_start(wgt[0:8, :], IN["L1_ffn_Wg"][:, :])),
                   inc=s_w, inc_n=16)
        vw2 = p.add("gpsimd", (lambda e: e.dma_start(bgr[0:1, :], IN["L1_ffn_bg"][:, :])),
                    inc=s_w, inc_n=16)
        vt_ = None
        for j in range(8):
            vt_ = p.add("tensor", (lambda e, j=j: e.transpose(
                psTb[0:128, j * 8:(j + 1) * 8], wgt[0:8, j * 128:(j + 1) * 128],
                G["ident_bf16"][0:8, 0:8])),
                waits=[(s_w, vw2)] if j == 0 else (), inc=s_mm)
        vv = p.add("vector", (lambda e: e.tensor_copy(wgT[:, :], psTb[:, 0:64])),
                   waits=[(s_mm, vt_)], inc=s_v)
        vm = p.add("tensor", (lambda e: e.matmul(
            psT[:, 64:72], G["bc_all"][0:1, :], bgr[0:1, :], start=True, stop=True)),
            waits=[(s_v, vv)], inc=s_mm)
        vv = p.add("vector", (lambda e: e.tensor_copy(bgb[:, :], psT[:, 64:72])),
                   waits=[(s_mm, vm)], inc=s_v)
        vpre = vv
        for i in range(NT):
            b = i % 2
            lw = ([(s_st[b], s_st[b].v)] if i >= 2 else [(s_v, vpre)]) + \
                 ([(s_mm, s_mm.v)] if i >= 1 else [])
            vld = None
            for j in range(8):
                vld = p.add("sync", (lambda e, j=j, i=i: e.dma_start(
                    xnf[j][:, :], xn1_t[i * 128:(i + 1) * 128,
                                        j * 128:(j + 1) * 128], transpose=True)),
                    waits=lw if j == 0 else (), inc=s_ld[b], inc_n=16)
            vm = None
            for j in range(8):
                vm = p.add("tensor", (lambda e, j=j: e.matmul(
                    psL[:, 0:8], xnf[j][:, :], wgT[:, j * 8:(j + 1) * 8],
                    start=(j == 0), stop=(j == 7))),
                    waits=[(s_ld[b], vld), (s_v, s_v.v)] if j == 0 else (),
                    inc=s_mm)
            vv = p.add("vector", (lambda e: e.tensor_add(lg[:, :], psL[:, 0:8],
                                                         bgb[:, :])),
                       waits=[(s_mm, vm)], inc=s_v)
            vv = p.add("vector", (lambda e: e.reduce_max(smm[:, 0:1], lg[:, :], AX.X,
                                                         negate=True)),
                       waits=[(s_v, vv)], inc=s_v)
            va = p.add("scalar", (lambda e: e.activation(pr[:, :], lg[:, :], AF.Exp,
                                                         bias=smm[:, 0:1])),
                       waits=[(s_v, vv)], inc=s_a)
            vv = p.add("vector", (lambda e: e.reduce_sum(smm[:, 1:2], pr[:, :], AX.X)),
                       waits=[(s_a, va)], inc=s_v)
            vv = p.add("vector", (lambda e: e.reciprocal(smm[:, 2:3], smm[:, 1:2])),
                       waits=[(s_v, vv)], inc=s_v)
            vv = p.add("vector", (lambda e: e.tensor_scalar(pr[:, :], pr[:, :],
                                                            smm[:, 2:3], None,
                                                            ALU.mult)),
                       waits=[(s_v, vv)], inc=s_v)
            vv = p.add("vector", (lambda e: e.reduce_max(smm[:, 3:4], pr[:, :], AX.X)),
                       waits=[(s_v, vv)], inc=s_v)
            vv = p.add("vector", (lambda e: e.tensor_scalar(t8[:, :], pr[:, :],
                                                            smm[:, 3:4], None,
                                                            ALU.is_ge)),
                       waits=[(s_v, vv)], inc=s_v)
            vv = p.add("vector", (lambda e: e.reduce_sum(smm[:, 4:5], t8[:, :], AX.X)),
                       waits=[(s_v, vv)], inc=s_v)
            vv = p.add("vector", (lambda e: e.scalar_tensor_tensor(
                t8[:, :], t8[:, :], -2.0, pr[:, :], ALU.mult, ALU.add)),
                waits=[(s_v, vv)], inc=s_v)
            vv = p.add("vector", (lambda e: e.reduce_max(smm[:, 5:6], t8[:, :], AX.X)),
                       waits=[(s_v, vv)], inc=s_v)
            vv = p.add("vector", (lambda e: e.tensor_scalar(smm[:, 4:5], smm[:, 4:5],
                                                            2.0, None, ALU.is_ge)),
                       waits=[(s_v, vv)], inc=s_v)
            vv = p.add("vector", (lambda e: e.tensor_sub(smm[:, 6:7], smm[:, 3:4],
                                                         smm[:, 5:6])),
                       waits=[(s_v, vv)], inc=s_v)
            vv = p.add("vector", (lambda e: e.scalar_tensor_tensor(
                smm[:, 7:8], smm[:, 4:5], smm[:, 6:7], smm[:, 5:6], ALU.mult,
                ALU.add)), waits=[(s_v, vv)], inc=s_v)
            vv = p.add("vector", (lambda e: e.tensor_scalar(t8[:, :], pr[:, :],
                                                            smm[:, 7:8], None,
                                                            ALU.is_ge)),
                       waits=[(s_v, vv)], inc=s_v)
            vv = p.add("vector", (lambda e: e.tensor_mul(t8[:, :], t8[:, :], pr[:, :])),
                       waits=[(s_v, vv)], inc=s_v)
            vv = p.add("vector", (lambda e: e.reduce_sum(smm[:, 8:9], t8[:, :], AX.X)),
                       waits=[(s_v, vv)], inc=s_v)
            vv = p.add("vector", (lambda e: e.reciprocal(smm[:, 9:10], smm[:, 8:9])),
                       waits=[(s_v, vv)], inc=s_v)
            vv = p.add("vector", (lambda e, b=b: e.tensor_scalar(
                ceb[b][:, :], t8[:, :], smm[:, 9:10], None, ALU.mult)),
                waits=[(s_v, vv)], inc=s_v)
            vm = p.add("tensor", (lambda e, b=b: e.transpose(
                psTb[0:8, 128:256], ceb[b][:, :], G["ident_bf16"][:, :])),
                waits=[(s_v, vv)], inc=s_mm)
            vv = p.add("vector", (lambda e, b=b: e.tensor_copy(cet[b][0:8, :],
                                                               psTb[0:8, 128:256])),
                       waits=[(s_mm, vm)], inc=s_v)
            p.add("sync", (lambda e, b=b, i=i: e.dma_start(
                ce_f[0:8, i * 128:(i + 1) * 128], cet[b][0:8, :])),
                waits=[(s_v, vv)], inc=s_st[b], inc_n=16)
        p.add("sync", lambda e: e.nop(), waits=[w(s_st[0]), w(s_st[1])])
        p.emit()


def make_in_maps(inputs):
    tabs = host_tables()
    x = np.ascontiguousarray(np.asarray(inputs["L0_attn_x"], dtype=np.float32))
    in_maps = []
    for c in range(8):
        m = {"x_in": x[2 * c:2 * c + 2].reshape(N, D)}
        for k, v in inputs.items():
            if k == "L0_attn_x":
                continue
            a = np.asarray(v, dtype=np.float32)
            if k in ("L1_attn_g_attn", "L1_attn_g_ffn", "L1_ffn_bg"):
                a = a.reshape(1, -1)
            m[k] = np.ascontiguousarray(a)
        for k, v in tabs.items():
            m["tab_" + k] = v
        in_maps.append(m)
    return in_maps


def kernel(**inputs):
    nc = build_nc()
    in_maps = make_in_maps(inputs)
    res = run_bass_kernel_spmd(nc, in_maps, core_ids=list(range(8)))
    return np.concatenate(
        [np.asarray(r["out"]).reshape(2, S, D) for r in res.results], axis=0)


# version 03 (final)
# speedup vs baseline: 1.3022x; 1.2231x over previous
import math
from contextlib import ExitStack

import numpy as np
import ml_dtypes
import concourse.bass as bass
import concourse.mybir as mybir
from concourse.bass_utils import run_bass_kernel_spmd

F32 = mybir.dt.float32
BF16 = mybir.dt.bfloat16
AF = mybir.ActivationFunctionType
ALU = mybir.AluOpType
AX = mybir.AxisListType
bf = ml_dtypes.bfloat16
ts = bass.ts

B_CORE = 2
S = 2048
N = B_CORE * S
D = 1024
NT = N // 128
H0 = 16
DH0 = 64

TAPS: list = []
STAGES = {"ln0", "rproj", "rrot", "rattn", "rfin", "ffn0", "comb0",
          "ln1", "gla", "moe", "comb1"}


class Sem:
    def __init__(self, handle):
        self.h = handle
        self.v = 0


class Prog:
    ENGINES = ("sync", "gpsimd", "tensor", "vector", "scalar")

    def __init__(self, nc, space=False):
        self.nc = nc
        self.space = space
        self.items = {e: [] for e in self.ENGINES}

    def add(self, eng, fn, waits=(), inc=None, inc_n=1):
        w = [(s.h, v) for (s, v) in waits]
        out = None
        if inc is not None:
            inc.v += inc_n
            out = inc.v
        self.items[eng].append((fn, w, inc.h if inc else None, inc_n))
        if self.space and eng in ("tensor", "vector") and inc is not None:
            self.items[eng].append((lambda e: e.nop(), [], None, 1))
        return out

    def emit(self):
        with self.nc.Block() as blk:
            for ename in self.ENGINES:
                items = self.items[ename]
                if not items:
                    continue
                def body(eng, items=items):
                    for fn, waits, inch, incn in items:
                        for h, v in waits:
                            eng.wait_ge(h, v)
                        inst = fn(eng)
                        if inch is not None:
                            inst.then_inc(inch, incn)
                getattr(blk, ename)(body)
        self.items = {e: [] for e in self.ENGINES}


def w(sem):
    return (sem, sem.v)


class SemBank:
    def __init__(self, nc, ctx):
        self.nc = nc
        self.ctx = ctx
        self.sems = {}

    def get(self, name):
        if name not in self.sems:
            self.sems[name] = Sem(self.ctx.enter_context(self.nc.semaphore(name)))
        return self.sems[name]


def _geo_rowsum(decay, q):
    return (1.0 - np.exp(decay * (q + 1))) / (1.0 - np.exp(decay))


def host_tables():
    t = {}
    kk = np.arange(128)[:, None]
    qq = np.arange(512)[None, :]
    t["ident_f32"] = np.eye(128, dtype=np.float32)
    t["ident_bf16"] = np.eye(128, dtype=np.float32).astype(bf)
    a = 1.0 / 10000.0 ** np.linspace(0.0, 1.0, DH0 // 2)
    idx = np.arange(S, dtype=np.float64)
    cos2 = np.cos(idx[:, None] * a[None, :])
    sin2 = np.sin(idx[:, None] * a[None, :])
    decay = np.log(1.0 - 2.0 ** (-5.0 - np.arange(H0, dtype=np.float64)))
    rowsum = _geo_rowsum(decay[None, :], idx[:, None])
    dq = np.exp(decay[None, :] * idx[:, None]) / np.sqrt(rowsum)
    dk = np.exp(-decay[None, :] * idx[:, None])
    t["rcq"] = (cos2[:, None, :] * dq[:, :, None]).reshape(S, 512).astype(np.float32)
    t["rsq"] = (sin2[:, None, :] * dq[:, :, None]).reshape(S, 512).astype(np.float32)
    t["rck"] = (cos2[:, None, :] * dk[:, :, None] / 32.0).reshape(S, 512).astype(np.float32)
    t["rsk"] = (sin2[:, None, :] * dk[:, :, None] / 32.0).reshape(S, 512).astype(np.float32)
    for j in range(4):
        t[f"trilq_{j}"] = (qq >= kk + 128 * j).astype(np.float32).astype(bf)
    for j in range(4):
        t[f"lamu_{j}"] = (qq <= kk + 128 * j).astype(np.float32).astype(bf)
    t["e0col"] = np.ones((128, 1), dtype=np.float32).astype(bf)
    r0 = np.zeros((1, 128), dtype=np.float32); r0[0, :64] = 1
    r1 = np.zeros((1, 128), dtype=np.float32); r1[0, 64:] = 1
    t["bc_lo"] = r0.astype(bf)
    t["bc_hi"] = r1.astype(bf)
    t["bc_all"] = np.ones((1, 128), dtype=np.float32).astype(bf)
    DHF, WINDOW = 64, 512
    half = np.arange(0, DHF, 2, dtype=np.float64)
    inv_freq = 1.0 / 10000.0 ** (half / DHF)
    tt = np.arange(S, dtype=np.float64)
    freqs = tt[:, None] * inv_freq[None, :]
    base = (half + 0.4 * DHF) / (1.4 * DHF)
    power = (tt - S // 2) / float(WINDOW // 2)
    scale = base[None, :] ** power[:, None]
    t["xc_q"] = (np.cos(freqs) * scale).astype(np.float32)
    t["xs_q"] = (np.sin(freqs) * scale).astype(np.float32)
    t["xc_k"] = (np.cos(freqs) / scale).astype(np.float32)
    t["xs_k"] = (np.sin(freqs) / scale).astype(np.float32)
    ss = np.arange(128)[:, None]
    t2 = np.arange(128)[None, :]
    t["tril_inc"] = (t2 >= ss).astype(np.float32).astype(bf)
    t["ucum"] = (t2 >= ss).astype(np.float32)
    t["sel63"] = (ss == 63).astype(np.float32) @ np.ones((1, 128), np.float32)
    t["sel127"] = (ss == 127).astype(np.float32) @ np.ones((1, 128), np.float32)
    t["ones_row"] = np.ones((1, 512), dtype=np.float32).astype(bf)
    t["zrow"] = np.zeros((1, 1024), dtype=np.float32).astype(bf)
    sel = np.zeros((8, 8, 128), dtype=np.float32)
    for e in range(8):
        sel[e, e, :] = 1.0
    t["sel_moe"] = np.concatenate(list(sel), axis=1).astype(bf)
    return t


def build_nc():
    nc = bass.Bass("TRN2", target_bir_lowering=False)
    tabs = host_tables()

    IN = {}
    def extin(name, shape, dt=F32):
        IN[name] = nc.dram_tensor(name, list(shape), dt, kind="ExternalInput")
        return IN[name]

    x_in = extin("x_in", (N, D))
    for nm, shp in [
        ("L0_attn_Wq", (D, D)), ("L0_attn_Wk", (D, D)), ("L0_attn_Wv", (D, D)),
        ("L0_attn_Wg", (D, D)),
        ("L0_ffn_wqkv", (2, D, 1536)), ("L0_ffn_q_scale", (2, 64)),
        ("L0_ffn_k_scale", (2, 64)), ("L0_ffn_wo", (2, 512, D)),
        ("L0_ffn_ln_g", (2, D)), ("L0_ffn_ln_b", (2, D)),
        ("L0_ffn_w1", (2, D, 5460)), ("L0_ffn_b1", (2, 5460)),
        ("L0_ffn_w2", (2, 2730, D)), ("L0_ffn_b2", (2, D)),
        ("L1_attn_Wq", (D, D)), ("L1_attn_Wk", (D, D)), ("L1_attn_Wv", (D, D)),
        ("L1_attn_Wg", (D, D)), ("L1_attn_Wo", (D, D)),
        ("L1_attn_w1", (D, 2736)), ("L1_attn_w2", (D, 2736)),
        ("L1_attn_w3", (2736, D)),
        ("L1_attn_g_attn", (1, D)), ("L1_attn_g_ffn", (1, D)),
        ("L1_ffn_Wg", (8, D)), ("L1_ffn_bg", (1, 8)),
        ("L1_ffn_w1", (8, 2048, D)), ("L1_ffn_w2", (8, D, 1024)),
    ]:
        extin(nm, shp)
    TAB = {}
    for nm, arr in tabs.items():
        dt = BF16 if arr.dtype == bf else F32
        TAB[nm] = nc.dram_tensor("tab_" + nm, list(arr.shape), dt, kind="ExternalInput")

    out = nc.dram_tensor("out", [N, D], F32, kind="ExternalOutput")

    SC = {}
    def scr(name, shape, dt):
        kind = "ExternalOutput" if name in TAPS else "Internal"
        SC[name] = nc.dram_tensor(name, list(shape), dt, kind=kind)
        return SC[name]

    xn_t = scr("XN_T", (N, D), BF16)
    xn_f32 = scr("XN_F32", (N, D), F32)
    q0_f = scr("Q0_F", (D, N), BF16)
    k0_f = scr("K0_F", (D, N), BF16)
    v0_f = scr("V0_F", (D, N), BF16)
    g0_f = scr("G0_F", (D, N), BF16)
    qr0_t = scr("QR0_T", (N, D), BF16)
    kr0_t = scr("KR0_T", (N, D), BF16)
    ot0_f = scr("OT0_F", (D, N), BF16)
    s0_t = scr("S0_T", (N, D), F32)
    scr("XN_PAD", (B_CORE * PAD_S, D), BF16)
    for l in range(2):
        scr(f"QKV{l}_F", (1536, N), BF16)
        scr(f"QRX{l}_T", (N, 512), BF16)
        scr(f"KRX{l}_T", (N, 512), BF16)
        scr(f"QN{l}_T", (N, 8), F32)
        scr(f"KN{l}_T", (N, 8), F32)
        scr(f"OT1{l}_F", (512, N), BF16)
        scr(f"WO{l}_F", (D, N), BF16)
        scr(f"XF1{l}_F32", (N, D), F32)
        scr(f"XF1{l}_PAD", (B_CORE * PAD_S, D), BF16)
        scr(f"H2N{l}_T", (N, D), BF16)
        scr(f"UA{l}_F", (2730, N), BF16)
        scr(f"UG{l}_F", (2730, N), BF16)
        scr(f"H3{l}_F", (2730, N), BF16)
        scr(f"W2O{l}_F", (D, N), BF16)
        scr(f"XF2{l}_F32", (N, D), F32)
    scr("XF20_PAD", (B_CORE * PAD_S, D), BF16)
    scr("X1_T", (N, D), F32)
    scr("XN1_T", (N, D), BF16)
    scr("XN1_F32", (N, D), F32)
    scr("HA_T", (N, D), BF16)
    for nm in ["QG_F", "KG_F", "VG_F", "ZG_F"]:
        scr(nm, (D, N), BF16)
    scr("QA_T", (N, D), BF16)
    scr("KD_T", (N, D), BF16)
    scr("KD2_T", (N, D), BF16)
    scr("CE63R", (B_CORE * NCH, D), BF16)
    scr("CE127R", (B_CORE * NCH, D), BF16)
    scr("OG_F", (D, N), BF16)
    scr("WOG_F", (D, N), BF16)
    scr("XA_T", (N, D), F32)
    scr("HF_T", (N, D), BF16)
    scr("U1_F", (2736, N), BF16)
    scr("U2_F", (2736, N), BF16)
    scr("FF1_F", (2736, N), BF16)
    scr("W3O_F", (D, N), BF16)
    scr("S1_T", (N, D), F32)
    scr("W1BF", (8, 2048, D), BF16)
    scr("W2BF", (8, D, 1024), BF16)
    scr("W2T", (8, 1024, D), BF16)
    scr("CE_F", (8, N), BF16)
    scr("HE_F", (2048, N), BF16)
    for e in range(8):
        scr(f"ACE{e}_F", (D, N), BF16)
    scr("MOE_F", (D, N), BF16)

    with ExitStack() as gctx:
        nc._sb = SemBank(nc, gctx)
        G = load_const_tiles(nc, gctx, TAB)

        final_src = x_in
        if "ln0" in STAGES:
            emit_ln(nc, gctx, G, x_in, xn_t, xn_f32, eps=1e-5, tag="ln0",
                    dst_pad=SC["XN_PAD"])
            final_src = xn_f32
        if "rproj" in STAGES:
            for nm, dst in [("L0_attn_Wq", q0_f), ("L0_attn_Wk", k0_f),
                            ("L0_attn_Wv", v0_f), ("L0_attn_Wg", g0_f)]:
                emit_mm(nc, gctx, G, tag="p0" + nm[-2:],
                        Ktot=D, Mtot=D,
                        wbatch=wbatch_plain(IN[nm]),
                        rhs_load=rhs_xbar(xn_t),
                        out_store=store_F(dst))
        if "rrot" in STAGES:
            emit_ret_rot(nc, gctx, G, TAB, q0_f, k0_f, qr0_t, kr0_t, tag="rrot")
        if "rattn" in STAGES:
            emit_ret_attn(nc, gctx, G, qr0_t, kr0_t, v0_f, ot0_f, tag="rat")
        if "rfin" in STAGES:
            emit_ret_final(nc, gctx, G, ot0_f, g0_f, s0_t, tag="rfin")
            final_src = s0_t

        if "ffn0" in STAGES:
            xn_pad = SC["XN_PAD"]
            emit_zero_pad_rows(nc, gctx, G, xn_pad, tag="zp0")
            stream_pad, stream_f32 = xn_pad, xn_f32
            for l in range(2):
                qkvf = SC[f"QKV{l}_F"]
                emit_mm(nc, gctx, G, tag=f"fq{l}", Ktot=D, Mtot=1536,
                        wbatch=wbatch_plain(IN["L0_ffn_wqkv"], l=l),
                        rhs_load=rhs_xbar_pad(stream_pad),
                        out_store=store_F(qkvf))
                emit_lffn_rot(nc, gctx, G, TAB, IN, l, qkvf,
                              SC[f"QRX{l}_T"], SC[f"KRX{l}_T"],
                              SC[f"QN{l}_T"], SC[f"KN{l}_T"], tag=f"fr{l}")
                emit_lattn(nc, gctx, G, SC[f"QRX{l}_T"], SC[f"KRX{l}_T"], qkvf,
                           SC[f"QN{l}_T"], SC[f"KN{l}_T"], SC[f"OT1{l}_F"],
                           tag=f"fa{l}")
                emit_mm(nc, gctx, G, tag=f"fo{l}", Ktot=512, Mtot=D,
                        wbatch=wbatch_plain(IN["L0_ffn_wo"], l=l),
                        rhs_load=rhs_direct(SC[f"OT1{l}_F"]),
                        out_store=store_F(SC[f"WO{l}_F"]))
                emit_zero_pad_rows(nc, gctx, G, SC[f"XF1{l}_PAD"], tag=f"zp1{l}")
                emit_combine(nc, gctx, G, tag=f"fc1{l}",
                             srcs={"a": stream_f32, "b": SC[f"WO{l}_F"]},
                             dst_f32=SC[f"XF1{l}_F32"], dst_pad=SC[f"XF1{l}_PAD"],
                             mode="resid")
                emit_ln(nc, gctx, G, None, SC[f"H2N{l}_T"], None, eps=1e-5,
                        tag=f"fl{l}", gain=IN["L0_ffn_ln_g"][l:l + 1, :],
                        bias_row=IN["L0_ffn_ln_b"][l:l + 1, :],
                        src_pad=SC[f"XF1{l}_PAD"])
                emit_mm(nc, gctx, G, tag=f"fwa{l}", Ktot=D, Mtot=2730,
                        wbatch=wbatch_plain(IN["L0_ffn_w1"], l=l, c0=0),
                        bias_get=bias_cols(IN["L0_ffn_b1"], l, 0),
                        rhs_load=rhs_xbar(SC[f"H2N{l}_T"]),
                        out_store=store_F(SC[f"UA{l}_F"]))
                emit_mm(nc, gctx, G, tag=f"fwg{l}", Ktot=D, Mtot=2730,
                        wbatch=wbatch_plain(IN["L0_ffn_w1"], l=l, c0=2730),
                        bias_get=bias_cols(IN["L0_ffn_b1"], l, 2730),
                        rhs_load=rhs_xbar(SC[f"H2N{l}_T"]),
                        out_store=store_F(SC[f"UG{l}_F"]))
                emit_ew_F(nc, gctx, G, tag=f"fgg{l}", Ktot=2730,
                          src_a=SC[f"UA{l}_F"], src_g=SC[f"UG{l}_F"],
                          dst=SC[f"H3{l}_F"], act=AF.Gelu)
                emit_mm(nc, gctx, G, tag=f"fw2{l}", Ktot=2730, Mtot=D,
                        wbatch=wbatch_plain(IN["L0_ffn_w2"], l=l),
                        bias_get=bias_cols(IN["L0_ffn_b2"], l, 0),
                        rhs_load=rhs_direct(SC[f"H3{l}_F"]),
                        out_store=store_F(SC[f"W2O{l}_F"]), t_ext=2048)
                if l == 0:
                    emit_zero_pad_rows(nc, gctx, G, SC["XF20_PAD"], tag="zp20")
                emit_combine(nc, gctx, G, tag=f"fc2{l}",
                             srcs={"a": SC[f"XF1{l}_F32"], "b": SC[f"W2O{l}_F"]},
                             dst_f32=SC[f"XF2{l}_F32"],
                             dst_pad=SC["XF20_PAD"] if l == 0 else None,
                             mode="resid")
                stream_pad = SC["XF20_PAD"] if l == 0 else None
                stream_f32 = SC[f"XF2{l}_F32"]
            final_src = SC["XF21_F32"]

        if "comb0" in STAGES:
            emit_combine(nc, gctx, G, tag="cmb0",
                         srcs={"a": x_in, "s": s0_t, "g": SC["XF21_F32"]},
                         dst_f32=SC["X1_T"], mode="gate")
            final_src = SC["X1_T"]

        if "gla_dbg" in STAGES:
            STAGES.add("gla")
        if "ln1" in STAGES:
            emit_ln(nc, gctx, G, SC["X1_T"], SC["XN1_T"], SC["XN1_F32"],
                    eps=1e-5, tag="ln1")
            final_src = SC["XN1_F32"]
        if "gla" in STAGES:
            import os as _os
            _gskip = set((_os.environ.get("GLA_SKIP") or "").split(","))
            emit_ln(nc, gctx, G, SC["XN1_F32"], SC["HA_T"], None, eps=1e-6,
                    tag="grms", gain=IN["L1_attn_g_attn"][0:1, :], rms=True)
            for nm, dst in [("L1_attn_Wq", "QG_F"), ("L1_attn_Wk", "KG_F"),
                            ("L1_attn_Wv", "VG_F"), ("L1_attn_Wg", "ZG_F")]:
                if "proj" not in _gskip:
                    emit_mm(nc, gctx, G, tag="g" + nm[-2:], Ktot=D, Mtot=D,
                            wbatch=wbatch_plain(IN[nm]), rhs_load=rhs_xbar(SC["HA_T"]),
                            out_store=store_F(SC[dst]))
            if "prep" not in _gskip:
                emit_gla_prep(nc, gctx, G, SC["QG_F"], SC["KG_F"], SC["ZG_F"],
                              SC["QA_T"], SC["KD_T"], SC["KD2_T"],
                              SC["CE63R"], SC["CE127R"], tag="gp")
            if "scan" not in _gskip:
                emit_gla_scan(nc, gctx, G, SC["QA_T"], SC["KD_T"], SC["KD2_T"],
                              SC["VG_F"], SC["CE63R"], SC["CE127R"], SC["OG_F"],
                              tag="gs")
            emit_mm(nc, gctx, G, tag="gwo", Ktot=D, Mtot=D,
                    wbatch=wbatch_plain(IN["L1_attn_Wo"]),
                    rhs_load=rhs_direct(SC["OG_F"]),
                    out_store=store_F(SC["WOG_F"]))
            emit_combine(nc, gctx, G, tag="gxa",
                         srcs={"a": SC["XN1_F32"], "b": SC["WOG_F"]},
                         dst_f32=SC["XA_T"], mode="resid")
            emit_ln(nc, gctx, G, SC["XA_T"], SC["HF_T"], None, eps=1e-6,
                    tag="grm2", gain=IN["L1_attn_g_ffn"][0:1, :], rms=True)
            emit_mm(nc, gctx, G, tag="gw1", Ktot=D, Mtot=2736,
                    wbatch=wbatch_plain(IN["L1_attn_w1"]),
                    rhs_load=rhs_xbar(SC["HF_T"]), out_store=store_F(SC["U1_F"]))
            emit_mm(nc, gctx, G, tag="gw2", Ktot=D, Mtot=2736,
                    wbatch=wbatch_plain(IN["L1_attn_w2"]),
                    rhs_load=rhs_xbar(SC["HF_T"]), out_store=store_F(SC["U2_F"]))
            emit_ew_F(nc, gctx, G, tag="gsm", Ktot=2736, src_a=SC["U2_F"],
                      src_g=SC["U1_F"], dst=SC["FF1_F"], act=AF.Silu)
            emit_mm(nc, gctx, G, tag="gw3", Ktot=2736, Mtot=D,
                    wbatch=wbatch_plain(IN["L1_attn_w3"]),
                    rhs_load=rhs_direct(SC["FF1_F"]),
                    out_store=store_F(SC["W3O_F"]), t_ext=2048)
            emit_combine(nc, gctx, G, tag="gs1",
                         srcs={"a": SC["XA_T"], "b": SC["W3O_F"]},
                         dst_f32=SC["S1_T"], mode="resid")
            final_src = SC["S1_T"]

        if "moe" in STAGES:
            emit_moe_cast(nc, gctx, G, IN["L1_ffn_w1"], IN["L1_ffn_w2"],
                          SC["W1BF"], SC["W2BF"], SC["W2T"], tag="mc")
            emit_moe_gate(nc, gctx, G, IN, SC["XN1_T"], SC["CE_F"], tag="mg")
            for e in range(8):
                emit_mm(nc, gctx, G, tag=f"m1e{e}", Ktot=D, Mtot=2048,
                        wget=wget_xbar3(SC["W1BF"], e, D),
                        rhs_load=rhs_xbar(SC["XN1_T"]),
                        out_store=store_F(SC["HE_F"]), wq="sync")
                emit_ew_F(nc, gctx, G, tag=f"ma{e}", Ktot=D,
                          src_a=SC["HE_F"], a_off=1024, src_g=SC["HE_F"], g_off=0,
                          dst=SC[f"ACE{e}_F"], act=AF.Silu,
                          ce_f=SC["CE_F"], ce_row=e)
            W2T_t = SC["W2T"]

            def s2_wbatch(dst_t, m0, ms, kch):
                outs = []
                for e_ in range(8):
                    srcap = bass.AP(W2T_t, e_ * 1024 * D + m0,
                                    [[D, 128], [128 * D, 8], [1, ms]])
                    dstap = bass.AP(dst_t, e_ * 1024,
                                    [[64 * 128, 128], [128, 8], [1, ms]])
                    outs.append(lambda e, srcap=srcap, dstap=dstap:
                                e.dma_start(dstap, srcap))
                return outs
            def s2_rhs(dst, k0, ks, t0, tsz):
                e_, kk = k0 // 1024, k0 % 1024
                return [lambda eng: eng.dma_start(
                    dst[0:ks, 0:tsz], SC[f"ACE{e_}_F"][kk:kk + ks, t0:t0 + tsz])]
            emit_mm(nc, gctx, G, tag="m2", Ktot=8 * D, Mtot=D,
                    wbatch=s2_wbatch, rhs_load=s2_rhs,
                    out_store=store_F(SC["MOE_F"]), wq="sync", t_ext=1024)

        if "comb1" in STAGES:
            emit_combine(nc, gctx, G, tag="cmb1",
                         srcs={"a": SC["X1_T"], "s": SC["S1_T"],
                               "g_f": SC["MOE_F"]},
                         dst_f32=out, mode="gate")
            final_src = None

        if "lnaff_dbg" in STAGES:
            emit_ln(nc, gctx, G, None, SC["H2N0_T"], None, eps=1e-5,
                    tag="fl0", gain=IN["L0_ffn_ln_g"][0:1, :],
                    bias_row=IN["L0_ffn_ln_b"][0:1, :],
                    src_pad=SC["XF10_PAD"])
            final_src = xn_f32

        if final_src is not None:
            emit_copy_out(nc, gctx, final_src, out, tag="cout")

    nc.finalize()
    return nc


def load_const_tiles(nc, ctx, TAB):
    sb = lambda *a, **k: ctx.enter_context(nc.sbuf_tensor(*a, **k))
    G = {}
    specs = [
        ("ident_bf16", [128, 128], BF16), ("ident_f32", [128, 128], F32),
        ("e0col", [128, 1], BF16), ("bc_lo", [1, 128], BF16),
        ("bc_hi", [1, 128], BF16), ("bc_all", [1, 128], BF16),
        ("ones_row", [1, 512], BF16),
        ("tril_inc", [128, 128], BF16), ("ucum", [128, 128], F32),
        ("sel63", [128, 128], F32), ("sel127", [128, 128], F32),
        ("sel_moe", [8, 1024], BF16), ("zrow", [1, 1024], BF16),
    ] + [(f"trilq_{j}", [128, 512], BF16) for j in range(4)] \
      + [(f"lamu_{j}", [128, 512], BF16) for j in range(4)]
    s_c = nc._sb.get("sb_const")
    p = Prog(nc)
    for nm, shape, dt in specs:
        G[nm] = sb("g_" + nm, shape, dt)
        p.add("sync", lambda e, g=G[nm], t=TAB[nm], sh=shape: e.dma_start(
            g[tuple(slice(0, x) for x in sh)], t[tuple(slice(0, x) for x in sh)]),
            inc=s_c, inc_n=16)
    p.add("sync", lambda e: e.nop(), waits=[w(s_c)])
    p.emit()
    return G


def wget_plain(wt):
    def f(eng, dst, k0, ks, m0, ms):
        return eng.dma_start(dst[0:ks, 0:ms], wt[k0:k0 + ks, m0:m0 + ms])
    return f


def wbatch_plain(wt, l=None, c0=0):
    shape = list(wt.shape)
    M_full = shape[-1]
    K_full = shape[-2]
    base0 = (l * K_full * M_full) if l is not None else 0

    def f(dst_t, m0, ms, kch):
        outs = []
        nfull = sum(1 for _, ks in kch if ks == 128)
        rowlen = len(kch) * 128
        if nfull:
            src = bass.AP(wt, base0 + c0 + m0,
                          [[M_full, 128], [128 * M_full, nfull], [1, ms]])
            dst = bass.AP(dst_t, 0, [[rowlen, 128], [128, nfull], [1, ms]])
            outs.append(lambda e, src=src, dst=dst: e.dma_start(dst, src))
        if nfull < len(kch):
            k0, ks = kch[nfull]
            src2 = bass.AP(wt, base0 + c0 + m0 + k0 * M_full,
                           [[M_full, ks], [1, ms]])
            dst2 = bass.AP(dst_t, nfull * 128, [[rowlen, ks], [1, ms]])
            outs.append(lambda e, src2=src2, dst2=dst2: e.dma_start(dst2, src2))
        return outs
    return f


def wget_plain3(wt, l):
    def f(eng, dst, k0, ks, m0, ms):
        return eng.dma_start(dst[0:ks, 0:ms], wt[l, k0:k0 + ks, m0:m0 + ms])
    return f


def wget_cols(wt, l, c0):
    def f(eng, dst, k0, ks, m0, ms):
        return eng.dma_start(dst[0:ks, 0:ms],
                             wt[l, k0:k0 + ks, c0 + m0:c0 + m0 + ms])
    return f


def bias_cols(bv, l, c0):
    def f(eng, dst, m0, ms):
        return eng.dma_start(dst, bv[l:l + 1, c0 + m0:c0 + m0 + ms])
    return f


def rhs_xbar(src_t, col_off=0, row_off=0):
    def f(dst, k0, ks, t0, tsz):
        return [lambda e: e.dma_start(dst[0:ks, 0:tsz],
                                      src_t[row_off + t0: row_off + t0 + tsz,
                                            col_off + k0: col_off + k0 + ks],
                                      transpose=True)]
    return f


def rhs_direct(src_f, row_off=0):
    def f(dst, k0, ks, t0, tsz):
        return [lambda e: e.dma_start(dst[0:ks, 0:tsz],
                                      src_f[row_off + k0: row_off + k0 + ks,
                                            t0:t0 + tsz])]
    return f


def rhs_xbar_pad(src_pad, shift_ge=512):
    def f(dst, k0, ks, t0, tsz):
        outs = []
        sh = 0 if k0 + ks <= shift_ge else 1
        assert k0 >= shift_ge or k0 + ks <= shift_ge
        for b0 in range(t0 // S, (t0 + tsz - 1) // S + 1):
            lo = max(t0, b0 * S)
            hi = min(t0 + tsz, (b0 + 1) * S)
            row = b0 * (S + 1) + (1 - sh) + (lo - b0 * S)
            outs.append(lambda e, lo=lo, hi=hi, row=row: e.dma_start(
                dst[0:ks, lo - t0: hi - t0],
                src_pad[row: row + (hi - lo), k0:k0 + ks], transpose=True))
        return outs
    return f


def store_F(dst_f, row_off=0):
    def f(eng, src, m0, ms, t0, tsz):
        return eng.dma_start(dst_f[row_off + m0: row_off + m0 + ms, t0:t0 + tsz], src)
    return f


def emit_mm(nc, ctx, G, tag, *, Ktot, Mtot, wget=None, rhs_load, out_store,
            bias_get=None, t_ext=None, evac=None, wq="gpsimd", wbatch=None):
    with ExitStack() as pctx:
        sb = lambda *a, **k: pctx.enter_context(nc.sbuf_tensor(*a, **k))
        psum = lambda *a, **k: pctx.enter_context(nc.psum_tensor(*a, **k))
        if t_ext is None:
            t_ext = N if Ktot <= 1100 else 2048
        kch = _chunks(Ktot, 128)
        mch = _chunks(Mtot, 128)
        groups = _chunks(N, t_ext)
        ntb = t_ext // 512
        nwpm = len(kch) + (1 if bias_get else 0)

        rhs_sb = [sb(f"{tag}_r{j}", [128, t_ext], BF16, is_transpose_dest=True)
                  for j in range(len(kch))]
        wsb_w = (len(kch) * 128) if wbatch is not None else 128
        wsb = [sb(f"{tag}_w{j}", [128, wsb_w], BF16) for j in range(2)]
        ev = [sb(f"{tag}_e{j}", [128, t_ext], BF16) for j in range(2)]
        ps = psum(f"{tag}_ps", [128, t_ext], F32)

        s_rhs = nc._sb.get("sb_rhs")
        s_w = [nc._sb.get(f"sb_w{j}" if wq == "gpsimd" else f"sb_x{j}")
               for j in range(2)]
        s_mm = nc._sb.get("sb_mm")
        s_ev = nc._sb.get("sb_ev")
        s_st = [nc._sb.get(f"sb_st{j}") for j in range(2)]

        p = Prog(nc)
        jw = 0
        wtile_done = []
        ev_done = []
        st_done = [[], []]
        rhs_done = []

        for g, (t0, text) in enumerate(groups):
            waits = [w(s_mm)] if g > 0 else []
            for j, (k0, ks) in enumerate(kch):
                for fi, fn in enumerate(rhs_load(rhs_sb[j], k0, ks, t0, text)):
                    p.add("sync", fn, waits=waits if (j == 0 and fi == 0) else (),
                          inc=s_rhs, inc_n=16)
            rhs_done.append(s_rhs.v)
            for mi, (m0, ms) in enumerate(mch):
                gm = g * len(mch) + mi
                e2 = gm % 2
                if wbatch is not None:
                    wb = jw % 2
                    wwaits = []
                    if jw >= 2:
                        wwaits.append((s_mm, wtile_done[jw - 2]))
                    for fi, fn in enumerate(wbatch(wsb[wb], m0, ms, kch)):
                        p.add(wq, fn, waits=wwaits if fi == 0 else (),
                              inc=s_w[wb], inc_n=16)
                    mm_waits = [w(s_w[wb]), (s_rhs, rhs_done[g])]
                    if gm > 0:
                        mm_waits.append((s_ev, ev_done[gm - 1]))
                    for ki, (k0, ks) in enumerate(kch):
                        for tb in range(ntb):
                            p.add("tensor", (lambda e, tb=tb, wb=wb, ki=ki, ks=ks,
                                             ms=ms, last=(ki == len(kch) - 1
                                                          and bias_get is None):
                                             e.matmul(ps[0:ms, tb * 512:(tb + 1) * 512],
                                                      wsb[wb][0:ks, ki * 128: ki * 128 + ms],
                                                      rhs_sb[ki][0:ks, tb * 512:(tb + 1) * 512],
                                                      start=(ki == 0), stop=last)),
                                  waits=mm_waits if (ki == 0 and tb == 0) else (),
                                  inc=s_mm if (ki == len(kch) - 1 and tb == ntb - 1) else None)
                    wtile_done.append(s_mm.v)
                    jw += 1
                else:
                  for ki, (k0, ks) in enumerate(kch):
                    wb = jw % 2
                    wwaits = []
                    if jw >= 2:
                        wwaits.append((s_mm, wtile_done[jw - 2]))
                    p.add(wq, (lambda e, dst=wsb[wb], k0=k0, ks=ks, m0=m0,
                               ms=ms, wg=wget: wg(e, dst, k0, ks, m0, ms)),
                          waits=wwaits, inc=s_w[wb], inc_n=16)
                    mm_waits = [w(s_w[wb]), (s_rhs, rhs_done[g])]
                    if gm > 0:
                        mm_waits.append((s_ev, ev_done[gm - 1]))
                    for tb in range(ntb):
                        p.add("tensor", (lambda e, tb=tb, wb=wb, ki=ki, ks=ks,
                                         ms=ms, j=ki, last=(ki == len(kch) - 1
                                                            and bias_get is None):
                                         e.matmul(ps[0:ms, tb * 512:(tb + 1) * 512],
                                                  wsb[wb][0:ks, 0:ms],
                                                  rhs_sb[j][0:ks, tb * 512:(tb + 1) * 512],
                                                  start=(ki == 0), stop=last)),
                              waits=mm_waits if tb == 0 else (),
                              inc=s_mm if tb == ntb - 1 else None)
                    wtile_done.append(s_mm.v)
                    jw += 1
                if bias_get is not None:
                    wb = jw % 2
                    wwaits = []
                    if jw >= 2:
                        wwaits.append((s_mm, wtile_done[jw - 2]))
                    p.add("gpsimd", (lambda e, dst=wsb[wb], m0=m0, ms=ms,
                                     bg=bias_get: bg(e, dst[0:1, 0:ms], m0, ms)),
                          waits=wwaits, inc=s_w[wb], inc_n=16)
                    for tb in range(ntb):
                        p.add("tensor", (lambda e, tb=tb, wb=wb, ms=ms:
                                         e.matmul(ps[0:ms, tb * 512:(tb + 1) * 512],
                                                  wsb[wb][0:1, 0:ms],
                                                  G["ones_row"][0:1, :],
                                                  start=False, stop=True)),
                              waits=[w(s_w[wb])] if tb == 0 else (),
                              inc=s_mm if tb == ntb - 1 else None)
                    wtile_done.append(s_mm.v)
                    jw += 1
                evw = [w(s_mm)]
                if gm >= 2:
                    evw.append((s_st[e2], st_done[e2][-1]))
                if evac is None:
                    p.add("vector", (lambda e, e2=e2, ms=ms, text=text:
                                     e.tensor_copy(ev[e2][0:ms, 0:text],
                                                   ps[0:ms, 0:text])),
                          waits=evw, inc=s_ev)
                else:
                    evac(p, evw, s_ev, ps, ev[e2], m0, ms, t0, text)
                ev_done.append(s_ev.v)
                p.add("sync", (lambda e, e2=e2, m0=m0, ms=ms, t0=t0, text=text,
                               os=out_store: os(e, ev[e2][0:ms, 0:text], m0, ms,
                                                t0, text)),
                      waits=[(s_ev, ev_done[gm])], inc=s_st[e2], inc_n=16)
                st_done[e2].append(s_st[e2].v)
        p.add("sync", lambda e: e.nop(), waits=[w(s_st[0]), w(s_st[1])])
        p.emit()


def emit_ln(nc, ctx, G, src, dst_bf, dst_f32, eps, tag, gain=None, bias_row=None,
            rms=False, src_pad=None, dst_pad=None):
    with ExitStack() as pctx:
        sb = lambda *a, **k: pctx.enter_context(nc.sbuf_tensor(*a, **k))
        s_ld = [nc._sb.get(f"sb_l{j}") for j in range(2)]
        s_v = nc._sb.get("sb_v")
        s_a = nc._sb.get("sb_a")
        s_st = [nc._sb.get(f"sb_s{j}") for j in range(2)]
        s_g = nc._sb.get("sb_g")
        xt = [sb(f"{tag}_x{j}", [128, D], BF16 if src_pad is not None else F32)
              for j in range(2)]
        xc = [sb(f"{tag}_c{j}", [128, D], F32) for j in range(2)]
        sq = sb(f"{tag}_sq", [128, D], F32)
        ob = [sb(f"{tag}_ob{j}", [128, D], BF16) for j in range(2)]
        of = [sb(f"{tag}_of{j}", [128, D], F32) for j in range(2)]
        sm = sb(f"{tag}_sm", [128, 8], F32)
        gb = sb(f"{tag}_gb", [128, D], F32) if gain is not None else None
        bb = sb(f"{tag}_bb", [128, D], F32) if bias_row is not None else None
        p = Prog(nc)
        if gain is not None or bias_row is not None:
            psb = pctx.enter_context(nc.psum_tensor(f"{tag}_pg", [128, 1024], F32))
            row = sb(f"{tag}_row", [1, D], BF16)
            rowf = sb(f"{tag}_rowf", [1, D], F32)
            for which, (vec, dstt) in enumerate([(gain, gb), (bias_row, bb)]):
                if vec is None:
                    continue
                v1 = p.add("sync", (lambda e, vec=vec: e.dma_start(
                    rowf[0:1, :], vec)), waits=[w(s_v)], inc=s_g, inc_n=16)
                v2 = p.add("vector", lambda e: e.tensor_copy(row[0:1, :], rowf[0:1, :]),
                           waits=[(s_g, v1)], inc=s_v)
                smm = nc._sb.get("sb_bcmm")
                for h in range(2):
                    p.add("tensor", (lambda e, h=h: e.matmul(
                        psb[:, h * 512:(h + 1) * 512], G["bc_all"][0:1, :],
                        row[0:1, h * 512:(h + 1) * 512],
                        start=True, stop=True)),
                        waits=[(s_v, v2)] if h == 0 else (), inc=smm)
                p.add("vector", (lambda e, dstt=dstt: e.tensor_copy(dstt[:, :], psb[:, :])),
                      waits=[w(smm)], inc=s_v)
        for i in range(NT):
            b = i % 2
            lw = []
            if i >= 2:
                lw.append((s_st[b], s_st[b].v))
            if src_pad is None:
                vld = p.add("sync", (lambda e, b=b, i=i: e.dma_start(
                    xt[b][:, :], src[ts(i, 128), :])), waits=lw, inc=s_ld[b], inc_n=16)
            else:
                pr = pad_row(i * 128)
                p.add("sync", (lambda e, b=b, pr=pr: e.dma_start(
                    xt[b][:, 0:512], src_pad[pr:pr + 128, 0:512])),
                    waits=lw, inc=s_ld[b], inc_n=16)
                vld = p.add("sync", (lambda e, b=b, pr=pr: e.dma_start(
                    xt[b][:, 512:1024], src_pad[pr - 1:pr + 127, 512:1024])),
                    inc=s_ld[b], inc_n=16)
            v1 = p.add("vector", (lambda e, b=b: e.reduce_sum(sm[:, 0:1], xt[b][:, :], AX.X)),
                       waits=[(s_ld[b], vld)], inc=s_v)
            if not rms:
                v2 = p.add("vector", lambda e: e.tensor_scalar(
                    sm[:, 1:2], sm[:, 0:1], -1.0 / D, None, ALU.mult),
                    waits=[(s_v, v1)], inc=s_v)
                v3 = p.add("vector", (lambda e, b=b: e.tensor_scalar(
                    xc[b][:, :], xt[b][:, :], sm[:, 1:2], None, ALU.add)),
                    waits=[(s_v, v2)], inc=s_v)
            else:
                v3 = p.add("vector", (lambda e, b=b: e.tensor_copy(xc[b][:, :], xt[b][:, :])),
                           waits=[(s_v, v1)], inc=s_v)
            va = p.add("scalar", (lambda e, b=b: e.activation(
                sq[:, :], xc[b][:, :], AF.Square, accum_out=sm[:, 2:3])),
                waits=[(s_v, v3)], inc=s_a)
            v4 = p.add("vector", lambda e: e.tensor_scalar(
                sm[:, 3:4], sm[:, 2:3], 1.0 / D, eps, ALU.mult, ALU.add),
                waits=[(s_a, va)], inc=s_v)
            v5 = p.add("vector", lambda e: e.reciprocal(sm[:, 4:5], sm[:, 3:4]),
                       waits=[(s_v, v4)], inc=s_v)
            va2 = p.add("scalar", lambda e: e.activation(sm[:, 5:6], sm[:, 4:5], AF.Sqrt),
                        waits=[(s_v, v5)], inc=s_a)
            if gain is None:
                v6 = p.add("vector", (lambda e, b=b: e.tensor_scalar(
                    ob[b][:, :], xc[b][:, :], sm[:, 5:6], None, ALU.mult)),
                    waits=[(s_a, va2)], inc=s_v)
                v7 = p.add("vector", (lambda e, b=b: e.tensor_scalar(
                    of[b][:, :], xc[b][:, :], sm[:, 5:6], None, ALU.mult)),
                    inc=s_v) if dst_f32 is not None else v6
            else:
                v6a = p.add("vector", (lambda e, b=b: e.tensor_scalar(
                    xc[b][:, :], xc[b][:, :], sm[:, 5:6], None, ALU.mult)),
                    waits=[(s_a, va2)], inc=s_v)
                if bias_row is not None:
                    v6b = p.add("vector", (lambda e, b=b: e.tensor_mul(
                        xc[b][:, :], xc[b][:, :], gb[:, :])),
                        waits=[(s_v, v6a)], inc=s_v)
                    v6 = p.add("vector", (lambda e, b=b: e.tensor_add(
                        xc[b][:, :], xc[b][:, :], bb[:, :])), waits=[(s_v, v6b)], inc=s_v)
                else:
                    v6 = p.add("vector", (lambda e, b=b: e.tensor_mul(
                        xc[b][:, :], xc[b][:, :], gb[:, :])), waits=[(s_v, v6a)], inc=s_v)
                v6 = p.add("vector", (lambda e, b=b: e.tensor_copy(ob[b][:, :], xc[b][:, :])),
                           waits=[(s_v, v6)], inc=s_v)
                v7 = p.add("vector", (lambda e, b=b: e.tensor_copy(of[b][:, :], xc[b][:, :])),
                           inc=s_v) if dst_f32 is not None else v6
            last_v = max(v6, v7 if dst_f32 is not None else v6)
            if dst_bf is not None:
                p.add("scalar", (lambda e, b=b, i=i: e.dma_start(
                    dst_bf[ts(i, 128), :], ob[b][:, :])), waits=[(s_v, last_v)],
                    inc=s_st[b], inc_n=16)
            if dst_f32 is not None:
                p.add("scalar", (lambda e, b=b, i=i: e.dma_start(
                    dst_f32[ts(i, 128), :], of[b][:, :])), waits=[(s_v, last_v)] if dst_bf is None else (),
                    inc=s_st[b], inc_n=16)
            if dst_pad is not None:
                pr = pad_row(i * 128)
                p.add("scalar", (lambda e, b=b, pr=pr: e.dma_start(
                    dst_pad[pr:pr + 128, :], ob[b][:, :])),
                    waits=[(s_v, last_v)] if (dst_bf is None and dst_f32 is None) else (),
                    inc=s_st[b], inc_n=16)
        p.add("scalar", lambda e: e.nop(), waits=[w(s_st[0]), w(s_st[1])])
        p.emit()


def emit_copy_out(nc, ctx, src, dst, tag):
    with ExitStack() as pctx:
        sb = lambda *a, **k: pctx.enter_context(nc.sbuf_tensor(*a, **k))
        s_ld = [nc._sb.get(f"sb_l{j}") for j in range(2)]
        s_st = [nc._sb.get(f"sb_y{j}") for j in range(2)]
        t = [sb(f"{tag}_t{j}", [128, D], F32) for j in range(2)]
        p = Prog(nc)
        for i in range(NT):
            b = i % 2
            lw = [(s_st[b], s_st[b].v)] if i >= 2 else []
            vld = p.add("sync", (lambda e, b=b, i=i: e.dma_start(
                t[b][:, :], src[ts(i, 128), :])), waits=lw, inc=s_ld[b], inc_n=16)
            p.add("sync", (lambda e, b=b, i=i: e.dma_start(
                dst[ts(i, 128), :], t[b][:, :])), waits=[(s_ld[b], vld)],
                inc=s_st[b], inc_n=16)
        p.add("sync", lambda e: e.nop(), waits=[w(s_st[0]), w(s_st[1])])
        p.emit()


def _chunks(total, step):
    out = []
    o = 0
    while o < total:
        out.append((o, min(step, total - o)))
        o += step
    return out


def emit_ret_rot(nc, ctx, G, TAB, q_f, k_f, qr_t, kr_t, tag):
    with ExitStack() as pctx:
        sb = lambda *a, **k: pctx.enter_context(nc.sbuf_tensor(*a, **k))
        s_ld = [nc._sb.get(f"sb_l{j}") for j in range(2)]
        s_v = nc._sb.get("sb_v")
        s_st = [nc._sb.get(f"sb_s{j}") for j in range(2)]
        qt = [sb(f"{tag}_q{j}", [128, D], BF16, is_transpose_dest=True) for j in range(2)]
        kt = [sb(f"{tag}_k{j}", [128, D], BF16, is_transpose_dest=True) for j in range(2)]
        rc = [sb(f"{tag}_rc{j}", [128, 512], F32) for j in range(2)]
        rs = [sb(f"{tag}_rs{j}", [128, 512], F32) for j in range(2)]
        rck = [sb(f"{tag}_rck{j}", [128, 512], F32) for j in range(2)]
        rsk = [sb(f"{tag}_rsk{j}", [128, 512], F32) for j in range(2)]
        tmp = sb(f"{tag}_tmp", [128, D], F32)
        oq = [sb(f"{tag}_oq{j}", [128, D], BF16) for j in range(2)]
        ok = [sb(f"{tag}_ok{j}", [128, D], BF16) for j in range(2)]

        def ev3(t, par):
            return bass.AP(t, par, [[D, 128], [64, 16], [2, 32]])

        def tb3(t):
            return bass.AP(t, 0, [[512, 128], [32, 16], [1, 32]])

        p = Prog(nc)
        for i in range(NT):
            b = i % 2
            srow = (i % (S // 128)) * 128
            lw = [(s_st[b], s_st[b].v)] if i >= 2 else []
            p.add("sync", (lambda e, b=b, i=i: e.dma_start(
                qt[b][:, :], q_f[0:D, i * 128:(i + 1) * 128], transpose=True)),
                waits=lw, inc=s_ld[b], inc_n=16)
            p.add("sync", (lambda e, b=b, i=i: e.dma_start(
                kt[b][:, :], k_f[0:D, i * 128:(i + 1) * 128], transpose=True)),
                inc=s_ld[b], inc_n=16)
            for dst, tnm in [(rc[b], "rcq"), (rs[b], "rsq"), (rck[b], "rck"),
                             (rsk[b], "rsk")]:
                p.add("sync", (lambda e, dst=dst, tnm=tnm, srow=srow: e.dma_start(
                    dst[:, :], TAB[tnm][srow:srow + 128, :])), inc=s_ld[b], inc_n=16)
            vld = s_ld[b].v
            for src_t, ctab, stab, dst_t in [(qt, rc, rs, oq), (kt, rck, rsk, ok)]:
                st, dt, ct, sb_ = src_t[b], dst_t[b], ctab[b], stab[b]
                tmp3 = lambda par: bass.AP(tmp, par, [[D, 128], [64, 16], [2, 32]])
                dst3 = lambda par, dt=dt: bass.AP(dt, par, [[D, 128], [64, 16], [2, 32]])
                p.add("vector", (lambda e, st=st, ct=ct, tmp3=tmp3:
                                 e.tensor_tensor(tmp3(0), ev3(st, 0), tb3(ct), ALU.mult)),
                      waits=[(s_ld[b], vld)], inc=s_v)
                p.add("vector", (lambda e, st=st, ct=ct, tmp3=tmp3:
                                 e.tensor_tensor(tmp3(1), ev3(st, 1), tb3(ct), ALU.mult)),
                      inc=s_v)
                a3 = p.add("vector", (lambda e, st=st, sb_=sb_, dst3=dst3:
                                      e.tensor_tensor(dst3(0), ev3(st, 1), tb3(sb_), ALU.mult)),
                           inc=s_v)
                p.add("vector", (lambda e, dst3=dst3, tmp3=tmp3:
                                 e.tensor_tensor(dst3(0), tmp3(0), dst3(0), ALU.subtract)),
                      waits=[(s_v, a3)], inc=s_v)
                a5 = p.add("vector", (lambda e, st=st, sb_=sb_, dst3=dst3:
                                      e.tensor_tensor(dst3(1), ev3(st, 0), tb3(sb_), ALU.mult)),
                           inc=s_v)
                p.add("vector", (lambda e, dst3=dst3, tmp3=tmp3:
                                 e.tensor_tensor(dst3(1), tmp3(1), dst3(1), ALU.add)),
                      waits=[(s_v, a5)], inc=s_v)
            vdone = s_v.v
            p.add("scalar", (lambda e, b=b, i=i: e.dma_start(
                qr_t[ts(i, 128), :], oq[b][:, :])), waits=[(s_v, vdone)],
                inc=s_st[b], inc_n=16)
            p.add("scalar", (lambda e, b=b, i=i: e.dma_start(
                kr_t[ts(i, 128), :], ok[b][:, :])), inc=s_st[b], inc_n=16)
        p.add("scalar", lambda e: e.nop(), waits=[w(s_st[0]), w(s_st[1])])
        p.emit()


def emit_ret_attn(nc, ctx, G, qr_t, kr_t, v_f, ot_f, tag):
    with ExitStack() as pctx:
        sb = lambda *a, **k: pctx.enter_context(nc.sbuf_tensor(*a, **k))
        psum = lambda *a, **k: pctx.enter_context(nc.psum_tensor(*a, **k))
        qr = [sb(f"{tag}_qr{j}", [128, S], BF16, is_transpose_dest=True) for j in range(8)]
        kr = [sb(f"{tag}_kr{j}", [128, S], BF16, is_transpose_dest=True) for j in range(8)]
        vt = [sb(f"{tag}_vt{j}", [128, D], BF16, is_transpose_dest=True) for j in range(16)]
        ssb = [sb(f"{tag}_s{j}", [128, 512], BF16) for j in range(2)]
        sab = [sb(f"{tag}_a{j}", [128, 512], BF16) for j in range(2)]
        dsb = sb(f"{tag}_d", [1, 512], F32)
        dsb2 = sb(f"{tag}_d2", [1, 512], F32)
        rcp = [sb(f"{tag}_rc{j}", [1, 512], BF16) for j in range(2)]
        rbc = sb(f"{tag}_rbc", [128, 512], BF16)
        oev = [sb(f"{tag}_o{j}", [128, 512], BF16) for j in range(2)]
        psS = [psum(f"{tag}_pS{j}", [128, 512], F32) for j in range(2)]
        psAb = [psum(f"{tag}_pA{j}", [1, 512], F32) for j in range(2)]
        psO = [psum(f"{tag}_pO{j}", [128, 512], F32) for j in range(2)]
        psR = psum(f"{tag}_pR", [128, 512], F32)

        s_ld = nc._sb.get("sb_ld")
        s_mm = nc._sb.get("sb_mm")
        s_v = nc._sb.get("sb_v")
        s_a = nc._sb.get("sb_a")
        s_st = [nc._sb.get(f"sb_t{j}") for j in range(2)]

        p = Prog(nc)
        for b in range(B_CORE):
            tb0 = b * S
            ldw = [w(s_mm)] if b > 0 else []
            for j in range(8):
                p.add("sync", (lambda e, j=j, tb0=tb0: e.dma_start(
                    qr[j][:, :], qr_t[tb0:tb0 + S, j * 128:(j + 1) * 128],
                    transpose=True)), waits=ldw if j == 0 else (), inc=s_ld, inc_n=16)
                p.add("sync", (lambda e, j=j, tb0=tb0: e.dma_start(
                    kr[j][:, :], kr_t[tb0:tb0 + S, j * 128:(j + 1) * 128],
                    transpose=True)), inc=s_ld, inc_n=16)
            for j in range(16):
                p.add("sync", (lambda e, j=j, tb0=tb0: e.dma_start(
                    vt[j][:, :], v_f[0:D, tb0 + j * 128: tb0 + (j + 1) * 128],
                    transpose=True)), inc=s_ld, inc_n=16)
            vld = s_ld.v
            for hp in range(8):
                h = 2 * hp
                for qc in range(4):
                    kmax = 4 * qc + 4
                    ev_prev_o = s_st
                    for kc in range(kmax):
                        diag = kc - 4 * qc
                        sc = kc % 2
                        mmw = [(s_ld, vld)]
                        mmw.append((s_v, s_v.v))
                        vmm = p.add("tensor", (lambda e, hp=hp, kc=kc, qc=qc, sc=sc:
                                               e.matmul(psS[sc][:, :],
                                                        kr[hp][0:64, kc * 128:(kc + 1) * 128],
                                                        qr[hp][0:64, qc * 512:(qc + 1) * 512],
                                                        start=True, stop=True,
                                                        tile_position=(0, 0))),
                                    waits=mmw, inc=s_mm)
                        vmm = p.add("tensor", (lambda e, hp=hp, kc=kc, qc=qc, sc=sc:
                                               e.matmul(psS[sc ^ 1][:, :],
                                                        kr[hp][64:128, kc * 128:(kc + 1) * 128],
                                                        qr[hp][64:128, qc * 512:(qc + 1) * 512],
                                                        start=True, stop=True,
                                                        tile_position=(64, 0))),
                                    inc=s_mm)
                        for hh in range(2):
                            src_ps = psS[sc] if hh == 0 else psS[sc ^ 1]
                            if diag >= 0:
                                vv = p.add("vector", (lambda e, hh=hh, src_ps=src_ps, diag=diag:
                                                      e.tensor_tensor(ssb[hh][:, :], src_ps[:, :],
                                                                      G[f"trilq_{diag}"][:, :],
                                                                      ALU.mult)),
                                           waits=[(s_mm, vmm)], inc=s_v)
                            else:
                                vv = p.add("vector", (lambda e, hh=hh, src_ps=src_ps:
                                                      e.tensor_copy(ssb[hh][:, :], src_ps[:, :])),
                                           waits=[(s_mm, vmm)], inc=s_v)
                            vaa = p.add("scalar", (lambda e, hh=hh: e.activation(
                                sab[hh][:, :], ssb[hh][:, :], AF.Abs)),
                                waits=[(s_v, vv)], inc=s_a)
                            p.add("tensor", (lambda e, hh=hh, kc=kc:
                                             e.matmul(psAb[hh][:, :], G["e0col"][:, 0:1],
                                                      sab[hh][:, :], start=(kc == 0),
                                                      stop=(kc == kmax - 1))),
                                  waits=[(s_a, vaa)], inc=s_mm)
                            p.add("tensor", (lambda e, hh=hh, kc=kc, hp=hp, kmax=kmax:
                                             e.matmul(psO[hh][hh * 64:(hh + 1) * 64, :],
                                                      vt[kc][:, (2 * hp + hh) * 64:(2 * hp + hh + 1) * 64],
                                                      ssb[hh][:, :], start=(kc == 0),
                                                      stop=(kc == kmax - 1),
                                                      tile_position=(0, 64 * hh))),
                                  waits=[(s_v, vv)], inc=s_mm)
                    vmm_all = s_mm.v
                    for hh in range(2):
                        vv = p.add("vector", (lambda e, hh=hh: e.tensor_scalar(
                            dsb[0:1, :], psAb[hh][0:1, :], 1.0, 50000.0, ALU.max, ALU.min)),
                            waits=[(s_mm, vmm_all)], inc=s_v)
                        vv = p.add("vector", lambda e: e.reciprocal(dsb2[0:1, :], dsb[0:1, :]),
                                   waits=[(s_v, vv)], inc=s_v)
                        vv = p.add("vector", (lambda e, hh=hh: e.tensor_copy(
                            rcp[hh][0:1, :], dsb2[0:1, :])), waits=[(s_v, vv)], inc=s_v)
                    vrc = s_v.v
                    vbc = p.add("tensor", (lambda e: e.matmul(
                        psR[:, :], G["bc_lo"][0:1, :], rcp[0][0:1, :], start=True,
                        stop=False)), waits=[(s_v, vrc)], inc=s_mm)
                    vbc = p.add("tensor", (lambda e: e.matmul(
                        psR[:, :], G["bc_hi"][0:1, :], rcp[1][0:1, :], start=False,
                        stop=True)), inc=s_mm)
                    vv = p.add("vector", lambda e: e.tensor_copy(rbc[:, :], psR[:, :]),
                               waits=[(s_mm, vbc)], inc=s_v)
                    e2 = (hp * 4 + qc) % 2
                    stw = [(s_st[e2], s_st[e2].v)]
                    vo = p.add("vector", (lambda e, e2=e2: e.tensor_tensor(
                        oev[e2][0:64, :], psO[0][0:64, :], rbc[0:64, :], ALU.mult)),
                        waits=[(s_v, vv)] + stw, inc=s_v)
                    vo = p.add("vector", (lambda e, e2=e2: e.tensor_tensor(
                        oev[e2][64:128, :], psO[1][64:128, :], rbc[64:128, :],
                        ALU.mult)), waits=[(s_v, vo)], inc=s_v)
                    p.add("sync", (lambda e, e2=e2, hp=hp, qc=qc, tb0=tb0: e.dma_start(
                        ot_f[hp * 128:(hp + 1) * 128, tb0 + qc * 512: tb0 + (qc + 1) * 512],
                        oev[e2][:, :])), waits=[(s_v, vo)], inc=s_st[e2], inc_n=16)
        p.add("sync", lambda e: e.nop(), waits=[w(s_st[0]), w(s_st[1])])
        p.emit()


def emit_ret_final(nc, ctx, G, ot_f, g_f, s0_t, tag):
    with ExitStack() as pctx:
        sb = lambda *a, **k: pctx.enter_context(nc.sbuf_tensor(*a, **k))
        s_ld = [nc._sb.get(f"sb_l{j}") for j in range(2)]
        s_v = nc._sb.get("sb_v")
        s_a = nc._sb.get("sb_a")
        s_st = [nc._sb.get(f"sb_s{j}") for j in range(2)]
        ot = [sb(f"{tag}_o{j}", [128, D], BF16, is_transpose_dest=True) for j in range(2)]
        gt = [sb(f"{tag}_g{j}", [128, D], BF16, is_transpose_dest=True) for j in range(2)]
        sq = sb(f"{tag}_sq", [128, D], F32)
        sil = sb(f"{tag}_sil", [128, D], F32)
        grp = sb(f"{tag}_grp", [128, 16 * 4], F32)
        o1 = sb(f"{tag}_onorm", [128, D], F32)
        of = [sb(f"{tag}_of{j}", [128, D], F32) for j in range(2)]

        def g3(col):
            return bass.AP(grp, col * 16, [[64, 128], [1, 16]])

        p = Prog(nc)
        for i in range(NT):
            b = i % 2
            lw = [(s_st[b], s_st[b].v)] if i >= 2 else []
            v1 = p.add("sync", (lambda e, b=b, i=i: e.dma_start(
                ot[b][:, :], ot_f[0:D, i * 128:(i + 1) * 128], transpose=True)),
                waits=lw, inc=s_ld[b], inc_n=16)
            p.add("sync", (lambda e, b=b, i=i: e.dma_start(
                gt[b][:, :], g_f[0:D, i * 128:(i + 1) * 128], transpose=True)),
                inc=s_ld[b], inc_n=16)
            vld = s_ld[b].v
            va = p.add("scalar", (lambda e, b=b: e.activation(sq[:, :], ot[b][:, :],
                                                              AF.Square)),
                       waits=[(s_ld[b], vld)], inc=s_a)
            vv = p.add("vector", lambda e: e.tensor_reduce(
                g3(0), bass.AP(sq, 0, [[D, 128], [64, 16], [1, 64]]), AX.X, ALU.add),
                waits=[(s_a, va)], inc=s_v)
            vv = p.add("vector", lambda e: e.tensor_scalar(
                g3(1), g3(0), 1.0 / 64, 1e-6, ALU.mult, ALU.add),
                waits=[(s_v, vv)], inc=s_v)
            vv = p.add("vector", lambda e: e.reciprocal(g3(2), g3(1)),
                       waits=[(s_v, vv)], inc=s_v)
            va2 = p.add("scalar", lambda e: e.activation(g3(3), g3(2), AF.Sqrt),
                        waits=[(s_v, vv)], inc=s_a)
            va3 = p.add("scalar", (lambda e, b=b: e.activation(sil[:, :], gt[b][:, :],
                                                               AF.Silu)),
                        waits=[(s_ld[b], vld)], inc=s_a)
            vv = p.add("vector", (lambda e, b=b: e.tensor_tensor(
                bass.AP(o1, 0, [[D, 128], [64, 16], [1, 64]]),
                bass.AP(ot[b], 0, [[D, 128], [64, 16], [1, 64]]),
                bass.AP(grp, 48, [[64, 128], [1, 16], [0, 64]]), ALU.mult)),
                waits=[(s_a, va2)], inc=s_v)
            vv = p.add("vector", (lambda e, b=b: e.tensor_mul(of[b][:, :], o1[:, :],
                                                              sil[:, :])),
                       waits=[(s_v, vv), (s_a, va3)], inc=s_v)
            p.add("scalar", (lambda e, b=b, i=i: e.dma_start(
                s0_t[ts(i, 128), :], of[b][:, :])), waits=[(s_v, vv)],
                inc=s_st[b], inc_n=16)
        p.add("scalar", lambda e: e.nop(), waits=[w(s_st[0]), w(s_st[1])])
        p.emit()


PAD_S = S + 1


def pad_row(tok):
    b0, s0 = tok // S, tok % S
    return b0 * PAD_S + 1 + s0


def emit_zero_pad_rows(nc, ctx, G, dst_pad, tag):
    s_z = nc._sb.get("sb_z")
    p = Prog(nc)
    for b0 in range(B_CORE):
        p.add("sync", (lambda e, b0=b0: e.dma_start(
            dst_pad[b0 * PAD_S: b0 * PAD_S + 1, :], G["zrow"][0:1, :])),
            inc=s_z, inc_n=16)
    p.add("sync", lambda e: e.nop(), waits=[w(s_z)])
    p.emit()


def emit_ew_F(nc, ctx, G, tag, *, Ktot, src_a, src_g, dst, act=AF.Gelu,
              t_step=2048, a_off=0, g_off=0, ce_f=None, ce_row=0):
    with ExitStack() as pctx:
        sb = lambda *a, **k: pctx.enter_context(nc.sbuf_tensor(*a, **k))
        psum = lambda *a, **k: pctx.enter_context(nc.psum_tensor(*a, **k))
        s_ld = [nc._sb.get(f"sb_l{j}") for j in range(2)]
        s_v = nc._sb.get("sb_v")
        s_a = nc._sb.get("sb_a")
        s_mm = nc._sb.get("sb_mm")
        s_st = [nc._sb.get(f"sb_y{j}") for j in range(2)]
        at = [sb(f"{tag}_at{j}", [128, t_step], BF16) for j in range(2)]
        gt = [sb(f"{tag}_gt{j}", [128, t_step], BF16) for j in range(2)]
        ga = [sb(f"{tag}_ga{j}", [128, t_step], BF16) for j in range(2)]
        ot = [sb(f"{tag}_ot{j}", [128, t_step], BF16) for j in range(2)]
        if ce_f is not None:
            cesb = sb(f"{tag}_ce", [128, t_step], BF16)
            cerow = sb(f"{tag}_cer", [8, t_step], BF16)
            psce = psum(f"{tag}_pce", [128, t_step], F32)
        p = Prog(nc)
        kch = _chunks(Ktot, 128)
        tiles = [(k0, ks, t0) for (t0, _) in _chunks(N, t_step)
                 for (k0, ks) in kch]
        vce = 0
        for i, (k0, ks, t0) in enumerate(tiles):
            b = i % 2
            first_k = (i % len(kch) == 0)
            lw = [(s_st[b], s_st[b].v)] if i >= 2 else []
            if ce_f is not None and first_k:
                vr = p.add("sync", (lambda e, t0=t0: e.dma_start(
                    cerow[0:8, :], ce_f[0:8, t0:t0 + t_step])),
                    waits=[(s_v, s_v.v)], inc=s_ld[0], inc_n=16)
                vm = None
                for j in range(t_step // 512):
                    vm = p.add("tensor", (lambda e, j=j: e.matmul(
                        psce[:, j * 512:(j + 1) * 512],
                        G["sel_moe"][0:8, ce_row * 128:(ce_row + 1) * 128],
                        cerow[0:8, j * 512:(j + 1) * 512], start=True, stop=True)),
                        waits=[(s_ld[0], vr)] if j == 0 else (), inc=s_mm)
                vce = p.add("vector", (lambda e: e.tensor_copy(cesb[:, :], psce[:, :])),
                            waits=[(s_mm, vm)], inc=s_v)
            p.add("sync", (lambda e, b=b, k0=k0, ks=ks, t0=t0: e.dma_start(
                at[b][0:ks, :], src_a[a_off + k0:a_off + k0 + ks, t0:t0 + t_step])),
                waits=lw, inc=s_ld[b], inc_n=16)
            vld = p.add("sync", (lambda e, b=b, k0=k0, ks=ks, t0=t0: e.dma_start(
                gt[b][0:ks, :], src_g[g_off + k0:g_off + k0 + ks, t0:t0 + t_step])),
                inc=s_ld[b], inc_n=16)
            va = p.add("scalar", (lambda e, b=b, ks=ks: e.activation(
                ga[b][0:ks, :], gt[b][0:ks, :], act)),
                waits=[(s_ld[b], vld)], inc=s_a)
            vv = p.add("vector", (lambda e, b=b, ks=ks: e.tensor_mul(
                ot[b][0:ks, :], at[b][0:ks, :], ga[b][0:ks, :])),
                waits=[(s_a, va)] + ([(s_v, vce)] if ce_f is not None else []),
                inc=s_v)
            if ce_f is not None:
                vv = p.add("vector", (lambda e, b=b, ks=ks: e.tensor_mul(
                    ot[b][0:ks, :], ot[b][0:ks, :], cesb[0:ks, :])),
                    waits=[(s_v, vv)], inc=s_v)
            p.add("sync", (lambda e, b=b, k0=k0, ks=ks, t0=t0: e.dma_start(
                dst[k0:k0 + ks, t0:t0 + t_step], ot[b][0:ks, :])),
                waits=[(s_v, vv)], inc=s_st[b], inc_n=16)
        p.add("sync", lambda e: e.nop(), waits=[w(s_st[0]), w(s_st[1])])
        p.emit()


def emit_lffn_rot(nc, ctx, G, TAB, IN, l, qkv_f, qr_t, kr_t, qn_t, kn_t, tag):
    with ExitStack() as pctx:
        sb = lambda *a, **k: pctx.enter_context(nc.sbuf_tensor(*a, **k))
        ps = pctx.enter_context(nc.psum_tensor(f"{tag}_ps", [128, 512], F32))
        s_ld = [nc._sb.get(f"sb_l{j}") for j in range(2)]
        s_v = nc._sb.get("sb_v")
        s_a = nc._sb.get("sb_a")
        s_mm = nc._sb.get("sb_mm")
        s_st = [nc._sb.get(f"sb_s{j}") for j in range(2)]
        qt = [sb(f"{tag}_q{j}", [128, 512], BF16, is_transpose_dest=True) for j in range(2)]
        kt = [sb(f"{tag}_k{j}", [128, 512], BF16, is_transpose_dest=True) for j in range(2)]
        xc_ = [[sb(f"{tag}_xc{b}{j}", [128, 32], F32) for j in range(4)]
               for b in range(2)]
        sq = sb(f"{tag}_sq", [128, 512], F32)
        nrm = sb(f"{tag}_n", [128, 32], F32)
        qsr = sb(f"{tag}_qsr", [1, 64], BF16)
        ksr = sb(f"{tag}_ksr", [1, 64], BF16)
        qsb = sb(f"{tag}_qsb", [128, 64], F32)
        ksb = sb(f"{tag}_ksb", [128, 64], F32)
        t1 = sb(f"{tag}_t1", [128, 512], F32)
        t2 = sb(f"{tag}_t2", [128, 512], F32)
        oq = [sb(f"{tag}_oq{j}", [128, 512], BF16) for j in range(2)]
        okk = [sb(f"{tag}_okk{j}", [128, 512], BF16) for j in range(2)]
        nq = [sb(f"{tag}_nq{j}", [128, 8], F32) for j in range(2)]
        nk = [sb(f"{tag}_nk{j}", [128, 8], F32) for j in range(2)]

        def v3(t, off=0, inner=64):
            return bass.AP(t, off, [[512, 128], [64, 8], [1, inner]])

        def v3h(t, off):
            return bass.AP(t, off, [[512, 128], [64, 8], [1, 32]])

        def tabb(t):
            return bass.AP(t, 0, [[32, 128], [0, 8], [1, 32]])

        s_qs = nc._sb.get("sb_qs")
        p = Prog(nc)
        vq = p.add("gpsimd", (lambda e: e.dma_start(
            qsr[0:1, :], IN["L0_ffn_q_scale"][l:l + 1, :])), inc=s_qs, inc_n=16)
        vq = p.add("gpsimd", (lambda e: e.dma_start(
            ksr[0:1, :], IN["L0_ffn_k_scale"][l:l + 1, :])), inc=s_qs, inc_n=16)
        vmm = p.add("tensor", (lambda e: e.matmul(
            ps[:, 0:64], G["bc_all"][0:1, :], qsr[0:1, :], start=True, stop=True)),
            waits=[(s_qs, s_qs.v)], inc=s_mm)
        vmm = p.add("tensor", (lambda e: e.matmul(
            ps[:, 64:128], G["bc_all"][0:1, :], ksr[0:1, :], start=True, stop=True)),
            inc=s_mm)
        vv0 = p.add("vector", lambda e: e.tensor_copy(qsb[:, :], ps[:, 0:64]),
                    waits=[(s_mm, vmm)], inc=s_v)
        vv0 = p.add("vector", lambda e: e.tensor_copy(ksb[:, :], ps[:, 64:128]),
                    inc=s_v)

        for i in range(NT):
            b = i % 2
            srow = (i % (S // 128)) * 128
            lw = [(s_st[b], s_st[b].v)] if i >= 2 else []
            p.add("sync", (lambda e, b=b, i=i: e.dma_start(
                qt[b][:, :], qkv_f[0:512, i * 128:(i + 1) * 128], transpose=True)),
                waits=lw, inc=s_ld[b], inc_n=16)
            vld = p.add("sync", (lambda e, b=b, i=i: e.dma_start(
                kt[b][:, :], qkv_f[512:1024, i * 128:(i + 1) * 128], transpose=True)),
                inc=s_ld[b], inc_n=16)
            for j, tnm in enumerate(["xc_q", "xs_q", "xc_k", "xs_k"]):
                vld = p.add("sync", (lambda e, b=b, j=j, tnm=tnm, srow=srow: e.dma_start(
                    xc_[b][j][:, :], TAB[tnm][srow:srow + 128, :])), inc=s_ld[b], inc_n=16)
            for src_t, scb, ci, si, dst_t, nrm_t in [
                    (qt, qsb, 0, 1, oq, nq), (kt, ksb, 2, 3, okk, nk)]:
                st, dt, nt = src_t[b], dst_t[b], nrm_t[b]
                va = p.add("scalar", (lambda e, st=st: e.activation(
                    sq[:, :], st[:, :], AF.Square)),
                    waits=[(s_ld[b], vld), (s_v, s_v.v)], inc=s_a)
                vv = p.add("vector", (lambda e: e.tensor_reduce(
                    nrm[:, 0:8], v3(sq), AX.X, ALU.add)),
                    waits=[(s_a, va)], inc=s_v)
                va = p.add("scalar", (lambda e: e.activation(
                    nrm[:, 8:16], nrm[:, 0:8], AF.Sqrt)),
                    waits=[(s_v, vv)], inc=s_a)
                vv = p.add("vector", (lambda e: e.tensor_scalar(
                    nrm[:, 16:24], nrm[:, 8:16], 1e-12, None, ALU.max)),
                    waits=[(s_a, va)], inc=s_v)
                vv = p.add("vector", (lambda e: e.reciprocal(
                    nrm[:, 24:32], nrm[:, 16:24])), waits=[(s_v, vv)], inc=s_v)
                vv = p.add("vector", (lambda e, st=st: e.tensor_tensor(
                    v3(t1), v3(st), bass.AP(nrm, 24, [[32, 128], [1, 8], [0, 64]]),
                    ALU.mult)), waits=[(s_v, vv)], inc=s_v)
                vv = p.add("vector", (lambda e, scb=scb: e.tensor_tensor(
                    v3(t1), v3(t1), bass.AP(scb, 0, [[64, 128], [0, 8], [1, 64]]),
                    ALU.mult)), waits=[(s_v, vv)], inc=s_v)
                va = p.add("scalar", (lambda e: e.activation(
                    sq[:, :], t1[:, :], AF.Square)), waits=[(s_v, vv)], inc=s_a)
                vvn = p.add("vector", (lambda e: e.tensor_reduce(
                    nrm[:, 0:8], v3(sq), AX.X, ALU.add)),
                    waits=[(s_a, va)], inc=s_v)
                va = p.add("scalar", (lambda e, nt=nt: e.activation(
                    nt[:, :], nrm[:, 0:8], AF.Sqrt)), waits=[(s_v, vvn)], inc=s_a)
                vv = p.add("vector", (lambda e, b=b, ci=ci: e.tensor_tensor(
                    v3h(t2, 0), v3h(t1, 0), tabb(xc_[b][ci]), ALU.mult)),
                    waits=[(s_v, vv)], inc=s_v)
                vv = p.add("vector", (lambda e, b=b, ci=ci: e.tensor_tensor(
                    v3h(t2, 32), v3h(t1, 32), tabb(xc_[b][ci]), ALU.mult)),
                    waits=[(s_v, vv)], inc=s_v)
                vv = p.add("vector", (lambda e, b=b, si=si, dt=dt: e.tensor_tensor(
                    v3h(dt, 0), v3h(t1, 32), tabb(xc_[b][si]), ALU.mult)),
                    waits=[(s_v, vv)], inc=s_v)
                vv = p.add("vector", (lambda e, dt=dt: e.tensor_tensor(
                    v3h(dt, 0), v3h(t2, 0), v3h(dt, 0), ALU.subtract)),
                    waits=[(s_v, vv)], inc=s_v)
                vv = p.add("vector", (lambda e, b=b, si=si, dt=dt: e.tensor_tensor(
                    v3h(dt, 32), v3h(t1, 0), tabb(xc_[b][si]), ALU.mult)),
                    waits=[(s_v, vv)], inc=s_v)
                vv = p.add("vector", (lambda e, dt=dt: e.tensor_tensor(
                    v3h(dt, 32), v3h(t2, 32), v3h(dt, 32), ALU.add)),
                    waits=[(s_v, vv)], inc=s_v)
            vdone_v, vdone_a = s_v.v, s_a.v
            for dst_hbm, src_sb, wv in [(qr_t, oq[b], (s_v, vdone_v)),
                                        (kr_t, okk[b], (s_v, vdone_v))]:
                p.add("scalar", (lambda e, dst_hbm=dst_hbm, src_sb=src_sb, i=i:
                                 e.dma_start(dst_hbm[ts(i, 128), :], src_sb[:, :])),
                      waits=[wv], inc=s_st[b], inc_n=16)
            for dst_hbm, src_sb in [(qn_t, nq[b]), (kn_t, nk[b])]:
                p.add("scalar", (lambda e, dst_hbm=dst_hbm, src_sb=src_sb, i=i:
                                 e.dma_start(dst_hbm[ts(i, 128), :], src_sb[:, :])),
                      waits=[(s_a, vdone_a)], inc=s_st[b], inc_n=16)
        p.add("scalar", lambda e: e.nop(), waits=[w(s_st[0]), w(s_st[1])])
        p.emit()


def emit_lattn(nc, ctx, G, qr_t, kr_t, qkv_f, qn_t, kn_t, ot_f, tag):
    with ExitStack() as pctx:
        sb = lambda *a, **k: pctx.enter_context(nc.sbuf_tensor(*a, **k))
        psum = lambda *a, **k: pctx.enter_context(nc.psum_tensor(*a, **k))
        qr = [sb(f"{tag}_qr{j}", [128, S], BF16, is_transpose_dest=True) for j in range(4)]
        kr = [sb(f"{tag}_kr{j}", [128, S], BF16, is_transpose_dest=True) for j in range(4)]
        vt = [sb(f"{tag}_vt{j}", [128, 512], BF16, is_transpose_dest=True) for j in range(16)]
        nqt = [sb(f"{tag}_nq{j}", [128, 8], F32) for j in range(16)]
        nkt = [sb(f"{tag}_nk{j}", [128, 8], F32) for j in range(16)]
        qnf = sb(f"{tag}_qnf", [8, S], F32)
        knf = sb(f"{tag}_knf", [8, S], F32)
        wrk = sb(f"{tag}_wrk", [8, 16], F32)
        cbt = sb(f"{tag}_cbt", [8, 4], BF16)
        cb = sb(f"{tag}_cb", [128, 32], F32)
        cbp = sb(f"{tag}_cbp", [128, 32], F32)
        pp2 = [sb(f"{tag}_pq{j}", [128, 512], BF16) for j in range(2)]
        pp = [sb(f"{tag}_pp{j}", [128, 512], BF16) for j in range(2)]
        dsb = sb(f"{tag}_d", [1, 512], F32)
        rcp = [sb(f"{tag}_rc{j}", [1, 512], BF16) for j in range(2)]
        rbc = sb(f"{tag}_rbc", [128, 512], BF16)
        oev = [sb(f"{tag}_o{j}", [128, 512], BF16) for j in range(2)]
        psS = [psum(f"{tag}_pS{j}", [128, 512], F32) for j in range(2)]
        psAb = [psum(f"{tag}_pA{j}", [1, 512], F32) for j in range(2)]
        psO = [psum(f"{tag}_pO{j}", [128, 512], F32) for j in range(2)]
        psR = psum(f"{tag}_pR", [128, 512], F32)
        psTC = psum(f"{tag}_pT", [128, 512], F32)

        s_ld = nc._sb.get("sb_ld")
        s_mm = nc._sb.get("sb_mm")
        s_v = nc._sb.get("sb_v")
        s_a = nc._sb.get("sb_a")
        s_st = [nc._sb.get(f"sb_t{j}") for j in range(2)]

        p = Prog(nc)
        for b in range(B_CORE):
            tb0 = b * S
            ldw = [w(s_mm), w(s_v)] if b > 0 else []
            for j in range(4):
                p.add("sync", (lambda e, j=j, tb0=tb0: e.dma_start(
                    qr[j][:, :], qr_t[tb0:tb0 + S, j * 128:(j + 1) * 128],
                    transpose=True)), waits=ldw if j == 0 else (), inc=s_ld, inc_n=16)
                p.add("sync", (lambda e, j=j, tb0=tb0: e.dma_start(
                    kr[j][:, :], kr_t[tb0:tb0 + S, j * 128:(j + 1) * 128],
                    transpose=True)), inc=s_ld, inc_n=16)
            for j in range(16):
                p.add("sync", (lambda e, j=j, tb0=tb0: e.dma_start(
                    vt[j][:, :], qkv_f[1024:1536, tb0 + j * 128: tb0 + (j + 1) * 128],
                    transpose=True)), inc=s_ld, inc_n=16)
                p.add("sync", (lambda e, j=j, tb0=tb0: e.dma_start(
                    nqt[j][:, :], qn_t[tb0 + j * 128: tb0 + (j + 1) * 128, :])),
                    inc=s_ld, inc_n=16)
                p.add("sync", (lambda e, j=j, tb0=tb0: e.dma_start(
                    nkt[j][:, :], kn_t[tb0 + j * 128: tb0 + (j + 1) * 128, :])),
                    inc=s_ld, inc_n=16)
            vld = s_ld.v
            for j in range(16):
                for src_sb, dst_sb in [(nqt[j], qnf), (nkt[j], knf)]:
                    vmm = p.add("tensor", (lambda e, src_sb=src_sb: e.transpose(
                        psTC[0:8, 0:128], src_sb[:, :], G["ident_f32"][:, :])),
                        waits=[(s_ld, vld), (s_v, s_v.v)], inc=s_mm)
                    p.add("vector", (lambda e, dst_sb=dst_sb, j=j: e.tensor_copy(
                        dst_sb[0:8, j * 128:(j + 1) * 128], psTC[0:8, 0:128])),
                          waits=[(s_mm, vmm)], inc=s_v)
            vv = p.add("vector", (lambda e: e.tensor_reduce(
                wrk[0:8, 0:4], bass.AP(qnf, 0, [[S, 8], [512, 4], [1, 512]]),
                AX.X, ALU.max)), waits=[(s_v, s_v.v)], inc=s_v)
            vv = p.add("vector", (lambda e: e.tensor_reduce(
                wrk[0:8, 4:8], bass.AP(knf, 0, [[S, 8], [512, 4], [1, 512]]),
                AX.X, ALU.max)), waits=[(s_v, vv)], inc=s_v)
            vv = p.add("vector", lambda e: e.tensor_copy(wrk[0:8, 8:9], wrk[0:8, 4:5]),
                       waits=[(s_v, vv)], inc=s_v)
            vv = p.add("vector", lambda e: e.tensor_tensor(
                wrk[0:8, 9:12], wrk[0:8, 4:7], wrk[0:8, 5:8], ALU.max),
                waits=[(s_v, vv)], inc=s_v)
            vv = p.add("vector", lambda e: e.tensor_tensor(
                wrk[0:8, 12:16], wrk[0:8, 0:4], wrk[0:8, 8:12], ALU.mult),
                waits=[(s_v, vv)], inc=s_v)
            vv = p.add("vector", lambda e: e.tensor_scalar(
                cbt[0:8, 0:4], wrk[0:8, 12:16], 8.0, None, ALU.mult),
                waits=[(s_v, vv)], inc=s_v)
            vmm = None
            for h in range(8):
                for ww in range(4):
                    j = h * 4 + ww
                    vmm = p.add("tensor", (lambda e, h=h, ww=ww, j=j: e.matmul(
                        psTC[:, 128 + j:129 + j], G["sel_moe"][0:8, h * 128:(h + 1) * 128],
                        cbt[0:8, ww:ww + 1], start=(j == 0), stop=(j == 31))),
                        waits=[(s_v, vv)] if j == 0 else (), inc=s_mm)
            vv = p.add("vector", lambda e: e.tensor_scalar(
                cb[:, :], psTC[:, 128:160], -1.0, None, ALU.mult),
                waits=[(s_mm, vmm)], inc=s_v)
            vv = p.add("vector", lambda e: e.tensor_scalar(
                cbp[:, :], psTC[:, 128:160], 0.125, None, ALU.mult),
                waits=[(s_v, vv)], inc=s_v)
            vcb = vv
            for hp in range(4):
                for ww in range(4):
                    kcs = list(range(4, 8)) if ww == 0 else list(range(8))
                    for kc in kcs:
                        kcol = 512 * (ww - 1) + 128 * kc
                        sc = kc % 2
                        mmw = [(s_ld, vld), (s_v, s_v.v)]
                        p.add("tensor", (lambda e, hp=hp, kcol=kcol, ww=ww, sc=sc:
                                         e.matmul(psS[sc][:, :],
                                                  kr[hp][0:64, kcol:kcol + 128],
                                                  qr[hp][0:64, ww * 512:(ww + 1) * 512],
                                                  start=True, stop=True,
                                                  tile_position=(0, 0))),
                              waits=mmw, inc=s_mm)
                        vmm = p.add("tensor", (lambda e, hp=hp, kcol=kcol, ww=ww, sc=sc:
                                               e.matmul(psS[sc ^ 1][:, :],
                                                        kr[hp][64:128, kcol:kcol + 128],
                                                        qr[hp][64:128, ww * 512:(ww + 1) * 512],
                                                        start=True, stop=True,
                                                        tile_position=(64, 0))),
                                    inc=s_mm)
                        mtile = G[f"lamu_{kc}"] if kc < 4 else G[f"trilq_{kc - 4}"]
                        for hh in range(2):
                            h = hp * 2 + hh
                            src_ps = psS[sc] if hh == 0 else psS[sc ^ 1]
                            vv = p.add("vector", (lambda e, hh=hh, src_ps=src_ps,
                                                  h=h, ww=ww:
                                                  e.tensor_scalar(pp2[hh][:, :],
                                                                  src_ps[:, :],
                                                                  cbp[:, h * 4 + ww: h * 4 + ww + 1],
                                                                  None, ALU.min)),
                                       waits=[(s_mm, vmm), (s_v, vcb)], inc=s_v)
                            va = p.add("scalar", (lambda e, hh=hh, h=h, ww=ww:
                                                  e.activation(pp[hh][:, :], pp2[hh][:, :],
                                                               AF.Exp, scale=8.0,
                                                               bias=cb[:, h * 4 + ww: h * 4 + ww + 1])),
                                       waits=[(s_v, vv)], inc=s_a)
                            vv = p.add("vector", (lambda e, hh=hh, mtile=mtile:
                                                  e.tensor_mul(pp[hh][:, :], pp[hh][:, :],
                                                               mtile[:, :])),
                                       waits=[(s_a, va)], inc=s_v)
                            p.add("tensor", (lambda e, hh=hh, first=(kc == kcs[0]),
                                             last=(kc == kcs[-1]):
                                             e.matmul(psAb[hh][:, :], G["e0col"][:, 0:1],
                                                      pp[hh][:, :], start=first, stop=last)),
                                  waits=[(s_v, vv)], inc=s_mm)
                            p.add("tensor", (lambda e, hh=hh, h=h, ww=ww, kc=kc,
                                             first=(kc == kcs[0]), last=(kc == kcs[-1]):
                                             e.matmul(psO[hh][hh * 64:(hh + 1) * 64, :],
                                                      vt[(512 * (ww - 1) + 128 * kc) // 128]
                                                      [:, h * 64:(h + 1) * 64],
                                                      pp[hh][:, :], start=first, stop=last,
                                                      tile_position=(0, 64 * hh))),
                                  inc=s_mm)
                    vmm_all = s_mm.v
                    for hh in range(2):
                        vv = p.add("vector", (lambda e, hh=hh: e.tensor_scalar(
                            dsb[0:1, :], psAb[hh][0:1, :], 1e-30, None, ALU.max)),
                            waits=[(s_mm, vmm_all)], inc=s_v)
                        vv = p.add("vector", lambda e: e.reciprocal(dsb[0:1, :], dsb[0:1, :]),
                                   waits=[(s_v, vv)], inc=s_v)
                        vv = p.add("vector", (lambda e, hh=hh: e.tensor_copy(
                            rcp[hh][0:1, :], dsb[0:1, :])), waits=[(s_v, vv)], inc=s_v)
                    vrc = s_v.v
                    vbc = p.add("tensor", (lambda e: e.matmul(
                        psR[:, :], G["bc_lo"][0:1, :], rcp[0][0:1, :], start=True,
                        stop=False)), waits=[(s_v, vrc)], inc=s_mm)
                    vbc = p.add("tensor", (lambda e: e.matmul(
                        psR[:, :], G["bc_hi"][0:1, :], rcp[1][0:1, :], start=False,
                        stop=True)), inc=s_mm)
                    vv = p.add("vector", lambda e: e.tensor_copy(rbc[:, :], psR[:, :]),
                               waits=[(s_mm, vbc)], inc=s_v)
                    e2 = (hp * 4 + ww) % 2
                    stw = [(s_st[e2], s_st[e2].v)]
                    vo = p.add("vector", (lambda e, e2=e2: e.tensor_tensor(
                        oev[e2][0:64, :], psO[0][0:64, :], rbc[0:64, :], ALU.mult)),
                        waits=[(s_v, vv)] + stw, inc=s_v)
                    vo = p.add("vector", (lambda e, e2=e2: e.tensor_tensor(
                        oev[e2][64:128, :], psO[1][64:128, :], rbc[64:128, :],
                        ALU.mult)), waits=[(s_v, vo)], inc=s_v)
                    p.add("sync", (lambda e, e2=e2, hp=hp, ww=ww, tb0=tb0: e.dma_start(
                        ot_f[hp * 128:(hp + 1) * 128, tb0 + ww * 512: tb0 + (ww + 1) * 512],
                        oev[e2][:, :])), waits=[(s_v, vo)], inc=s_st[e2], inc_n=16)
        p.add("sync", lambda e: e.nop(), waits=[w(s_st[0]), w(s_st[1])])
        p.emit()


def emit_combine(nc, ctx, G, tag, *, srcs, dst_f32=None, dst_pad=None,
                 dst_bf=None, mode="resid"):
    with ExitStack() as pctx:
        sb = lambda *a, **k: pctx.enter_context(nc.sbuf_tensor(*a, **k))
        s_ld = [nc._sb.get(f"sb_l{j}") for j in range(2)]
        s_v = nc._sb.get("sb_v")
        s_a = nc._sb.get("sb_a")
        s_st = [nc._sb.get(f"sb_s{j}") for j in range(2)]
        at = [sb(f"{tag}_a{j}", [128, D], F32) for j in range(2)]
        bt = [sb(f"{tag}_b{j}", [128, D], BF16, is_transpose_dest=True) for j in range(2)]
        gt = [sb(f"{tag}_g{j}", [128, D], F32) for j in range(2)]
        sil2 = [sb(f"{tag}_sil{j}", [128, D], F32) for j in range(2)]
        st2 = [sb(f"{tag}_si{j}", [128, D], F32) for j in range(2)]
        of = [sb(f"{tag}_of{j}", [128, D], F32) for j in range(2)]
        ob = [sb(f"{tag}_ob{j}", [128, D], BF16) for j in range(2)]
        nst = (dst_f32 is not None) + (dst_pad is not None) + (dst_bf is not None)
        p = Prog(nc)
        for i in range(NT):
            b = i % 2
            lw = [(s_st[b], s_st[b].v)] if i >= 2 else []
            p.add("sync", (lambda e, b=b, i=i: e.dma_start(
                at[b][:, :], srcs["a"][ts(i, 128), :])), waits=lw, inc=s_ld[b], inc_n=16)
            if mode == "resid":
                vld = p.add("sync", (lambda e, b=b, i=i: e.dma_start(
                    bt[b][:, :], srcs["b"][0:D, i * 128:(i + 1) * 128],
                    transpose=True)), inc=s_ld[b], inc_n=16)
                vv = p.add("vector", (lambda e, b=b: e.tensor_add(
                    of[b][:, :], at[b][:, :], bt[b][:, :])),
                    waits=[(s_ld[b], vld)], inc=s_v)
            else:
                if "g_f" in srcs:
                    p.add("sync", (lambda e, b=b, i=i: e.dma_start(
                        bt[b][:, :], srcs["g_f"][0:D, i * 128:(i + 1) * 128],
                        transpose=True)), inc=s_ld[b], inc_n=16)
                else:
                    p.add("sync", (lambda e, b=b, i=i: e.dma_start(
                        gt[b][:, :], srcs["g"][ts(i, 128), :])), inc=s_ld[b], inc_n=16)
                vld = p.add("sync", (lambda e, b=b, i=i: e.dma_start(
                    st2[b][:, :], srcs["s"][ts(i, 128), :])), inc=s_ld[b], inc_n=16)
                va = p.add("scalar", (lambda e, b=b, gf=("g_f" in srcs): e.activation(
                    sil2[b][:, :], bt[b][:, :] if gf else gt[b][:, :], AF.Silu)),
                    waits=[(s_ld[b], vld), w(s_v)], inc=s_a)
                vv = p.add("vector", (lambda e, b=b: e.tensor_mul(
                    sil2[b][:, :], sil2[b][:, :], st2[b][:, :])), waits=[(s_a, va)], inc=s_v)
                vv = p.add("vector", (lambda e, b=b: e.tensor_add(
                    of[b][:, :], at[b][:, :], sil2[b][:, :])), waits=[(s_v, vv)], inc=s_v)
            if dst_pad is not None or dst_bf is not None:
                vv2 = p.add("vector", (lambda e, b=b: e.tensor_copy(
                    ob[b][:, :], of[b][:, :])), waits=[(s_v, vv)], inc=s_v)
            else:
                vv2 = vv
            if dst_f32 is not None:
                p.add("scalar", (lambda e, b=b, i=i: e.dma_start(
                    dst_f32[ts(i, 128), :], of[b][:, :])), waits=[(s_v, vv2)],
                    inc=s_st[b], inc_n=16)
            if dst_pad is not None:
                pr = pad_row(i * 128)
                p.add("scalar", (lambda e, b=b, pr=pr: e.dma_start(
                    dst_pad[pr:pr + 128, :], ob[b][:, :])),
                    waits=[(s_v, vv2)] if dst_f32 is None else (),
                    inc=s_st[b], inc_n=16)
            if dst_bf is not None:
                p.add("scalar", (lambda e, b=b, i=i: e.dma_start(
                    dst_bf[ts(i, 128), :], ob[b][:, :])),
                    waits=[(s_v, vv2)] if (dst_f32 is None and dst_pad is None) else (),
                    inc=s_st[b], inc_n=16)
        p.add("scalar", lambda e: e.nop(), waits=[w(s_st[0]), w(s_st[1])])
        p.emit()


NCH = S // 128


def emit_gla_prep(nc, ctx, G, q_f, k_f, z_f, qa_t, kd_t, kd2_t, ce63r, ce127r, tag):
    with ExitStack() as pctx:
        sb = lambda *a, **k: pctx.enter_context(nc.sbuf_tensor(*a, **k))
        psum = lambda *a, **k: pctx.enter_context(nc.psum_tensor(*a, **k))
        s_ld = [nc._sb.get(f"sb_l{j}") for j in range(2)]
        s_v = nc._sb.get("sb_v")
        s_a = nc._sb.get("sb_a")
        s_mm = nc._sb.get("sb_mm")
        s_st = [nc._sb.get(f"sb_s{j}") for j in range(2)]
        s_row = [nc._sb.get(f"sb_r{j}") for j in range(2)]
        qt = [sb(f"{tag}_q{j}", [128, D], BF16, is_transpose_dest=True) for j in range(2)]
        kt = [sb(f"{tag}_k{j}", [128, D], BF16, is_transpose_dest=True) for j in range(2)]
        zt = [sb(f"{tag}_z{j}", [128, D], BF16, is_transpose_dest=True) for j in range(2)]
        sq = sb(f"{tag}_sq", [128, D], F32)
        sig = sb(f"{tag}_sig", [128, D], F32)
        cum = sb(f"{tag}_cum", [128, D], F32)
        e63 = sb(f"{tag}_e63", [128, D], F32)
        e63i = sb(f"{tag}_e63i", [128, D], F32)
        e127 = sb(f"{tag}_e127", [128, D], F32)
        rowb = [sb(f"{tag}_rowb{j}", [1, D], BF16) for j in range(2)]
        nrm = sb(f"{tag}_n", [128, 64], F32)
        oqa = [sb(f"{tag}_oqa{j}", [128, D], BF16) for j in range(2)]
        okd = [sb(f"{tag}_okd{j}", [128, D], BF16) for j in range(2)]
        okd2 = [sb(f"{tag}_okd2{j}", [128, D], BF16) for j in range(2)]
        psC = psum(f"{tag}_pc", [128, 1024], F32)
        psB = psum(f"{tag}_pb", [128, 1024], F32)

        def v3(t):
            return bass.AP(t, 0, [[D, 128], [64, 16], [1, 64]])

        p = Prog(nc)
        for i in range(NT):
            b = i % 2
            bi, ch = i // NCH, i % NCH
            lw = [(s_st[b], s_st[b].v)] if i >= 2 else []
            p.add("sync", (lambda e, b=b, i=i: e.dma_start(
                qt[b][:, :], q_f[0:D, i * 128:(i + 1) * 128], transpose=True)),
                waits=lw, inc=s_ld[b], inc_n=16)
            p.add("sync", (lambda e, b=b, i=i: e.dma_start(
                kt[b][:, :], k_f[0:D, i * 128:(i + 1) * 128], transpose=True)),
                inc=s_ld[b], inc_n=16)
            vld = p.add("sync", (lambda e, b=b, i=i: e.dma_start(
                zt[b][:, :], z_f[0:D, i * 128:(i + 1) * 128], transpose=True)),
                inc=s_ld[b], inc_n=16)
            va = p.add("scalar", (lambda e, b=b: e.activation(
                sig[:, :], zt[b][:, :], AF.Sigmoid)),
                waits=[(s_ld[b], vld), (s_v, s_v.v)], inc=s_a)
            va = p.add("scalar", (lambda e: e.activation(
                sig[:, :], sig[:, :], AF.Ln)), waits=[(s_a, va)], inc=s_a)
            vmm = p.add("tensor", (lambda e: e.matmul(
                psC[:, 0:512], G["ucum"][:, :], sig[:, 0:512], start=True, stop=True)),
                waits=[(s_a, va)], inc=s_mm)
            vmm = p.add("tensor", (lambda e: e.matmul(
                psC[:, 512:1024], G["ucum"][:, :], sig[:, 512:1024], start=True,
                stop=True)), inc=s_mm)
            vv = p.add("vector", lambda e: e.tensor_copy(cum[:, :], psC[:, :]),
                       waits=[(s_mm, vmm)], inc=s_v)
            vmm = p.add("tensor", (lambda e: e.matmul(
                psB[:, 0:512], G["sel63"][:, :], cum[:, 0:512], start=True, stop=True)),
                waits=[(s_v, vv)], inc=s_mm)
            vmm = p.add("tensor", (lambda e: e.matmul(
                psB[:, 512:1024], G["sel63"][:, :], cum[:, 512:1024], start=True,
                stop=True)), inc=s_mm)
            vv = p.add("vector", (lambda e: e.tensor_sub(e63[:, :], psB[:, :], cum[:, :])),
                       waits=[(s_mm, vmm)], inc=s_v)
            va = p.add("scalar", (lambda e: e.activation(e63i[:, :], e63[:, :],
                                                         AF.Exp, scale=-1.0)),
                       waits=[(s_v, vv)], inc=s_a)
            va = p.add("scalar", (lambda e: e.activation(e63[:, :], e63[:, :], AF.Exp)),
                       waits=[(s_a, va)], inc=s_a)
            va2 = p.add("scalar", (lambda e: e.activation(sq[:, :], psB[:, :],
                                                          AF.Exp)),
                        waits=[(s_a, va)], inc=s_a)
            vv = p.add("vector", (lambda e, b=b: e.tensor_copy(rowb[b][0:1, :], sq[0:1, :])),
                       waits=[(s_a, va2), (s_row[b], s_row[b].v)], inc=s_v)
            vrow = p.add("scalar", (lambda e, b=b, bi=bi, ch=ch: e.dma_start(
                ce63r[bi * NCH + ch: bi * NCH + ch + 1, :], rowb[b][0:1, :])),
                waits=[(s_v, vv)], inc=s_row[b], inc_n=16)
            vmm = p.add("tensor", (lambda e: e.matmul(
                psB[:, 0:512], G["sel127"][:, :], cum[:, 0:512], start=True,
                stop=True)), waits=[(s_v, vv), (s_a, va2)], inc=s_mm)
            vmm = p.add("tensor", (lambda e: e.matmul(
                psB[:, 512:1024], G["sel127"][:, :], cum[:, 512:1024], start=True,
                stop=True)), inc=s_mm)
            vv = p.add("vector", (lambda e: e.tensor_sub(e127[:, :], psB[:, :], cum[:, :])),
                       waits=[(s_mm, vmm)], inc=s_v)
            va3 = p.add("scalar", (lambda e: e.activation(e127[:, :], e127[:, :], AF.Exp)),
                        waits=[(s_v, vv)], inc=s_a)
            va3b = p.add("scalar", (lambda e: e.activation(sq[:, :], psB[:, :],
                                                           AF.Exp)),
                         waits=[(s_a, va3)], inc=s_a)
            vv = p.add("vector", (lambda e, b=b: e.tensor_copy(rowb[b][0:1, :], sq[0:1, :])),
                       waits=[(s_a, va3b), (s_row[b], s_row[b].v)], inc=s_v)
            p.add("scalar", (lambda e, b=b, bi=bi, ch=ch: e.dma_start(
                ce127r[bi * NCH + ch: bi * NCH + ch + 1, :], rowb[b][0:1, :])),
                waits=[(s_v, vv)], inc=s_row[b], inc_n=16)
            for src_t, dst_list in [(qt, [("qa", oqa, e63i)]),
                                    (kt, [("kd", okd, e63), ("kd2", okd2, e127)])]:
                st = src_t[b]
                va4 = p.add("scalar", (lambda e, st=st: e.activation(
                    sq[:, :], st[:, :], AF.Square)), waits=[(s_v, s_v.v)], inc=s_a)
                vv = p.add("vector", (lambda e: e.tensor_reduce(
                    nrm[:, 0:16], bass.AP(sq, 0, [[D, 128], [64, 16], [1, 64]]),
                    AX.X, ALU.add)), waits=[(s_a, va4)], inc=s_v)
                va4 = p.add("scalar", (lambda e: e.activation(
                    nrm[:, 16:32], nrm[:, 0:16], AF.Sqrt)), waits=[(s_v, vv)], inc=s_a)
                vv = p.add("vector", (lambda e: e.tensor_scalar(
                    nrm[:, 32:48], nrm[:, 16:32], 1e-12, None, ALU.max)),
                    waits=[(s_a, va4)], inc=s_v)
                vv = p.add("vector", (lambda e: e.reciprocal(
                    nrm[:, 48:64], nrm[:, 32:48])), waits=[(s_v, vv)], inc=s_v)
                vv = p.add("vector", (lambda e, st=st: e.tensor_tensor(
                    v3(sq), v3(st), bass.AP(nrm, 48, [[64, 128], [1, 16], [0, 64]]),
                    ALU.mult)), waits=[(s_v, vv)], inc=s_v)
                for _, dst_t, fac in dst_list:
                    vv = p.add("vector", (lambda e, dst_t=dst_t, fac=fac, b=b:
                                          e.tensor_mul(dst_t[b][:, :], sq[:, :],
                                                       fac[:, :])),
                               waits=[(s_v, vv)], inc=s_v)
            vdone = s_v.v
            for dst_hbm, src_list in [(qa_t, oqa), (kd_t, okd), (kd2_t, okd2)]:
                p.add("scalar", (lambda e, dst_hbm=dst_hbm, src_list=src_list,
                                 b=b, i=i: e.dma_start(
                    dst_hbm[ts(i, 128), :], src_list[b][:, :])),
                    waits=[(s_v, vdone)], inc=s_st[b], inc_n=16)
        p.add("scalar", lambda e: e.nop(),
              waits=[w(s_st[0]), w(s_st[1]), w(s_row[0]), w(s_row[1])])
        p.emit()


def emit_gla_scan(nc, ctx, G, qa_t, kd_t, kd2_t, v_f, ce63r, ce127r, og_f, tag):
    import os as _os
    CUT = set((_os.environ.get("GS_CUT") or "").split(","))
    SPACE = "space" in CUT
    MAXC = int(_os.environ.get("GS_MAXC") or NCH)
    MAXHP = int(_os.environ.get("GS_MAXHP") or 8)
    MAXB = int(_os.environ.get("GS_MAXB") or B_CORE)
    with ExitStack() as pctx:
        sb = lambda *a, **k: pctx.enter_context(nc.sbuf_tensor(*a, **k))
        psum = lambda *a, **k: pctx.enter_context(nc.psum_tensor(*a, **k))
        s_ld = nc._sb.get("sb_ld")
        s_ld2 = [nc._sb.get(f"sb_l{j}") for j in range(2)]
        s_v = nc._sb.get("sb_v")
        s_mm = nc._sb.get("sb_mm")
        s_st = [nc._sb.get(f"sb_y{j}") for j in range(2)]
        c63 = [sb(f"{tag}_c63_{hp}", [128, NCH], F32) for hp in range(8)]
        c127 = [sb(f"{tag}_c127_{hp}", [128, NCH], F32) for hp in range(8)]
        c63b = [sb(f"{tag}_c63b_{hp}", [128, NCH], BF16, is_transpose_dest=True)
                for hp in range(8)]
        c127b = [sb(f"{tag}_c127b_{hp}", [128, NCH], BF16, is_transpose_dest=True)
                 for hp in range(8)]
        qa = [sb(f"{tag}_qa{j}", [128, 512], BF16, is_transpose_dest=True)
              for j in range(16)]
        kd = [sb(f"{tag}_kd{j}", [128, 512], BF16, is_transpose_dest=True)
              for j in range(16)]
        kd2 = [sb(f"{tag}_k2{j}", [128, D], BF16) for j in range(2)]
        vt = [sb(f"{tag}_vt{j}", [128, D], BF16, is_transpose_dest=True) for j in range(2)]
        Sf = [sb(f"{tag}_Sf{hp}", [128, 64], F32) for hp in range(8)]
        Sb = [sb(f"{tag}_Sb{hp}", [128, 64], BF16) for hp in range(8)]
        Spp = sb(f"{tag}_Spp", [128, 64], F32)
        Sppb = sb(f"{tag}_Sppb", [128, 64], BF16)
        smx = [sb(f"{tag}_sm{j}", [128, 128], BF16) for j in range(2)]
        oev = [sb(f"{tag}_oev{j}", [128, 128], BF16) for j in range(2)]
        psS = [psum(f"{tag}_pS{j}", [128, 512], F32) for j in range(2)]
        psO = [[psum(f"{tag}_pO{j}{h}", [128, 512], F32) for h in range(2)]
               for j in range(2)]
        psA1 = [psum(f"{tag}_pA{h}", [128, 512], F32) for h in range(2)]
        psA = [psA1, psA1]

        p = Prog(nc, space=SPACE)
        for b in range(MAXB):
            tb0 = b * S
            ldw = [w(s_mm), w(s_v)] if b > 0 else []
            if "noce" not in CUT:
                for hp in range(8):
                    v1 = p.add("sync", (lambda e, hp=hp, b=b: e.dma_start(
                        c63b[hp][:, :], ce63r[b * NCH:(b + 1) * NCH,
                                              hp * 128:(hp + 1) * 128], transpose=True)),
                        waits=ldw if hp == 0 else (), inc=s_ld2[0], inc_n=16)
                    v1 = p.add("sync", (lambda e, hp=hp, b=b: e.dma_start(
                        c127b[hp][:, :], ce127r[b * NCH:(b + 1) * NCH,
                                                hp * 128:(hp + 1) * 128], transpose=True)),
                        inc=s_ld2[0], inc_n=16)
                vce = s_ld2[0].v
                for hp in range(8):
                    p.add("vector", (lambda e, hp=hp: e.tensor_copy(c63[hp][:, :],
                                                                    c63b[hp][:, :])),
                          waits=[(s_ld2[0], vce)] if hp == 0 else (), inc=s_v)
                    p.add("vector", (lambda e, hp=hp: e.tensor_copy(c127[hp][:, :],
                                                                    c127b[hp][:, :])),
                          inc=s_v)
            else:
                for hp in range(8):
                    p.add("vector", (lambda e, hp=hp: e.memset(c63[hp][:, :], 0.5)),
                          waits=ldw if hp == 0 else (), inc=s_v)
                    p.add("vector", (lambda e, hp=hp: e.memset(c127[hp][:, :], 0.5)),
                          inc=s_v)
            vcec = s_v.v
            for hp in range(8):
                p.add("vector", (lambda e, hp=hp: e.memset(Sf[hp][:, :], 0.0)), inc=s_v)
                p.add("vector", (lambda e, hp=hp: e.memset(Sb[hp][:, :], 0.0)), inc=s_v)
            for c in range(MAXC):
                cb2 = c % 2
                g4 = c // 4
                if c % 4 == 0 and "noqa" not in CUT:
                    gw = [w(s_mm)] if (c > 0 or b > 0) else []
                    for hp in range(8):
                        p.add("sync", (lambda e, hp=hp, g4=g4, tb0=tb0: e.dma_start(
                            qa[hp * 2 + g4 % 2][:, :],
                            qa_t[tb0 + g4 * 512: tb0 + (g4 + 1) * 512,
                                 hp * 128:(hp + 1) * 128], transpose=True)),
                            waits=gw if hp == 0 else (), inc=s_ld2[1], inc_n=16)
                        p.add("sync", (lambda e, hp=hp, g4=g4, tb0=tb0: e.dma_start(
                            kd[hp * 2 + g4 % 2][:, :],
                            kd_t[tb0 + g4 * 512: tb0 + (g4 + 1) * 512,
                                 hp * 128:(hp + 1) * 128], transpose=True)),
                            inc=s_ld2[1], inc_n=16)
                vqa = s_ld2[1].v
                if "nokv" not in CUT:
                    lw2 = [(s_mm, s_mm.v)] if (c >= 2 or b > 0) else []
                    p.add("sync", (lambda e, c=c, cb2=cb2, tb0=tb0: e.dma_start(
                        kd2[cb2][:, :], kd2_t[tb0 + c * 128: tb0 + (c + 1) * 128, :])),
                        waits=lw2, inc=s_ld2[1], inc_n=16)
                    vk2 = p.add("sync", (lambda e, c=c, cb2=cb2, tb0=tb0: e.dma_start(
                        vt[cb2][:, :], v_f[0:D, tb0 + c * 128: tb0 + (c + 1) * 128],
                        transpose=True)), inc=s_ld2[1], inc_n=16)
                else:
                    vk2 = s_ld2[1].v
                for hp in range(0 if "nobody" in CUT else MAXHP):
                    gi = hp * 2 + g4 % 2
                    coff = (c % 4) * 128
                    mw = [(s_ld2[1], vk2), (s_v, s_v.v)]
                    p.add("tensor", (lambda e, gi=gi, coff=coff:
                                     e.matmul(psS[0][:, 0:128],
                                              kd[gi][0:64, coff:coff + 128],
                                              qa[gi][0:64, coff:coff + 128],
                                              start=True, stop=True,
                                              tile_position=(0, 0))),
                          waits=mw, inc=s_mm)
                    vmm = p.add("tensor", (lambda e, gi=gi, coff=coff:
                                           e.matmul(psS[1][:, 0:128],
                                                    kd[gi][64:128, coff:coff + 128],
                                                    qa[gi][64:128, coff:coff + 128],
                                                    start=True, stop=True,
                                                    tile_position=(64, 0))),
                                inc=s_mm)
                    if "nospp" not in CUT:
                        vv = p.add("vector", (lambda e, hp=hp, c=c: e.tensor_scalar(
                            Spp[:, :], Sf[hp][:, :], c63[hp][:, c:c + 1], None, ALU.mult)),
                            waits=[(s_v, s_v.v), (s_mm, s_mm.v)], inc=s_v)
                        vsp = p.add("vector", (lambda e: e.tensor_copy(Sppb[:, :], Spp[:, :])),
                                   waits=[(s_v, vv)], inc=s_v)
                    else:
                        vsp = s_v.v
                    vs0 = p.add("vector", (lambda e: e.tensor_tensor(
                        smx[0][:, :], psS[0][:, 0:128], G["tril_inc"][:, :],
                        ALU.mult)), waits=[(s_mm, vmm)], inc=s_v)
                    vs1 = p.add("vector", (lambda e: e.tensor_tensor(
                        smx[1][:, :], psS[1][:, 0:128], G["tril_inc"][:, :],
                        ALU.mult)), waits=[(s_v, vs0)], inc=s_v)
                    nointer = "nointer" in CUT
                    nopv = "nopv" in CUT
                    if not nopv:
                        p.add("tensor", (lambda e, cb2=cb2, hp=hp, st=nointer: e.matmul(
                            psO[cb2][0][0:64, 0:128], vt[cb2][:, hp * 128: hp * 128 + 64],
                            smx[0][:, :], start=True, stop=st, tile_position=(0, 0))),
                            waits=[(s_v, vs1)], inc=s_mm)
                        vo = p.add("tensor", (lambda e, cb2=cb2, hp=hp, st=nointer: e.matmul(
                            psO[cb2][1][64:128, 0:128], vt[cb2][:, hp * 128 + 64: hp * 128 + 128],
                            smx[1][:, :], start=True, stop=st, tile_position=(0, 64))),
                            inc=s_mm)
                    else:
                        vo = s_mm.v
                    if not nointer and not nopv:
                        p.add("tensor", (lambda e, cb2=cb2, gi=gi, coff=coff: e.matmul(
                            psO[cb2][0][0:64, 0:128], Sppb[0:64, :],
                            qa[gi][0:64, coff:coff + 128], start=False, stop=True,
                            tile_position=(0, 0))),
                            waits=[(s_v, vsp)], inc=s_mm)
                        vo = p.add("tensor", (lambda e, cb2=cb2, gi=gi, coff=coff: e.matmul(
                            psO[cb2][1][64:128, 0:128], Sppb[64:128, :],
                            qa[gi][64:128, coff:coff + 128], start=False, stop=True,
                            tile_position=(64, 64))),
                            inc=s_mm)
                    if "nosadd" in CUT:
                        vsa = s_mm.v
                    vsa = s_mm.v if "nosadd" in CUT else p.add("tensor", (lambda e, cb2=cb2, hp=hp: e.matmul(
                        psA[cb2][0][0:64, 0:64], kd2[cb2][:, hp * 128: hp * 128 + 64],
                        vt[cb2][:, hp * 128: hp * 128 + 64], start=True, stop=True,
                        tile_position=(0, 0))), inc=s_mm)
                    if "nosadd" not in CUT:
                        vsa = p.add("tensor", (lambda e, cb2=cb2, hp=hp: e.matmul(
                            psA[cb2][1][64:128, 0:64], kd2[cb2][:, hp * 128 + 64: hp * 128 + 128],
                            vt[cb2][:, hp * 128 + 64: hp * 128 + 128], start=True, stop=True,
                            tile_position=(0, 64))), inc=s_mm)
                        vv = p.add("vector", (lambda e, hp=hp, c=c, cb2=cb2:
                                              e.scalar_tensor_tensor(
                            Sf[hp][0:64, :], Sf[hp][0:64, :], c127[hp][0:64, c:c + 1],
                            psA[cb2][0][0:64, 0:64], ALU.mult, ALU.add)),
                            waits=[(s_mm, vsa), (s_v, vcec)], inc=s_v)
                        vv = p.add("vector", (lambda e, hp=hp, c=c, cb2=cb2:
                                              e.scalar_tensor_tensor(
                            Sf[hp][64:128, :], Sf[hp][64:128, :], c127[hp][64:128, c:c + 1],
                            psA[cb2][1][64:128, 0:64], ALU.mult, ALU.add)),
                            waits=[(s_v, vv)], inc=s_v)
                        vv = p.add("vector", (lambda e, hp=hp: e.tensor_copy(
                            Sb[hp][:, :], Sf[hp][:, :])), waits=[(s_v, vv)], inc=s_v)
                    e2 = hp % 2
                    if nopv:
                        continue
                    vo2 = p.add("vector", (lambda e, cb2=cb2, e2=e2: e.tensor_copy(
                        oev[e2][0:64, :], psO[cb2][0][0:64, 0:128])),
                        waits=[(s_mm, vo), (s_st[e2], s_st[e2].v)], inc=s_v)
                    vo2 = p.add("vector", (lambda e, cb2=cb2, e2=e2: e.tensor_copy(
                        oev[e2][64:128, :], psO[cb2][1][64:128, 0:128])),
                        waits=[(s_v, vo2)], inc=s_v)
                    if "nost" not in CUT:
                        p.add("sync", (lambda e, hp=hp, c=c, tb0=tb0, e2=e2: e.dma_start(
                            og_f[hp * 128:(hp + 1) * 128, tb0 + c * 128: tb0 + (c + 1) * 128],
                            oev[e2][:, :])), waits=[(s_v, vo2)], inc=s_st[e2], inc_n=16)
        p.add("sync", lambda e: e.nop(), waits=[w(s_st[0]), w(s_st[1])])
        p.emit()


def emit_moe_cast(nc, ctx, G, w1_in, w2_in, w1bf, w2bf, w2t, tag):
    with ExitStack() as pctx:
        sb = lambda *a, **k: pctx.enter_context(nc.sbuf_tensor(*a, **k))
        s_ld = [nc._sb.get(f"sb_w{j}") for j in range(2)]
        s_st = [nc._sb.get(f"sb_y{j}") for j in range(2)]
        t = [sb(f"{tag}_t{j}", [128, D], BF16) for j in range(2)]
        p = Prog(nc)
        jobs = []
        for e in range(8):
            for r in range(16):
                jobs.append((w1_in, w1bf, e, r, 2048))
            for r in range(8):
                jobs.append((w2_in, w2bf, e, r, 1024))
        for i, (src, dst, ee, r, rows) in enumerate(jobs):
            b = i % 2
            lw = [(s_st[b], s_st[b].v)] if i >= 2 else []
            vld = p.add("gpsimd", (lambda e_, src=src, ee=ee, r=r, b=b: e_.dma_start(
                t[b][:, :], src[ee, r * 128:(r + 1) * 128, :])),
                waits=lw, inc=s_ld[b], inc_n=16)
            p.add("sync", (lambda e_, dst=dst, ee=ee, r=r, b=b: e_.dma_start(
                dst[ee, r * 128:(r + 1) * 128, :], t[b][:, :])),
                waits=[(s_ld[b], vld)], inc=s_st[b], inc_n=16)
        tb2 = [pctx.enter_context(nc.sbuf_tensor(f"{tag}_tt{j}", [128, D], BF16,
                                                 is_transpose_dest=True))
               for j in range(2)]
        for i2 in range(64):
            ee, fc = i2 // 8, i2 % 8
            b = i2 % 2
            vld = p.add("sync", (lambda e_, ee=ee, fc=fc, b=b: e_.dma_start(
                tb2[b][:, :], w2bf[ee, 0:1024, fc * 128:(fc + 1) * 128],
                transpose=True)),
                waits=[(s_st[b], s_st[b].v)], inc=s_ld[b], inc_n=16)
            p.add("sync", (lambda e_, ee=ee, fc=fc, b=b: e_.dma_start(
                w2t[ee, fc * 128:(fc + 1) * 128, :], tb2[b][:, :])),
                waits=[(s_ld[b], vld)], inc=s_st[b], inc_n=16)
        p.add("sync", lambda e_: e_.nop(), waits=[w(s_st[0]), w(s_st[1])])
        p.emit()


def wget_xbar3(wbf, e, Krows):
    def f(eng, dst, k0, ks, m0, ms):
        return eng.dma_start(dst[0:ks, 0:ms],
                             wbf[e, m0:m0 + ms, k0:k0 + ks], transpose=True)
    return f


def emit_moe_gate(nc, ctx, G, IN, xn1_t, ce_f, tag):
    with ExitStack() as pctx:
        sb = lambda *a, **k: pctx.enter_context(nc.sbuf_tensor(*a, **k))
        psum = lambda *a, **k: pctx.enter_context(nc.psum_tensor(*a, **k))
        s_ld = [nc._sb.get(f"sb_l{j}") for j in range(2)]
        s_w = nc._sb.get("sb_qs")
        s_v = nc._sb.get("sb_v")
        s_a = nc._sb.get("sb_a")
        s_mm = nc._sb.get("sb_mm")
        s_st = [nc._sb.get(f"sb_y{j}") for j in range(2)]
        wgt = sb(f"{tag}_wg", [8, D], BF16)
        wgT = sb(f"{tag}_wgT", [128, 64], BF16)
        bgr = sb(f"{tag}_bg", [1, 8], BF16)
        bgb = sb(f"{tag}_bgb", [128, 8], F32)
        xnf = [sb(f"{tag}_x{j}", [128, 128], BF16, is_transpose_dest=True)
               for j in range(8)]
        lg = sb(f"{tag}_lg", [128, 8], F32)
        pr = sb(f"{tag}_pr", [128, 8], F32)
        t8 = sb(f"{tag}_t8", [128, 8], F32)
        smm = sb(f"{tag}_smm", [128, 16], F32)
        ceb = [sb(f"{tag}_ce{j}", [128, 8], BF16) for j in range(2)]
        cet = [sb(f"{tag}_cet{j}", [8, 128], BF16) for j in range(2)]
        psL = psum(f"{tag}_pL", [128, 512], F32)
        psT = psum(f"{tag}_pT", [128, 512], F32)
        psTb = psum(f"{tag}_pTb", [128, 512], BF16)

        p = Prog(nc)
        vw = p.add("gpsimd", (lambda e: e.dma_start(wgt[0:8, :], IN["L1_ffn_Wg"][:, :])),
                   inc=s_w, inc_n=16)
        vw2 = p.add("gpsimd", (lambda e: e.dma_start(bgr[0:1, :], IN["L1_ffn_bg"][:, :])),
                    inc=s_w, inc_n=16)
        vt_ = None
        for j in range(8):
            vt_ = p.add("tensor", (lambda e, j=j: e.transpose(
                psTb[0:128, j * 8:(j + 1) * 8], wgt[0:8, j * 128:(j + 1) * 128],
                G["ident_bf16"][0:8, 0:8])),
                waits=[(s_w, vw2)] if j == 0 else (), inc=s_mm)
        vv = p.add("vector", (lambda e: e.tensor_copy(wgT[:, :], psTb[:, 0:64])),
                   waits=[(s_mm, vt_)], inc=s_v)
        vm = p.add("tensor", (lambda e: e.matmul(
            psT[:, 64:72], G["bc_all"][0:1, :], bgr[0:1, :], start=True, stop=True)),
            waits=[(s_v, vv)], inc=s_mm)
        vv = p.add("vector", (lambda e: e.tensor_copy(bgb[:, :], psT[:, 64:72])),
                   waits=[(s_mm, vm)], inc=s_v)
        vpre = vv
        for i in range(NT):
            b = i % 2
            lw = ([(s_st[b], s_st[b].v)] if i >= 2 else [(s_v, vpre)]) + \
                 ([(s_mm, s_mm.v)] if i >= 1 else [])
            vld = None
            for j in range(8):
                vld = p.add("sync", (lambda e, j=j, i=i: e.dma_start(
                    xnf[j][:, :], xn1_t[i * 128:(i + 1) * 128,
                                        j * 128:(j + 1) * 128], transpose=True)),
                    waits=lw if j == 0 else (), inc=s_ld[b], inc_n=16)
            vm = None
            for j in range(8):
                vm = p.add("tensor", (lambda e, j=j: e.matmul(
                    psL[:, 0:8], xnf[j][:, :], wgT[:, j * 8:(j + 1) * 8],
                    start=(j == 0), stop=(j == 7))),
                    waits=[(s_ld[b], vld), (s_v, s_v.v)] if j == 0 else (),
                    inc=s_mm)
            vv = p.add("vector", (lambda e: e.tensor_add(lg[:, :], psL[:, 0:8],
                                                         bgb[:, :])),
                       waits=[(s_mm, vm)], inc=s_v)
            vv = p.add("vector", (lambda e: e.reduce_max(smm[:, 0:1], lg[:, :], AX.X,
                                                         negate=True)),
                       waits=[(s_v, vv)], inc=s_v)
            va = p.add("scalar", (lambda e: e.activation(pr[:, :], lg[:, :], AF.Exp,
                                                         bias=smm[:, 0:1])),
                       waits=[(s_v, vv)], inc=s_a)
            vv = p.add("vector", (lambda e: e.reduce_sum(smm[:, 1:2], pr[:, :], AX.X)),
                       waits=[(s_a, va)], inc=s_v)
            vv = p.add("vector", (lambda e: e.reciprocal(smm[:, 2:3], smm[:, 1:2])),
                       waits=[(s_v, vv)], inc=s_v)
            vv = p.add("vector", (lambda e: e.tensor_scalar(pr[:, :], pr[:, :],
                                                            smm[:, 2:3], None,
                                                            ALU.mult)),
                       waits=[(s_v, vv)], inc=s_v)
            vv = p.add("vector", (lambda e: e.reduce_max(smm[:, 3:4], pr[:, :], AX.X)),
                       waits=[(s_v, vv)], inc=s_v)
            vv = p.add("vector", (lambda e: e.tensor_scalar(t8[:, :], pr[:, :],
                                                            smm[:, 3:4], None,
                                                            ALU.is_ge)),
                       waits=[(s_v, vv)], inc=s_v)
            vv = p.add("vector", (lambda e: e.reduce_sum(smm[:, 4:5], t8[:, :], AX.X)),
                       waits=[(s_v, vv)], inc=s_v)
            vv = p.add("vector", (lambda e: e.scalar_tensor_tensor(
                t8[:, :], t8[:, :], -2.0, pr[:, :], ALU.mult, ALU.add)),
                waits=[(s_v, vv)], inc=s_v)
            vv = p.add("vector", (lambda e: e.reduce_max(smm[:, 5:6], t8[:, :], AX.X)),
                       waits=[(s_v, vv)], inc=s_v)
            vv = p.add("vector", (lambda e: e.tensor_scalar(smm[:, 4:5], smm[:, 4:5],
                                                            2.0, None, ALU.is_ge)),
                       waits=[(s_v, vv)], inc=s_v)
            vv = p.add("vector", (lambda e: e.tensor_sub(smm[:, 6:7], smm[:, 3:4],
                                                         smm[:, 5:6])),
                       waits=[(s_v, vv)], inc=s_v)
            vv = p.add("vector", (lambda e: e.scalar_tensor_tensor(
                smm[:, 7:8], smm[:, 4:5], smm[:, 6:7], smm[:, 5:6], ALU.mult,
                ALU.add)), waits=[(s_v, vv)], inc=s_v)
            vv = p.add("vector", (lambda e: e.tensor_scalar(t8[:, :], pr[:, :],
                                                            smm[:, 7:8], None,
                                                            ALU.is_ge)),
                       waits=[(s_v, vv)], inc=s_v)
            vv = p.add("vector", (lambda e: e.tensor_mul(t8[:, :], t8[:, :], pr[:, :])),
                       waits=[(s_v, vv)], inc=s_v)
            vv = p.add("vector", (lambda e: e.reduce_sum(smm[:, 8:9], t8[:, :], AX.X)),
                       waits=[(s_v, vv)], inc=s_v)
            vv = p.add("vector", (lambda e: e.reciprocal(smm[:, 9:10], smm[:, 8:9])),
                       waits=[(s_v, vv)], inc=s_v)
            vv = p.add("vector", (lambda e, b=b: e.tensor_scalar(
                ceb[b][:, :], t8[:, :], smm[:, 9:10], None, ALU.mult)),
                waits=[(s_v, vv)], inc=s_v)
            vm = p.add("tensor", (lambda e, b=b: e.transpose(
                psTb[0:8, 128:256], ceb[b][:, :], G["ident_bf16"][:, :])),
                waits=[(s_v, vv)], inc=s_mm)
            vv = p.add("vector", (lambda e, b=b: e.tensor_copy(cet[b][0:8, :],
                                                               psTb[0:8, 128:256])),
                       waits=[(s_mm, vm)], inc=s_v)
            p.add("sync", (lambda e, b=b, i=i: e.dma_start(
                ce_f[0:8, i * 128:(i + 1) * 128], cet[b][0:8, :])),
                waits=[(s_v, vv)], inc=s_st[b], inc_n=16)
        p.add("sync", lambda e: e.nop(), waits=[w(s_st[0]), w(s_st[1])])
        p.emit()


def make_in_maps(inputs):
    tabs = host_tables()
    x = np.ascontiguousarray(np.asarray(inputs["L0_attn_x"], dtype=np.float32))
    in_maps = []
    for c in range(8):
        m = {"x_in": x[2 * c:2 * c + 2].reshape(N, D)}
        for k, v in inputs.items():
            if k == "L0_attn_x":
                continue
            a = np.asarray(v, dtype=np.float32)
            if k in ("L1_attn_g_attn", "L1_attn_g_ffn", "L1_ffn_bg"):
                a = a.reshape(1, -1)
            m[k] = np.ascontiguousarray(a)
        for k, v in tabs.items():
            m["tab_" + k] = v
        in_maps.append(m)
    return in_maps


def kernel(**inputs):
    nc = build_nc()
    in_maps = make_in_maps(inputs)
    res = run_bass_kernel_spmd(nc, in_maps, core_ids=list(range(8)))
    return np.concatenate(
        [np.asarray(r["out"]).reshape(2, S, D) for r in res.results], axis=0)
